# version 01
# baseline (speedup 1.0000x reference)
from contextlib import ExitStack

import numpy as np
import concourse.bass as bass
import concourse.mybir as mybir
from concourse.bass_utils import run_bass_kernel_spmd

F32 = mybir.dt.float32
BF16 = mybir.dt.bfloat16
ALU = mybir.AluOpType
ACT = mybir.ActivationFunctionType
AX = mybir.AxisListType

NC = 8
L = 16384
D = 1024
TC = L // NC
DM = 2048
DCH = DM // NC
DC2 = DCH // 128
DS = 16
DR = 128
E = 8
H = 256
HC = H // 128
RA = 64
TS = 1024
NSEG = L // TS
KD = D // 128
F = 512
NT = TC // F
EPS = 1e-5

_CACHE = {}


class Sem:
    __slots__ = ("h", "n")

    def __init__(self, nc, name):
        self.h = nc.alloc_semaphore(name)
        self.n = 0

    def inc(self, instr, k=1):
        instr.then_inc(self.h, k)
        self.n += k
        return instr

    def incd(self, instr):
        return self.inc(instr, 16)

    def wait(self, eng, val=None):
        eng.wait_ge(self.h, self.n if val is None else val)


class DStream:
    _uid = [0]

    def __init__(self, nc, name, n=2):
        DStream._uid[0] += 1
        u = DStream._uid[0]
        self.sems = [Sem(nc, f"d{u}_{name}{i}") for i in range(n)]
        self.n = n

    def inc(self, instr, i):
        self.sems[i % self.n].inc(instr, 16)
        return instr

    def wait(self, eng, i):
        eng.wait_ge(self.sems[i % self.n].h, 16 * (i // self.n + 1))


class B:

    def __init__(self, nc, block, stack):
        self.nc = nc
        self.block = block
        self.stack = stack
        self._semi = 0
        self.pV = self.sem("pV")
        self.pS = self.sem("pS")
        self.pT = self.sem("pT")
        self.pG = self.sem("pG")

    def sem(self, tag=""):
        self._semi += 1
        return Sem(self.nc, f"s{self._semi}_{tag}")

    def pooled(self, tag):
        if not hasattr(self, "_pool"):
            self._pool = {}
        if tag not in self._pool:
            self._pool[tag] = self.sem(tag)
        return self._pool[tag]

    def sb(self, name, shape, dt, stack=None):
        return (stack or self.stack).enter_context(self.nc.sbuf_tensor(name, shape, dt))

    def fence(self, scratch, dma_sems=()):
        blk = self.block
        pV, pS, pT, pG = self.pV, self.pS, self.pT, self.pG

        @blk.vector
        def _(v):
            v.wait_ge(pV.h, pV.n)
            pV.inc(v.memset(scratch[0:1, 0:1], 0.0))

        @blk.scalar
        def _(s):
            s.wait_ge(pS.h, pS.n)
            pS.inc(s.memzero(scratch[0:1, 1:2]))

        @blk.gpsimd
        def _(g):
            g.wait_ge(pG.h, pG.n)
            pG.inc(g.memset(scratch[0:1, 2:3], 0.0))

        @blk.tensor
        def _(t):
            pT.inc(t.nop())

        marks = [self.pV, self.pS, self.pT, self.pG] + list(dma_sems)
        for mk in marks:
            @blk.vector
            def _(v, mk=mk):
                mk.wait(v)

            @blk.scalar
            def _(s, mk=mk):
                mk.wait(s)

            @blk.tensor
            def _(t, mk=mk):
                mk.wait(t)

            @blk.gpsimd
            def _(g, mk=mk):
                mk.wait(g)

            @blk.sync
            def _(sy, mk=mk):
                mk.wait(sy)


def TT(eng, out, a, bb, op=ALU.mult):
    return eng.tensor_tensor(out=out, in0=a, in1=bb, op=op)


def TSC(eng, out, a, s1, op0, s2=None, op1=ALU.add):
    if s2 is None:
        return eng.tensor_scalar(out=out, in0=a, scalar1=s1, scalar2=None,
                                 op0=op0, op1=ALU.bypass)
    return eng.tensor_scalar(out=out, in0=a, scalar1=s1, scalar2=s2, op0=op0, op1=op1)


def STT(eng, out, a, s, bb, op0=ALU.mult, op1=ALU.add):
    return eng.scalar_tensor_tensor(out=out, in0=a, scalar=s, in1=bb, op0=op0, op1=op1)


def build_program(upto="all", parts="all"):
    nc = bass.Bass("TRN2", target_bir_lowering=False, detect_race_conditions=False)

    def inp(name, shape):
        return nc.dram_tensor(name, shape, F32, kind="ExternalInput").ap()

    x0_h = inp("x0", [TC, D])
    ident_h = inp("ident", [128, 128])
    sel_h = inp("sel32", [64, 32, 128])
    wbase_h = inp("w_base", [D, D])
    bbase_h = inp("b_base", [128, KD])
    wd1_h = inp("wd1", [E, D, H])
    bd1_h = inp("bd1", [E, 128, HC])
    wd2_h = inp("wd2", [E, H, D])
    bd2_h = inp("bd2", [E, 128, KD])
    wad_h = inp("wad", [E, D, RA])
    bad_h = inp("bad", [RA, E])
    wau_h = inp("wau", [E, RA, D])
    bau_h = inp("bau", [E, 128, KD])
    win_h = inp("w_in", [D, DCH])
    wres_h = inp("w_res", [D, DCH])
    wout_h = inp("w_out", [DM, D])
    convw_h = inp("conv_w", [128, DC2, 4])
    convb_h = inp("conv_b", [128, DC2])
    alog_h = inp("a_log", [128, DC2, DS])
    dvec_h = inp("d_vec", [128, DC2])
    wb_h = inp("w_b", [DCH, DS])
    wc_h = inp("w_c", [DCH, DS])
    wdt_h = inp("w_dt", [DCH, DR])
    dtw_h = inp("dt_w", [DR, DCH])
    dtb_h = inp("dt_b", [128, DC2])

    out_h = nc.dram_tensor("out", [TC, D], F32, kind="ExternalOutput").ap()

    def dram(name, shape, dt=F32):
        return nc.dram_tensor(name, shape, dt).ap()

    wd1c_h = [dram(f"wd1c{e}", [128, KD * H], BF16) for e in range(E)]
    wd2c_h = [dram(f"wd2c{e}", [128, HC * D], BF16) for e in range(E)]
    woutc_h = dram("woutc", [128, (DM // 128) * D], BF16)
    u2_in_h = dram("u2_in", [D, TC], BF16)
    u2_out_h = dram("u2_out", [NC, D, TC], BF16)
    seg_in_h = [dram(f"seg_in{s}", [DR + 2 * DS, TS]) for s in range(NSEG)]
    seg_out_h = [dram(f"seg_out{s}", [DR + 2 * DS, TS]) for s in range(NSEG)]
    loss_in_h = [dram(f"loss_in{c}", [1, E]) for c in range(2)]
    loss_out_h = [dram(f"loss_out{c}", [1, E]) for c in range(2)]
    y_in_h = dram("y_in", [NC, DCH, TC], BF16)
    y_out_h = dram("y_out", [NC, DCH, TC], BF16)
    sspill_h = dram("sspill", [128, KD * TC])

    with ExitStack() as stack, nc.Block() as block:
        b = B(nc, block, stack)
        sb = b.sb

        identf = sb("identf", [128, 128], F32)
        identb = sb("identb", [128, 128], BF16)
        negidb = sb("negidb", [128, 128], BF16)
        ones1b = sb("ones1b", [1, 128], BF16)
        ones128f = sb("ones128f", [128, 1], F32)
        bbase = sb("bbase", [128, KD], F32)
        bd1s = sb("bd1s", [128, E * HC], F32)
        bd2s = sb("bd2s", [128, E * KD], F32)
        bads = sb("bads", [RA, E], F32)
        baus = sb("baus", [128, E * KD], F32)
        convb = sb("convbs", [128, DC2], F32)
        dvec = sb("dvecs", [128, DC2], F32)
        dtbn = sb("dtbn", [128, DC2], F32)
        ppos = sb("ppos", [128, DC2 * DS], F32)
        convdiag = sb("convdiag", [128, DC2 * 4, 128], BF16)
        hb05 = sb("hb05", [128, KD], F32)
        lrow = sb("lrow", [1, E], F32)
        onerow = sb("onerow", [1, E], F32)
        maskrow = sb("maskrow", [1, E], F32)
        tiny = sb("tiny", [1, 8], F32)
        fbuf = sb("fbuf", [1, 4], F32)
        ones1f = sb("ones1f", [1, 128], F32)
        win_s = sb("win_s", [128, KD, DCH], BF16)
        wres_s = sb("wres_s", [128, KD, DCH], BF16)
        wdt_s = sb("wdt_s", [128, DC2, DR], BF16)
        dtw_s = sb("dtw_s", [128, DCH], BF16)
        wbneg_s = sb("wbneg_s", [128, DC2, DS], BF16)
        wc_s = sb("wc_s", [128, DC2, DS], BF16)
        hstate = sb("hstate", [128, DC2, DS], F32)
        selb = sb("selb", [64, 32, 128], BF16)
        wbase_s = sb("wbase_s", [128, KD, D], BF16)
        wad_all = sb("wad_all", [128, KD, E, RA], BF16)
        wau_all = sb("wau_all", [RA, E, D], BF16)

        pb = [stack.enter_context(nc.psum_tensor(f"pbank{i}", [128, F], F32))
              for i in range(8)]

        es_stream = ExitStack()
        stream = b.sb("stream", [128, KD, TC], F32, es_stream)

        with ExitStack() as p0:
            stage = b.sb("stage", [128, 8192], F32, p0)
            stgb = b.sb("stgb", [128, 8192], BF16, p0)

            s_c = b.sem("const")

            @block.sync
            def _(sy):
                s_c.incd(sy.dma_start(out=identf[:], in_=ident_h[:]))
                s_c.incd(sy.dma_start(out=bbase[:], in_=bbase_h[:]))
                s_c.incd(sy.dma_start(out=bd1s[:].rearrange("p (e h) -> p e h", e=E), in_=bd1_h.rearrange("e p h -> p e h")))
                s_c.incd(sy.dma_start(out=bd2s[:].rearrange("p (e k) -> p e k", e=E), in_=bd2_h.rearrange("e p k -> p e k")))
                s_c.incd(sy.dma_start(out=bads[:], in_=bad_h[:]))
                s_c.incd(sy.dma_start(out=baus[:].rearrange("p (e k) -> p e k", e=E), in_=bau_h.rearrange("e p k -> p e k")))
                s_c.incd(sy.dma_start(out=convb[:], in_=convb_h[:]))
                s_c.incd(sy.dma_start(out=dvec[:], in_=dvec_h[:]))
                s_c.incd(sy.dma_start(out=dtbn[:], in_=dtb_h[:]))
                s_c.incd(sy.dma_start(out=ppos[:], in_=alog_h.rearrange("p a b -> p (a b)")))
                s_c.incd(sy.dma_start(out=hb05[:, 0:8], in_=convw_h.rearrange("p a b -> p (a b)")))

            s_c0 = b.sem("c0")

            @block.vector
            def _(v):
                s_c.wait(v)
                v.tensor_copy(out=identb[:], in_=identf[:])
                TSC(v, negidb[:], identf[:], -1.0, ALU.mult)
                v.memset(ones1b[:], 1.0)
                v.memset(ones128f[:], 1.0)
                v.memset(ones1f[:], 1.0)
                v.memset(hstate.rearrange("p a b -> p (a b)"), 0.0)
                v.memset(onerow[:], 1.0)
                TSC(v, dtbn[:], dtbn[:], -1.0, ALU.mult)
                for dc in range(DC2):
                    for i in range(4):
                        TSC(v, convdiag[:, dc * 4 + i, :], identf[:],
                            hb05[:, dc * 4 + i : dc * 4 + i + 1], ALU.mult)
                s_c0.inc(v.tensor_copy(out=tiny[0:1, 0:2], in_=hb05[0:1, 0:2]))

            @block.scalar
            def _(s):
                s_c.wait(s)
                s_c0.inc(s.activation(out=ppos[:], in_=ppos[:], func=ACT.Exp))

            jobs = []

            def wj(src, dst, cache, parts, cols, neg=False):
                assert cols <= 4096
                jobs.append((src, dst, cache, parts, cols, neg))

            for hh in range(2):
                wj(wbase_h[hh * 512 : (hh + 1) * 512].rearrange("(a p) c -> p a c", p=128),
                   wbase_s[:, hh * 4 : (hh + 1) * 4, :].rearrange("p a c -> p (a c)"),
                   None, 128, KD * D // 2)
            wj(win_h.rearrange("(a p) c -> p a c", p=128),
               win_s.rearrange("p a c -> p (a c)"), None, 128, KD * DCH)
            wj(wres_h.rearrange("(a p) c -> p a c", p=128),
               wres_s.rearrange("p a c -> p (a c)"), None, 128, KD * DCH)
            wj(wdt_h.rearrange("(a p) c -> p a c", p=128),
               wdt_s.rearrange("p a c -> p (a c)"), None, 128, DC2 * DR)
            wj(dtw_h[:], dtw_s[:], None, 128, DCH)
            wj(wb_h.rearrange("(a p) c -> p a c", p=128),
               wbneg_s.rearrange("p a c -> p (a c)"), None, 128, DC2 * DS, neg=True)
            wj(wc_h.rearrange("(a p) c -> p a c", p=128),
               wc_s.rearrange("p a c -> p (a c)"), None, 128, DC2 * DS)
            wj(sel_h.rearrange("k n p -> k (n p)"),
               selb.rearrange("k n p -> k (n p)"), None, 64, 32 * 128)
            for e in range(E):
                wj(wad_h[e].rearrange("(a p) r -> p a r", p=128),
                   wad_all[:, :, e, :], None, 128, KD * RA)
                wj(wau_h[e][:], wau_all[:, e, :], None, RA, D)
                wj(wd1_h[e].rearrange("(a p) c -> p a c", p=128), None,
                   wd1c_h[e][:], 128, KD * H)
                wj(wd2_h[e].rearrange("(a p) c -> p a c", p=128), None,
                   wd2c_h[e][:], 128, HC * D)
            for qq in range(4):
                wj(wout_h[qq * 512 : (qq + 1) * 512].rearrange("(a p) c -> p a c", p=128),
                   None, woutc_h[:, qq * 4 * D : (qq + 1) * 4 * D], 128, 4 * D)

            s_wl = DStream(nc, "wload")
            s_wc = b.sem("wcast")
            s_ws = DStream(nc, "wstore")
            ncache = sum(1 for j in jobs if j[2] is not None)

            @block.sync
            def _(sy):
                for i, (src, dst, cache, parts, cols, neg) in enumerate(jobs):
                    if i >= 2:
                        sy.wait_ge(s_wc.h, i - 1)
                    half = (i % 2) * 4096
                    dst_st = stage[0:parts, half : half + cols]
                    if len(src.shape) == 3:
                        dst_st = dst_st.rearrange("p (a c) -> p a c", c=src.shape[2])
                    s_wl.inc(sy.dma_start(out=dst_st, in_=src), i)

            cache_seen = [0]

            @block.vector
            def _(v):
                nstore = 0
                for i, (src, dst, cache, parts, cols, neg) in enumerate(jobs):
                    half = (i % 2) * 4096
                    s_wl.wait(v, i)
                    if cache is not None:
                        if nstore >= 2:
                            s_ws.wait(v, nstore - 2)
                        tgt = stgb[0:parts, (nstore % 2) * 4096 : (nstore % 2) * 4096 + cols]
                        nstore += 1
                    else:
                        tgt = dst
                    if neg:
                        s_wc.inc(TSC(v, tgt, stage[0:parts, half : half + cols], -1.0, ALU.mult))
                    else:
                        s_wc.inc(v.tensor_copy(out=tgt, in_=stage[0:parts, half : half + cols]))

            @block.scalar
            def _(s):
                nstore = 0
                for i, (src, dst, cache, parts, cols, neg) in enumerate(jobs):
                    if cache is None:
                        continue
                    s.wait_ge(s_wc.h, i + 1)
                    src_b = stgb[0:parts, (nstore % 2) * 4096 : (nstore % 2) * 4096 + cols]
                    s_ws.inc(s.dma_start(out=cache, in_=src_b), nstore)
                    nstore += 1

            s_si = DStream(nc, "sin")
            s_tr = b.sem("trp")
            s_se = b.sem("sev")

            @block.sync
            def _(sy):
                sy.wait_ge(s_wc.h, len(jobs))
                for tt in range(TC // 128):
                    half = (tt % 2) * 1024
                    if tt >= 2:
                        sy.wait_ge(s_tr.h, (tt - 1) * KD)
                    s_si.inc(sy.dma_start(out=stage[:, half : half + 1024],
                                          in_=x0_h[tt * 128 : (tt + 1) * 128, :]), tt)

            @block.tensor
            def _(t):
                for tt in range(TC // 128):
                    s_si.wait(t, tt)
                    half = (tt % 2) * 1024
                    for kc in range(KD):
                        j = tt * KD + kc
                        if j >= 2:
                            t.wait_ge(s_se.h, j - 1)
                        s_tr.inc(t.transpose(pb[j % 2][:, 0:128],
                                             stage[:, half + kc * 128 : half + (kc + 1) * 128],
                                             identf[:]))

            @block.vector
            def _(v):
                for tt in range(TC // 128):
                    for kc in range(KD):
                        j = tt * KD + kc
                        v.wait_ge(s_tr.h, j + 1)
                        s_se.inc(v.tensor_copy(
                            out=stream[:, kc, tt * 128 : (tt + 1) * 128],
                            in_=pb[j % 2][:, 0:128]))

            b.fence(fbuf, dma_sems=s_ws.sems + s_si.sems)

        def emit_ln(uout, tag, lp=4):
            s_sq = b.pooled("lnsq")
            s_mm = b.pooled("lnmm")
            s_vs = b.pooled("lnvs")
            s_ro = b.pooled("lnro")
            s_bc = b.pooled("lnbc")
            s_ap = b.pooled("lnap")
            b_sq, b_mm, b_vs = s_sq.n, s_mm.n, s_vs.n
            b_ro, b_bc, b_ap = s_ro.n, s_bc.n, s_ap.n
            with ExitStack() as px:
                lnsq = b.sb(f"ln{tag}s", [128, KD, F], F32, px)
                lnm = b.sb(f"ln{tag}m", [1, TC], F32, px)
                lnr = b.sb(f"ln{tag}r", [1, TC], F32, px)
                lnap = b.sb(f"ln{tag}a", [128, F], F32, px)
                lnq = b.sb(f"ln{tag}q", [1, F], F32, px)
                lnsc = b.sb(f"ln{tag}c", [1, F], F32, px)

                @block.scalar
                def _(sc):
                    for tt in range(NT + 1):
                        if tt < NT:
                            if tt >= 1:
                                sc.wait_ge(s_mm.h, b_mm + 2 * tt)
                            ss = None
                            for kc in range(KD):
                                ss = sc.activation(out=lnsq[:, kc, :],
                                                   in_=stream[:, kc, tt * F : (tt + 1) * F],
                                                   func=ACT.Square)
                            s_sq.inc(ss)
                        if tt >= 1 and lp >= 2:
                            u = tt - 1
                            sc.wait_ge(s_vs.h, b_vs + u + 1)
                            sl = slice(u * F, (u + 1) * F)
                            sc.activation(out=lnr[0:1, sl], in_=lnq[0:1, :], func=ACT.Ln)
                            sc.drain()
                            s_ro.inc(sc.activation(out=lnr[0:1, sl], in_=lnr[0:1, sl],
                                                   func=ACT.Exp, scale=-0.5))

                @block.tensor
                def _(t):
                    for tt in range(NT + 1):
                        if tt < NT:
                            if tt >= 1:
                                t.wait_ge(s_vs.h, b_vs + tt)
                            for kc in range(KD):
                                t.matmul(pb[6][0:1, 0:F], ones128f[:],
                                         stream[:, kc, tt * F : (tt + 1) * F],
                                         start=(kc == 0), stop=(kc == KD - 1))
                            t.wait_ge(s_sq.h, b_sq + tt + 1)
                            mm = None
                            for kc in range(KD):
                                mm = t.matmul(pb[7][0:1, 0:F], ones128f[:], lnsq[:, kc, :],
                                              start=(kc == 0), stop=(kc == KD - 1))
                            s_mm.inc(mm, 2)
                        if tt >= 1 and lp >= 3:
                            u = tt - 1
                            t.wait_ge(s_ro.h, b_ro + u + 1)
                            if u >= 1:
                                t.wait_ge(s_ap.h, b_ap + u * KD)
                            t.matmul(pb[4][:, 0:F], ones1f[:],
                                     lnm[0:1, u * F : (u + 1) * F], start=True, stop=True)
                            s_bc.inc(t.matmul(pb[5][:, 0:F], ones1f[:],
                                              lnr[0:1, u * F : (u + 1) * F],
                                              start=True, stop=True))

                @block.vector
                def _(v):
                    for tt in range(NT + 1):
                        if tt < NT:
                            v.wait_ge(s_mm.h, b_mm + 2 * (tt + 1))
                            if tt >= 1:
                                v.wait_ge(s_ro.h, b_ro + tt)
                            sl = slice(tt * F, (tt + 1) * F)
                            TSC(v, lnm[0:1, sl], pb[6][0:1, 0:F], 1.0 / D, ALU.mult)
                            TSC(v, lnq[0:1, :], pb[7][0:1, 0:F], 1.0 / D, ALU.mult)
                            v.drain()
                            TT(v, lnsc[0:1, :], lnm[0:1, sl], lnm[0:1, sl])
                            v.drain()
                            TT(v, lnq[0:1, :], lnq[0:1, :], lnsc[0:1, :], ALU.subtract)
                            v.drain()
                            s_vs.inc(TSC(v, lnq[0:1, :], lnq[0:1, :], EPS, ALU.add))
                        if tt >= 1 and lp >= 4:
                            u = tt - 1
                            v.wait_ge(s_bc.h, b_bc + u + 1)
                            sl = slice(u * F, (u + 1) * F)
                            for kc in range(KD):
                                STT(v, lnap[:], stream[:, kc, sl], 1.0,
                                    pb[4][:, 0:F], ALU.mult, ALU.subtract)
                                v.drain()
                                s_ap.inc(TT(v, uout[:, kc, sl], lnap[:], pb[5][:, 0:F]))
                                v.drain()

                b.fence(fbuf)

        def emit_ffn(uin, call_idx, parts="all"):
            P = {"experts": 1, "loss": 2, "base": 3, "all": 4}[parts]
            s_w = DStream(nc, "fw", 4)
            s_h = b.pooled("fh")
            s_he = b.pooled("fhe")
            s_r = b.pooled("fr")
            s_sqd = b.pooled("fsq")
            s_ls = b.pooled("fls")
            s_ccd = b.pooled("fcc")
            s_msk = b.pooled("fmsk")
            s_sel = b.pooled("fsel")
            s_bmm = b.pooled("fbmm")
            s_bev = b.pooled("fbev")
            s_amm = b.pooled("famm")
            s_aev = b.pooled("faev")
            s_omm = b.pooled("fomm")
            s_oev = b.pooled("foev")
            b_h, b_he, b_r, b_sqd, b_ls = s_h.n, s_he.n, s_r.n, s_sqd.n, s_ls.n
            b_msk, b_bmm, b_bev = s_msk.n, s_bmm.n, s_bev.n
            b_amm, b_aev, b_omm, b_oev = s_amm.n, s_aev.n, s_omm.n, s_oev.n

            with ExitStack() as px:
                wexp = [b.sb(f"f{call_idx}wex{i}", [128, KD * H + HC * D], BF16, px) for i in range(2)]
                hbuf = b.sb(f"f{call_idx}hbuf", [128, HC, TC], BF16, px)
                abuf = b.sb(f"f{call_idx}abuf", [RA, TC], BF16, px)
                accb = b.sb(f"f{call_idx}accb", [128, E * KD * NT], F32, px)
                lcol = b.sb(f"f{call_idx}lcol", [128, E], F32, px)
                wadsel = b.sb(f"f{call_idx}wadsel", [128, KD, RA], BF16, px)
                wausel = b.sb(f"f{call_idx}wausel", [RA, D], BF16, px)
                badsel = b.sb(f"f{call_idx}badsel", [RA, 1], F32, px)

                @block.sync
                def _(sy):
                    for e in range(E):
                        if e >= 2:
                            sy.wait_ge(s_r.h, b_r + (e - 1) * KD * NT)
                        w = wexp[e % 2]
                        s_w.inc(sy.dma_start(out=w[:, 0 : KD * H], in_=wd1c_h[e][:]), 2 * e)
                        s_w.inc(sy.dma_start(out=w[:, KD * H :], in_=wd2c_h[e][:]), 2 * e + 1)

                @block.tensor
                def _(t):
                    for e in range(E):
                        s_w.wait(t, 2 * e)
                        s_w.wait(t, 2 * e + 1)
                        w = wexp[e % 2]
                        wd1v = w[:, 0 : KD * H].rearrange("p (a c) -> p a c", c=H)
                        wd2v = w[:, KD * H :].rearrange("p (a c) -> p a c", c=D)
                        for hc in range(HC):
                            for tt in range(NT):
                                j = hc * NT + tt
                                bank = pb[j % 2]
                                jj = e * HC * NT + j
                                if jj >= 2:
                                    t.wait_ge(s_he.h, b_he + jj - 1)
                                mm = None
                                for kc in range(KD):
                                    mm = t.matmul(bank[:, 0:F],
                                                  wd1v[:, kc, hc * 128 : (hc + 1) * 128],
                                                  uin[:, kc, tt * F : (tt + 1) * F],
                                                  start=(kc == 0), stop=(kc == KD - 1))
                                s_h.inc(mm)
                        t.wait_ge(s_he.h, b_he + (e + 1) * HC * NT)
                        for pc in range(KD):
                            for tt in range(NT):
                                j = pc * NT + tt
                                bank = pb[2 + j % 2]
                                jj = e * KD * NT + j
                                if jj >= 2:
                                    t.wait_ge(s_sqd.h, b_sqd + jj - 1)
                                for hc in range(HC):
                                    t.matmul(bank[:, 0:F],
                                             wd2v[:, hc, pc * 128 : (pc + 1) * 128],
                                             hbuf[:, hc, tt * F : (tt + 1) * F],
                                             start=(hc == 0), stop=False)
                                s_r.inc(t.matmul(bank[:, 0:F], negidb[:],
                                                 uin[:, pc, tt * F : (tt + 1) * F],
                                                 start=False, stop=True))

                @block.scalar
                def _(s):
                    for e in range(E):
                        for hc in range(HC):
                            for tt in range(NT):
                                j = hc * NT + tt
                                s.wait_ge(s_h.h, b_h + e * HC * NT + j + 1)
                                bank = pb[j % 2]
                                s_he.inc(s.activation(
                                    out=hbuf[:, hc, tt * F : (tt + 1) * F],
                                    in_=bank[:, 0:F], func=ACT.Relu,
                                    bias=bd1s[:, e * HC + hc : e * HC + hc + 1]))
                        for pc in range(KD):
                            for tt in range(NT):
                                j = pc * NT + tt
                                s.wait_ge(s_r.h, b_r + e * KD * NT + j + 1)
                                bank = pb[2 + j % 2]
                                s_sqd.inc(s.activation(
                                    out=bank[:, 0:F], in_=bank[:, 0:F], func=ACT.Square,
                                    bias=bd2s[:, e * KD + pc : e * KD + pc + 1],
                                    accum_out=accb[:, e * KD * NT + j : e * KD * NT + j + 1]))

                @block.tensor
                def _(t):
                    if P < 3:
                        return
                    for pc in range(KD):
                        for tt in range(NT):
                            j = pc * NT + tt
                            bank = pb[4 + j % 2]
                            if j >= 2:
                                t.wait_ge(s_bev.h, b_bev + j - 1)
                            mm = None
                            for kc in range(KD):
                                mm = t.matmul(bank[:, 0:F],
                                              wbase_s[:, kc, pc * 128 : (pc + 1) * 128],
                                              uin[:, kc, tt * F : (tt + 1) * F],
                                              start=(kc == 0), stop=(kc == KD - 1))
                            s_bmm.inc(mm)

                @block.vector
                def _(v):
                    if P < 3:
                        return
                    for pc in range(KD):
                        for tt in range(NT):
                            j = pc * NT + tt
                            v.wait_ge(s_bmm.h, b_bmm + j + 1)
                            bank = pb[4 + j % 2]
                            sl = slice(tt * F, (tt + 1) * F)
                            s_bev.inc(STT(v, stream[:, pc, sl], bank[:, 0:F], 0.5,
                                          stream[:, pc, sl]))

                @block.vector
                def _(v):
                    if P < 2:
                        return
                    for e in range(E):
                        v.wait_ge(s_sqd.h, b_sqd + (e + 1) * KD * NT)
                        s_ls.inc(v.tensor_reduce(out=lcol[:, e : e + 1],
                                                 in_=accb[:, e * KD * NT : (e + 1) * KD * NT], axis=AX.X, op=ALU.add))

                @block.tensor
                def _(t):
                    if P < 2:
                        return
                    t.wait_ge(s_ls.h, b_ls + E)
                    s_ls.inc(t.matmul(pb[6][0:1, 0:E], ones128f[:], lcol[:, :],
                                      start=True, stop=True))

                @block.vector
                def _(v):
                    if P < 2:
                        return
                    v.wait_ge(s_ls.h, b_ls + E + 1)
                    s_ls.inc(v.tensor_copy(out=lrow[:], in_=pb[6][0:1, 0:E]))

                s_cd1 = b.pooled("fcd1")
                s_cd2 = b.pooled("fcd2")
                b_cd1, b_ccd2 = s_cd1.n, s_ccd.n

                @block.gpsimd
                def _(g):
                    if P < 2:
                        return
                    g.wait_ge(s_ls.h, b_ls + E + 2)
                    s_cd1.incd(g.dma_start(out=loss_in_h[call_idx][:], in_=lrow[:]))
                    g.wait_ge(s_cd1.h, b_cd1 + 16)
                    s_ccd.inc(g.collective_compute(
                        "AllReduce", ALU.add, replica_groups=[list(range(NC))],
                        ins=[loss_in_h[call_idx].opt()],
                        outs=[loss_out_h[call_idx].opt()]))
                    g.wait_ge(s_ccd.h, b_ccd2 + 1)
                    s_cd2.incd(g.dma_start(out=lrow[:], in_=loss_out_h[call_idx][:]))

                @block.vector
                def _(v):
                    if P < 2:
                        return
                    s_cd2.wait(v)
                    s_msk.inc(v.tensor_reduce(out=tiny[0:1, 0:1], in_=lrow[:],
                                              axis=AX.X, op=ALU.min))
                    s_msk.wait(v)
                    TSC(v, maskrow[:], lrow[:], tiny[0:1, 0:1], ALU.is_equal)
                    v.drain()
                    s_msk.inc(v.tensor_tensor_scan(out=tiny[0:1, 0:E], data0=onerow[:],
                                                   data1=maskrow[:], initial=0.0,
                                                   op0=ALU.mult, op1=ALU.add))
                    s_msk.wait(v)
                    TSC(v, tiny[0:1, 0:E], tiny[0:1, 0:E], 1.0, ALU.is_equal)
                    v.drain()
                    s_msk.inc(TT(v, maskrow[:], maskrow[:], tiny[0:1, 0:E]))

                @block.tensor
                def _(t):
                    if P < 2:
                        return
                    s_msk.wait(t)
                    s_msk.inc(t.matmul(pb[6][:, 0:E], ones1f[:], maskrow[0:1, :],
                                       start=True, stop=True))

                @block.vector
                def _(v):
                    if P < 4:
                        return
                    s_msk.wait(v)
                    for e in range(E):
                        if e > 0:
                            v.drain()
                        sc = pb[6][:, e : e + 1]
                        scr = pb[6][0:RA, e : e + 1]
                        wadv = wadsel[:, :, :]
                        wallv = wad_all[:, :, e, :]
                        if e == 0:
                            TSC(v, wadv, wallv, sc, ALU.mult)
                            TSC(v, wausel[:], wau_all[:, e, :], scr, ALU.mult)
                            TSC(v, badsel[:], bads[:, e : e + 1], scr, ALU.mult)
                            TSC(v, hb05[:], baus[:, e * KD : (e + 1) * KD], sc, ALU.mult)
                        else:
                            STT(v, wadv, wallv, sc, wadv)
                            STT(v, wausel[:], wau_all[:, e, :], scr, wausel[:])
                            STT(v, badsel[:], bads[:, e : e + 1], scr, badsel[:])
                            STT(v, hb05[:], baus[:, e * KD : (e + 1) * KD], sc, hb05[:])
                    v.drain()
                    STT(v, hb05[:], bbase[:], 1.0, hb05[:], ALU.mult, ALU.add)
                    v.drain()
                    s_sel.inc(TSC(v, hb05[:], hb05[:], 0.5, ALU.mult))

                @block.tensor
                def _(t):
                    if P < 4:
                        return
                    s_sel.wait(t)
                    for tt in range(NT):
                        bank = pb[0 + tt % 2]
                        if tt >= 2:
                            t.wait_ge(s_aev.h, b_aev + tt - 1)
                        mm = None
                        for kc in range(KD):
                            mm = t.matmul(bank[0:RA, 0:F], wadsel[:, kc, :],
                                          uin[:, kc, tt * F : (tt + 1) * F],
                                          start=(kc == 0), stop=(kc == KD - 1))
                        s_amm.inc(mm)

                @block.scalar
                def _(s):
                    if P < 4:
                        return
                    for tt in range(NT):
                        s.wait_ge(s_amm.h, b_amm + tt + 1)
                        bank = pb[0 + tt % 2]
                        s_aev.inc(s.activation(out=abuf[:, tt * F : (tt + 1) * F],
                                               in_=bank[0:RA, 0:F], func=ACT.Relu,
                                               bias=badsel[:]))

                @block.tensor
                def _(t):
                    if P < 4:
                        return
                    for pc in range(KD):
                        for tt in range(NT):
                            j = pc * NT + tt
                            bank = pb[2 + j % 2]
                            t.wait_ge(s_aev.h, b_aev + tt + 1)
                            if j >= 2:
                                t.wait_ge(s_oev.h, b_oev + j - 1)
                            s_omm.inc(t.matmul(bank[:, 0:F],
                                               wausel[:, pc * 128 : (pc + 1) * 128],
                                               abuf[:, tt * F : (tt + 1) * F],
                                               start=True, stop=True))

                @block.vector
                def _(v):
                    if P < 4:
                        return
                    s_bev.wait(v)
                    for pc in range(KD):
                        for tt in range(NT):
                            j = pc * NT + tt
                            v.wait_ge(s_omm.h, b_omm + j + 1)
                            bank = pb[2 + j % 2]
                            sl = slice(tt * F, (tt + 1) * F)
                            STT(v, stream[:, pc, sl], bank[:, 0:F], 0.5, stream[:, pc, sl])
                            v.drain()
                            s_oev.inc(TSC(v, stream[:, pc, sl], stream[:, pc, sl],
                                          hb05[:, pc : pc + 1], ALU.add))

                b.fence(fbuf)

        with ExitStack() as pf:
            u1f = b.sb("u1f", [128, KD, TC], BF16, pf)
            if upto != "p0":
                lnlvl = {"lnA": 1, "lnB": 2, "lnC": 3}.get(upto, 4)
                emit_ln(u1f, "1", lp=lnlvl)
            if upto not in ("ln1", "p0", "lnA", "lnB", "lnC"):
                emit_ffn(u1f, 0, parts=("all" if upto in ("ffn1", "all") else upto))

        if upto in ("ffn1", "ln1", "experts", "loss", "base", "p0", "lnA", "lnB", "lnC"):
            emit_store(nc, block, b, stream, pb, identf, out_h, fbuf)
            return nc

        s_ag = b.sem("ag")
        with ExitStack() as p3:
            u2f = b.sb("u2f", [128, KD, TC], BF16, p3)
            emit_ln(u2f, "2")
            s_u2d = b.sem("u2d")
            s_spl = b.sem("spl")

            @block.sync
            def _(sy):
                s_u2d.incd(sy.dma_start(out=u2_in_h.rearrange("(a p) t -> p a t", p=128),
                                        in_=u2f[:, :, :]))
                s_spl.incd(sy.dma_start(out=sspill_h[:],
                                        in_=stream.rearrange("p a t -> p (a t)")))

            @block.gpsimd
            def _(g):
                g.wait_ge(s_u2d.h, 16)
                s_ag.inc(g.collective_compute(
                    "AllGather", ALU.bypass, replica_groups=[list(range(NC))],
                    ins=[u2_in_h.opt()], outs=[u2_out_h.opt()]))

            b.fence(fbuf, dma_sems=[s_spl, s_u2d])
        es_stream.close()

        with ExitStack() as p4:
            u2t = [b.sb(f"u2t{i}", [128, KD, TS], BF16, p4) for i in range(2)]
            xcv = [b.sb(f"xcq{i}", [128, DC2, TS], F32, p4) for i in range(2)]
            xcb = b.sb("xcbq", [128, DC2, TS], BF16, p4)
            srv = [b.sb(f"srq{i}", [128, DC2, TS], F32, p4) for i in range(2)]
            xconv = [b.sb(f"xcvq{i}", [128, DC2, 3 + TS], BF16, p4) for i in range(2)]
            dpre = [b.sb(f"dpreq{i}", [128, TS], F32, p4) for i in range(2)]
            dpreb = b.sb("dprebq", [128, TS], BF16, p4)
            bcstage = [b.sb(f"bcstq{i}", [64, TS], F32, p4) for i in range(2)]
            bcb = [b.sb(f"bcbq{i}", [64, TS], BF16, p4) for i in range(2)]
            dnegv = b.sb("dnegv", [128, DC2, TS], F32, p4)
            dxn = b.sb("dxnq", [128, DC2, TS], F32, p4)
            scan_a = b.sb("scana", [128, 2, TS], F32, p4)
            scan_u = b.sb("scanu", [128, TS], F32, p4)
            scan_h = b.sb("scanh", [128, TS], F32, p4)
            scan_g = b.sb("scang", [128, 2, TS], BF16, p4)
            ytmp = b.sb("ytmp", [128, F], F32, p4)
            ybuf = b.sb("ybufq", [128, DC2, TS], BF16, p4)

            @block.vector
            def _(v):
                for i in range(2):
                    v.memset(bcb[i][0:64, :], 0.0)

            s_u2l = DStream(nc, "u2l")
            s_fm = b.sem("fm")
            s_fe = b.sem("fe")
            s_cb = b.sem("cb")
            s_gd = DStream(nc, "gd")
            s_ar = b.sem("ar")
            s_ld = DStream(nc, "arld")
            s_dc = b.sem("dcst")
            s_zm = b.sem("zm")
            s_bb = b.sem("bb")
            s_bc2 = b.sem("bc2")
            s_ze = b.sem("ze")
            s_dx = b.sem("dx")
            s_ae = b.sem("ae")
            s_ub = b.sem("ub")
            s_sc = b.sem("sc")
            s_hs = b.sem("hs")
            s_cm = b.sem("cm")
            s_ir = b.sem("ir")
            s_yf = b.sem("yf")
            s_ys = DStream(nc, "ys")
            s_sd = b.sem("sd")

            fm_after_res = {}
            fm_c = [0]
            fe_c = [0]

            def front(sy_ops, s):
                bq = s % 2
                @block.sync
                def _(sy):
                    s_ag.wait(sy)
                    if s >= 2:
                        sy.wait_ge(s_fm.h, fm_after_res[s - 2])
                    src = u2_out_h[s // 2].rearrange("(a p) t -> p a t", p=128)
                    s_u2l.inc(sy.dma_start(
                        out=u2t[bq][:, :, :],
                        in_=src[:, :, (s % 2) * TS : (s % 2) * TS + TS]), s)

                jobs = []
                for dc in range(DC2):
                    for tt in range(2):
                        jobs.append(("x", dc, tt))
                for dc in range(DC2):
                    for tt in range(2):
                        jobs.append(("conv", dc, tt))
                for dc in range(DC2):
                    for tt in range(2):
                        jobs.append(("res", dc, tt))
                jobs += [("B", None, 0), ("B", None, 1), ("C", None, 0), ("C", None, 1),
                         ("dpre", None, 0), ("dpre", None, 1)]

                fm0 = fm_c[0]
                fe0 = fe_c[0]

                @block.tensor
                def _(t):
                    for j, (kind, dc, tt) in enumerate(jobs):
                        bank = pb[j % 2]
                        gj = fm0 + j
                        if gj >= 2:
                            t.wait_ge(s_fe.h, gj - 1)
                        if kind == "x":
                            if j == 0:
                                s_u2l.wait(t, s)
                            mm = None
                            for kc in range(KD):
                                mm = t.matmul(bank[:, 0:F],
                                              win_s[:, kc, dc * 128 : (dc + 1) * 128],
                                              u2t[bq][:, kc, tt * F : (tt + 1) * F],
                                              start=(kc == 0), stop=(kc == KD - 1))
                        elif kind == "conv":
                            t.wait_ge(s_fe.h, fe0 + dc * 2 + 2)
                            mm = None
                            for i4 in range(4):
                                mm = t.matmul(bank[:, 0:F],
                                              convdiag[:, dc * 4 + i4, :],
                                              xconv[bq][:, dc, tt * F + i4 : tt * F + i4 + F],
                                              start=(i4 == 0), stop=(i4 == 3))
                        elif kind == "res":
                            mm = None
                            for kc in range(KD):
                                mm = t.matmul(bank[:, 0:F],
                                              wres_s[:, kc, dc * 128 : (dc + 1) * 128],
                                              u2t[bq][:, kc, tt * F : (tt + 1) * F],
                                              start=(kc == 0), stop=(kc == KD - 1))
                        elif kind == "B":
                            t.wait_ge(s_cb.h, 2 * s + 2)
                            mm = None
                            for dcc in range(DC2):
                                mm = t.matmul(bank[0:DS, 0:F],
                                              wbneg_s[:, dcc, :],
                                              xcb[:, dcc, tt * F : (tt + 1) * F],
                                              start=(dcc == 0), stop=(dcc == DC2 - 1))
                        elif kind == "C":
                            mm = None
                            for dcc in range(DC2):
                                mm = t.matmul(bank[0:DS, 0:F],
                                              wc_s[:, dcc, :],
                                              xcb[:, dcc, tt * F : (tt + 1) * F],
                                              start=(dcc == 0), stop=(dcc == DC2 - 1))
                        elif kind == "dpre":
                            mm = None
                            for dcc in range(DC2):
                                mm = t.matmul(bank[:, 0:F],
                                              wdt_s[:, dcc, :],
                                              xcb[:, dcc, tt * F : (tt + 1) * F],
                                              start=(dcc == 0), stop=(dcc == DC2 - 1))
                        s_fm.inc(mm)
                fm_c[0] = fm0 + len(jobs)
                fm_after_res[s] = fm0 + 12

                @block.vector
                def _(v):
                    if s >= 1:
                        v.wait_ge(s_fe.h, fe0 - 18 + 4)
                    if s >= 2:
                        v.wait_ge(s_sd.h, s - 1)
                    if s == 0:
                        for dc in range(DC2):
                            v.memset(xconv[bq][:, dc, 0:3], 0.0)
                    else:
                        for dc in range(DC2):
                            v.tensor_copy(out=xconv[bq][:, dc, 0:3],
                                          in_=xconv[1 - bq][:, dc, TS : TS + 3])

                @block.scalar
                def _(sc):
                    if s >= 2:
                        sc.wait_ge(s_sd.h, s - 1)
                    for j in range(12):
                        kind, dc, tt = jobs[j]
                        sc.wait_ge(s_fm.h, fm0 + j + 1)
                        bank = pb[j % 2]
                        if kind == "x":
                            ev = sc.activation(out=xconv[bq][:, dc, 3 + tt * F : 3 + (tt + 1) * F],
                                               in_=bank[:, 0:F], func=ACT.Identity)
                        elif kind == "conv":
                            ev = sc.activation(out=xcv[bq][:, dc, tt * F : (tt + 1) * F],
                                               in_=bank[:, 0:F], func=ACT.Silu,
                                               bias=convb[:, dc : dc + 1])
                        elif kind == "res":
                            ev = sc.activation(out=srv[bq][:, dc, tt * F : (tt + 1) * F],
                                               in_=bank[:, 0:F], func=ACT.Silu)
                        s_fe.inc(ev)

                @block.vector
                def _(v):
                    if s >= 1:
                        v.wait_ge(s_fm.h, fm0 - 18 + 18)
                    for dc in range(DC2):
                        v.wait_ge(s_fe.h, fe0 + 4 + dc * 2 + 2)
                        s_cb.inc(v.tensor_copy(out=xcb[:, dc, :], in_=xcv[bq][:, dc, :]))
                    for j in range(12, 18):
                        kind, dc, tt = jobs[j]
                        v.wait_ge(s_fm.h, fm0 + j + 1)
                        bank = pb[j % 2]
                        if kind == "B":
                            ev = v.tensor_copy(out=bcstage[bq][0:DS, tt * F : (tt + 1) * F],
                                               in_=bank[0:DS, 0:F])
                        elif kind == "C":
                            ev = v.tensor_copy(out=bcstage[bq][32 : 32 + DS, tt * F : (tt + 1) * F],
                                               in_=bank[0:DS, 0:F])
                        elif kind == "dpre":
                            ev = v.tensor_copy(out=dpre[bq][:, tt * F : (tt + 1) * F],
                                               in_=bank[:, 0:F])
                        s_fe.inc(ev)
                fe_c[0] = fe0 + 18

                @block.gpsimd
                def _(g):
                    g.wait_ge(s_fe.h, fe0 + 18)
                    s_gd.inc(g.dma_start(out=seg_in_h[s][0:DR, :], in_=dpre[bq][:]), 2 * s)
                    s_gd.inc(g.dma_start(out=seg_in_h[s][DR : DR + DS, :],
                                         in_=bcstage[bq][0:DS, :]), 2 * s + 1)
                    s_gd.inc(g.dma_start(out=seg_in_h[s][DR + DS : DR + 2 * DS, :],
                                         in_=bcstage[bq][32 : 32 + DS, :]), 2 * s + 1)
                    s_gd.wait(g, 2 * s)
                    s_gd.wait(g, 2 * s + 1)
                    s_ar.inc(g.collective_compute(
                        "AllReduce", ALU.add, replica_groups=[list(range(NC))],
                        ins=[seg_in_h[s].opt()], outs=[seg_out_h[s].opt()]))

                @block.sync
                def _(sy):
                    sy.wait_ge(s_ar.h, s + 1)
                    s_ld.inc(sy.dma_start(out=dpre[bq][:], in_=seg_out_h[s][0:DR, :]), 2 * s)
                    s_ld.inc(sy.dma_start(out=bcstage[bq][0:DS, :],
                                          in_=seg_out_h[s][DR : DR + DS, :]), 2 * s + 1)
                    s_ld.inc(sy.dma_start(out=bcstage[bq][32 : 32 + DS, :],
                                          in_=seg_out_h[s][DR + DS : DR + 2 * DS, :]), 2 * s + 1)

            def scan_seg(s):
                bq = s % 2
                @block.vector
                def _(v):
                    s_ld.wait(v, 2 * s)
                    s_ld.wait(v, 2 * s + 1)
                    v.tensor_copy(out=dpreb[:], in_=dpre[bq][:])
                    v.tensor_copy(out=bcb[bq][0:DS, :], in_=bcstage[bq][0:DS, :])
                    s_dc.inc(v.tensor_copy(out=bcb[bq][32 : 32 + DS, :],
                                           in_=bcstage[bq][32 : 32 + DS, :]), 2)

                zm0 = 4 * s
                @block.tensor
                def _(t):
                    t.wait_ge(s_dc.h, 2 * (s + 1))
                    for j in range(4):
                        dc, tt = j // 2, j % 2
                        bank = pb[2 + j % 2]
                        if s >= 1 or j >= 2:
                            if j >= 2:
                                t.wait_ge(s_ze.h, zm0 + j - 1)
                            else:
                                t.wait_ge(s_yf.h, 4 * s)
                        mm = t.matmul(bank[:, 0:F], dtw_s[:, dc * 128 : (dc + 1) * 128],
                                      dpreb[:, tt * F : (tt + 1) * F],
                                      start=True, stop=True)
                        s_zm.inc(mm)

                @block.scalar
                def _(sc):
                    for j in range(4):
                        dc, tt = j // 2, j % 2
                        sc.wait_ge(s_zm.h, zm0 + j + 1)
                        bank = pb[2 + j % 2]
                        sc.activation(out=bank[:, 0:F], in_=bank[:, 0:F],
                                      func=ACT.Sigmoid, scale=-1.0,
                                      bias=dtbn[:, dc : dc + 1])
                        sc.drain()
                        ev = sc.activation(out=dnegv[:, dc, tt * F : (tt + 1) * F],
                                           in_=bank[:, 0:F], func=ACT.Ln)
                        s_ze.inc(ev)

                @block.vector
                def _(v):
                    v.wait_ge(s_ze.h, zm0 + 4)
                    s_dx.inc(TT(v, dxn.rearrange("p a t -> p (a t)"),
                                dnegv.rearrange("p a t -> p (a t)"),
                                xcv[bq].rearrange("p a t -> p (a t)")))

                @block.scalar
                def _(sc):
                    sc.wait_ge(s_ze.h, zm0 + 4)
                    for dc in range(DC2):
                        for n in range(DS):
                            gi = (s * DC2 + dc) * DS + n
                            if gi >= 2:
                                sc.wait_ge(s_sc.h, gi - 1)
                            ev = sc.activation(out=scan_a[:, n % 2, :],
                                               in_=dnegv[:, dc, :], func=ACT.Exp,
                                               scale=ppos[:, dc * DS + n : dc * DS + n + 1])
                            s_ae.inc(ev)

                @block.tensor
                def _(t):
                    for dc in range(DC2):
                        if dc == 1:
                            t.wait_ge(s_yf.h, 4 * s + 2)
                        for n in range(DS):
                            gi = (s * DC2 + dc) * DS + n
                            if gi >= 1:
                                t.wait_ge(s_ub.h, 2 * gi)
                            for c in range(2):
                                mm = t.matmul(pb[4 + c][:, 0:F], selb[:, n, :],
                                              bcb[bq][:, c * F : (c + 1) * F],
                                              start=True, stop=True)
                            s_bb.inc(mm)
                            if gi >= 1:
                                t.wait_ge(s_cm.h, 2 * gi)
                            for c in range(2):
                                mm = t.matmul(pb[6 + c][:, 0:F], selb[:, DS + n, :],
                                              bcb[bq][:, c * F : (c + 1) * F],
                                              start=True, stop=True)
                            s_bc2.inc(mm)
                            t.wait_ge(s_cm.h, 2 * (gi + 1))
                            mm = None
                            for c in range(2):
                                mm = t.matmul(pb[2 + c][:, 0:F], identb[:],
                                              scan_g[:, n % 2, c * F : (c + 1) * F],
                                              start=(n == 0), stop=(n == DS - 1))
                            s_ir.inc(mm)

                @block.vector
                def _(v):
                    v.wait_ge(s_dx.h, s + 1)
                    for dc in range(DC2):
                        for n in range(DS):
                            gi = (s * DC2 + dc) * DS + n
                            v.wait_ge(s_bb.h, gi + 1)
                            if gi >= 1:
                                v.wait_ge(s_sc.h, gi)
                            for c in range(2):
                                tv = TT(v, scan_u[:, c * F : (c + 1) * F],
                                        dxn[:, dc, c * F : (c + 1) * F],
                                        pb[4 + c][:, 0:F])
                                s_ub.inc(tv)
                            v.wait_ge(s_ub.h, 2 * (gi + 1))
                            v.wait_ge(s_ae.h, gi + 1)
                            if s >= 1:
                                v.wait_ge(s_hs.h, (s - 1) * DC2 * DS + dc * DS + n + 1)
                            tv = v.tensor_tensor_scan(
                                out=scan_h[:], data0=scan_a[:, n % 2, :],
                                data1=scan_u[:],
                                initial=(0.0 if s == 0 else hstate[:, dc, n : n + 1]),
                                op0=ALU.mult, op1=ALU.add)
                            s_sc.inc(tv)
                            v.wait_ge(s_sc.h, gi + 1)
                            s_hs.inc(v.tensor_copy(out=hstate[:, dc, n : n + 1],
                                                   in_=scan_h[:, TS - 1 : TS]))
                            v.wait_ge(s_bc2.h, gi + 1)
                            if gi >= 2:
                                v.wait_ge(s_ir.h, gi - 1)
                            for c in range(2):
                                tv = TT(v, scan_g[:, n % 2, c * F : (c + 1) * F],
                                        scan_h[:, c * F : (c + 1) * F], pb[6 + c][:, 0:F])
                                s_cm.inc(tv)
                        v.wait_ge(s_ir.h, (s * DC2 + dc) * DS + DS)
                        if 2 * s + dc - 2 >= 0:
                            s_ys.wait(v, 2 * s + dc - 2)
                        for c in range(2):
                            STT(v, ytmp[:], xcv[bq][:, dc, c * F : (c + 1) * F],
                                dvec[:, dc : dc + 1], pb[2 + c][:, 0:F],
                                ALU.mult, ALU.add)
                            v.drain()
                            tv = TT(v, ybuf[:, dc, c * F : (c + 1) * F],
                                    ytmp[:], srv[bq][:, dc, c * F : (c + 1) * F])
                            v.drain()
                            s_yf.inc(tv)
                    s_sd.inc(v.memset(ytmp[0:1, 0:1], 0.0))

                @block.sync
                def _(sy):
                    for dc in range(DC2):
                        sy.wait_ge(s_yf.h, 4 * s + 2 * (dc + 1))
                        dst = y_in_h[s // 2, dc * 128 + 0 : dc * 128 + 128,
                                     (s % 2) * TS : (s % 2) * TS + TS]
                        s_ys.inc(sy.dma_start(out=dst, in_=ybuf[:, dc, :]), 2 * s + dc)

            for s_i in range(NSEG + 1):
                if s_i < NSEG:
                    front(None, s_i)
                if s_i >= 1:
                    scan_seg(s_i - 1)

            @block.sync
            def _(sy):
                for _ss in s_ys.sems:
                    sy.wait_ge(_ss.h, _ss.n)
            b.fence(fbuf, dma_sems=s_ys.sems + s_ld.sems + s_u2l.sems)

        stream = b.sb("stream2", [128, KD, TC], F32, es_stream2 := ExitStack())
        s_a2a = b.sem("a2a")
        with ExitStack() as p5:
            wout_s = b.sb("wout_s", [128, DM // 128, D], BF16, p5)
            ytile = [b.sb(f"ytile{i}", [128, DM // 128, F], BF16, p5) for i in range(2)]
            s_wol = b.sem("wol")
            s_srl = DStream(nc, "srl")
            s_ytl = DStream(nc, "ytl")
            s_om = b.sem("om")
            s_oe = b.sem("oe")

            @block.gpsimd
            def _(g):
                from concourse.aot_env import allow_ici_all_to_all
                with allow_ici_all_to_all():
                    s_a2a.inc(g.collective_compute(
                        "AllToAll", ALU.bypass, replica_groups=[list(range(NC))],
                        ins=[y_in_h.opt()], outs=[y_out_h.opt()]))

            @block.sync
            def _(sy):
                s_wol.incd(sy.dma_start(
                    out=wout_s.rearrange("p a c -> p (a c)"), in_=woutc_h[:]))
                s_wol.incd(sy.dma_start(out=stream.rearrange("p a t -> p (a t)"),
                                        in_=sspill_h[:]))
                s_a2a.wait(sy)
                yv = y_out_h.rearrange("i (c p) t -> p i c t", p=128)
                for tt in range(NT):
                    if tt >= 2:
                        sy.wait_ge(s_om.h, (tt - 1) * KD)
                    s_ytl.inc(sy.dma_start(
                        out=ytile[tt % 2].rearrange("p a f -> p a f"),
                        in_=yv[:, :, :, tt * F : (tt + 1) * F].rearrange("p i c t -> p (i c) t")), tt)

            @block.tensor
            def _(t):
                t.wait_ge(s_wol.h, 16)
                for tt in range(NT):
                    s_ytl.wait(t, tt)
                    for pc in range(KD):
                        j = tt * KD + pc
                        bank = pb[j % 2]
                        if j >= 2:
                            t.wait_ge(s_oe.h, j - 1)
                        mm = None
                        for kc in range(DM // 128):
                            mm = t.matmul(bank[:, 0:F],
                                          wout_s[:, kc, pc * 128 : (pc + 1) * 128],
                                          ytile[tt % 2][:, kc, :],
                                          start=(kc == 0), stop=(kc == DM // 128 - 1))
                        s_om.inc(mm)

            @block.vector
            def _(v):
                v.wait_ge(s_wol.h, 32)
                for tt in range(NT):
                    for pc in range(KD):
                        j = tt * KD + pc
                        v.wait_ge(s_om.h, j + 1)
                        bank = pb[j % 2]
                        s_oe.inc(STT(v, stream[:, pc, tt * F : (tt + 1) * F],
                                     bank[:, 0:F], 1.0,
                                     stream[:, pc, tt * F : (tt + 1) * F]))

            b.fence(fbuf, dma_sems=s_ytl.sems + s_srl.sems)

        with ExitStack() as pf2:
            u3f = b.sb("u3f", [128, KD, TC], BF16, pf2)
            emit_ln(u3f, "3")
            emit_ffn(u3f, 1)

        emit_store(nc, block, b, stream, pb, identf, out_h, fbuf)
        es_stream2.close()
        return nc


def emit_store(nc, block, b, stream, pb, identf, out_h, tiny):
    s_tr = b.sem("otr")
    s_ev = b.sem("oev")
    s_st = DStream(nc, "ost")
    with ExitStack() as px:
        ostage = b.sb("ostage", [128, 2, 1024], F32, px)

        @block.tensor
        def _(t):
            for tt in range(TC // 128):
                for kc in range(KD):
                    j = tt * KD + kc
                    if j >= 2:
                        t.wait_ge(s_ev.h, j - 1)
                    s_tr.inc(t.transpose(pb[j % 2][:, 0:128],
                                         stream[:, kc, tt * 128 : (tt + 1) * 128],
                                         identf[:]))

        @block.vector
        def _(v):
            for tt in range(TC // 128):
                if tt >= 2:
                    s_st.wait(v, tt - 2)
                for kc in range(KD):
                    j = tt * KD + kc
                    v.wait_ge(s_tr.h, j + 1)
                    s_ev.inc(v.tensor_copy(out=ostage[:, tt % 2, kc * 128 : (kc + 1) * 128],
                                           in_=pb[j % 2][:, 0:128]))

        @block.sync
        def _(sy):
            for tt in range(TC // 128):
                sy.wait_ge(s_ev.h, (tt + 1) * KD)
                s_st.inc(sy.dma_start(out=out_h[tt * 128 : (tt + 1) * 128, :],
                                      in_=ostage[:, tt % 2, :]), tt)
            for _ss in s_st.sems:
                sy.wait_ge(_ss.h, _ss.n)


def _prep_inputs(inputs):
    g = lambda k: np.asarray(inputs[k])
    x = g("L0_attn_x_in")[0]
    per_core = []
    ident = np.eye(128, dtype=np.float32)
    sel32 = np.zeros((64, 32, 128), dtype=np.float32)
    for j in range(16):
        sel32[j, j, :] = 1.0
        sel32[32 + j, 16 + j, :] = 1.0
    for c in range(NC):
        ch = slice(c * DCH, (c + 1) * DCH)
        m = {}
        m["x0"] = np.ascontiguousarray(x[c * TC : (c + 1) * TC])
        m["ident"] = ident
        m["sel32"] = sel32
        m["w_base"] = np.ascontiguousarray(g("L0_ffn_W_base"))
        m["b_base"] = np.ascontiguousarray(g("L0_ffn_b_base").reshape(KD, 128).T)
        m["wd1"] = np.ascontiguousarray(g("L0_ffn_Wd1"))
        m["bd1"] = np.ascontiguousarray(g("L0_ffn_bd1").reshape(E, HC, 128).transpose(0, 2, 1))
        m["wd2"] = np.ascontiguousarray(g("L0_ffn_Wd2"))
        m["bd2"] = np.ascontiguousarray(g("L0_ffn_bd2").reshape(E, KD, 128).transpose(0, 2, 1))
        m["wad"] = np.ascontiguousarray(g("L0_ffn_Wa_down"))
        m["bad"] = np.ascontiguousarray(g("L0_ffn_ba_down").T)
        m["wau"] = np.ascontiguousarray(g("L0_ffn_Wa_up"))
        m["bau"] = np.ascontiguousarray(g("L0_ffn_ba_up").reshape(E, KD, 128).transpose(0, 2, 1))
        m["w_in"] = np.ascontiguousarray(g("L0_attn_W_in")[:, ch])
        m["w_res"] = np.ascontiguousarray(g("L0_attn_W_res")[:, ch])
        m["w_out"] = np.ascontiguousarray(g("L0_attn_W_out"))
        m["conv_w"] = np.ascontiguousarray(
            g("L0_attn_conv_w")[ch].reshape(DC2, 128, 4).transpose(1, 0, 2))
        m["conv_b"] = np.ascontiguousarray(
            g("L0_attn_conv_b")[ch].reshape(DC2, 128).T)
        m["a_log"] = np.ascontiguousarray(
            g("L0_attn_A_log")[ch].reshape(DC2, 128, DS).transpose(1, 0, 2))
        m["d_vec"] = np.ascontiguousarray(
            g("L0_attn_D")[ch].reshape(DC2, 128).T)
        m["w_b"] = np.ascontiguousarray(g("L0_attn_W_B")[ch])
        m["w_c"] = np.ascontiguousarray(g("L0_attn_W_C")[ch])
        m["w_dt"] = np.ascontiguousarray(g("L0_attn_W_dt")[ch])
        m["dt_w"] = np.ascontiguousarray(g("L0_attn_dt_w")[:, ch])
        m["dt_b"] = np.ascontiguousarray(
            g("L0_attn_dt_b")[ch].reshape(DC2, 128).T)
        per_core.append(m)
    return per_core


def kernel(**inputs):
    key = "all"
    if key not in _CACHE:
        _CACHE[key] = build_program(upto=key)
    nc = _CACHE[key]
    in_maps = _prep_inputs(inputs)
    res = run_bass_kernel_spmd(nc, in_maps, core_ids=list(range(NC)))
    out = np.concatenate([res.results[c]["out"] for c in range(NC)], axis=0)
    return out[None, :, :]


# version 02
# speedup vs baseline: 1.1220x; 1.1220x over previous
from contextlib import ExitStack

import numpy as np
import concourse.bass as bass
import concourse.mybir as mybir
from concourse.bass_utils import run_bass_kernel_spmd

F32 = mybir.dt.float32
BF16 = mybir.dt.bfloat16
ALU = mybir.AluOpType
ACT = mybir.ActivationFunctionType
AX = mybir.AxisListType

NC = 8
L = 16384
D = 1024
TC = L // NC
DM = 2048
DCH = DM // NC
DC2 = DCH // 128
DS = 16
DR = 128
E = 8
H = 256
HC = H // 128
RA = 64
TS = 1024
NSEG = L // TS
KD = D // 128
F = 512
NT = TC // F
EPS = 1e-5

_CACHE = {}


class Sem:
    __slots__ = ("h", "n")

    def __init__(self, nc, name):
        self.h = nc.alloc_semaphore(name)
        self.n = 0

    def inc(self, instr, k=1):
        instr.then_inc(self.h, k)
        self.n += k
        return instr

    def incd(self, instr):
        return self.inc(instr, 16)

    def wait(self, eng, val=None):
        eng.wait_ge(self.h, self.n if val is None else val)


class DStream:
    _uid = [0]

    def __init__(self, nc, name, n=2):
        DStream._uid[0] += 1
        u = DStream._uid[0]
        self.sems = [Sem(nc, f"d{u}_{name}{i}") for i in range(n)]
        self.n = n

    def inc(self, instr, i):
        self.sems[i % self.n].inc(instr, 16)
        return instr

    def wait(self, eng, i):
        eng.wait_ge(self.sems[i % self.n].h, 16 * (i // self.n + 1))


class B:

    def __init__(self, nc, block, stack):
        self.nc = nc
        self.block = block
        self.stack = stack
        self._semi = 0
        self.pV = self.sem("pV")
        self.pS = self.sem("pS")
        self.pT = self.sem("pT")
        self.pG = self.sem("pG")

    def sem(self, tag=""):
        self._semi += 1
        return Sem(self.nc, f"s{self._semi}_{tag}")

    def pooled(self, tag):
        if not hasattr(self, "_pool"):
            self._pool = {}
        if tag not in self._pool:
            self._pool[tag] = self.sem(tag)
        return self._pool[tag]

    def sb(self, name, shape, dt, stack=None):
        return (stack or self.stack).enter_context(self.nc.sbuf_tensor(name, shape, dt))

    def fence(self, scratch, dma_sems=()):
        blk = self.block
        pV, pS, pT, pG = self.pV, self.pS, self.pT, self.pG

        @blk.vector
        def _(v):
            v.wait_ge(pV.h, pV.n)
            pV.inc(v.memset(scratch[0:1, 0:1], 0.0))

        @blk.scalar
        def _(s):
            s.wait_ge(pS.h, pS.n)
            pS.inc(s.memzero(scratch[0:1, 1:2]))

        @blk.gpsimd
        def _(g):
            g.wait_ge(pG.h, pG.n)
            pG.inc(g.memset(scratch[0:1, 2:3], 0.0))

        @blk.tensor
        def _(t):
            pT.inc(t.nop())

        marks = [self.pV, self.pS, self.pT, self.pG] + list(dma_sems)
        for mk in marks:
            @blk.vector
            def _(v, mk=mk):
                mk.wait(v)

            @blk.scalar
            def _(s, mk=mk):
                mk.wait(s)

            @blk.tensor
            def _(t, mk=mk):
                mk.wait(t)

            @blk.gpsimd
            def _(g, mk=mk):
                mk.wait(g)

            @blk.sync
            def _(sy, mk=mk):
                mk.wait(sy)


def TT(eng, out, a, bb, op=ALU.mult):
    return eng.tensor_tensor(out=out, in0=a, in1=bb, op=op)


def TSC(eng, out, a, s1, op0, s2=None, op1=ALU.add):
    if s2 is None:
        return eng.tensor_scalar(out=out, in0=a, scalar1=s1, scalar2=None,
                                 op0=op0, op1=ALU.bypass)
    return eng.tensor_scalar(out=out, in0=a, scalar1=s1, scalar2=s2, op0=op0, op1=op1)


def STT(eng, out, a, s, bb, op0=ALU.mult, op1=ALU.add):
    return eng.scalar_tensor_tensor(out=out, in0=a, scalar=s, in1=bb, op0=op0, op1=op1)


def build_program(upto="all", parts="all"):
    nc = bass.Bass("TRN2", target_bir_lowering=False, detect_race_conditions=False)

    def inp(name, shape):
        return nc.dram_tensor(name, shape, F32, kind="ExternalInput").ap()

    x0_h = inp("x0", [TC, D])
    ident_h = inp("ident", [128, 128])
    sel_h = inp("sel32", [64, 32, 128])
    wbase_h = inp("w_base", [D, D])
    bbase_h = inp("b_base", [128, KD])
    wd1_h = inp("wd1", [E, D, H])
    bd1_h = inp("bd1", [E, 128, HC])
    wd2_h = inp("wd2", [E, H, D])
    bd2_h = inp("bd2", [E, 128, KD])
    wad_h = inp("wad", [E, D, RA])
    bad_h = inp("bad", [RA, E])
    wau_h = inp("wau", [E, RA, D])
    bau_h = inp("bau", [E, 128, KD])
    win_h = inp("w_in", [D, DCH])
    wres_h = inp("w_res", [D, DCH])
    wout_h = inp("w_out", [DM, D])
    convw_h = inp("conv_w", [128, DC2, 4])
    convb_h = inp("conv_b", [128, DC2])
    alog_h = inp("a_log", [128, DC2, DS])
    dvec_h = inp("d_vec", [128, DC2])
    wb_h = inp("w_b", [DCH, DS])
    wc_h = inp("w_c", [DCH, DS])
    wdt_h = inp("w_dt", [DCH, DR])
    dtw_h = inp("dt_w", [DR, DCH])
    dtb_h = inp("dt_b", [128, DC2])

    out_h = nc.dram_tensor("out", [TC, D], F32, kind="ExternalOutput").ap()

    def dram(name, shape, dt=F32):
        return nc.dram_tensor(name, shape, dt).ap()

    wd1c_h = [dram(f"wd1c{e}", [128, KD * H], BF16) for e in range(E)]
    wd2c_h = [dram(f"wd2c{e}", [128, HC * D], BF16) for e in range(E)]
    woutc_h = dram("woutc", [128, (DM // 128) * D], BF16)
    u2_in_h = dram("u2_in", [D, TC], BF16)
    u2_out_h = dram("u2_out", [NC, D, TC], BF16)
    seg_in_h = [dram(f"seg_in{s}", [DR + 2 * DS, TS]) for s in range(NSEG)]
    seg_out_h = [dram(f"seg_out{s}", [DR + 2 * DS, TS]) for s in range(NSEG)]
    loss_in_h = [dram(f"loss_in{c}", [1, E]) for c in range(2)]
    loss_out_h = [dram(f"loss_out{c}", [1, E]) for c in range(2)]
    y_in_h = dram("y_in", [NC, DCH, TC], BF16)
    y_out_h = dram("y_out", [NC, DCH, TC], BF16)
    sspill_h = dram("sspill", [128, KD * TC])

    with ExitStack() as stack, nc.Block() as block:
        b = B(nc, block, stack)
        sb = b.sb

        identf = sb("identf", [128, 128], F32)
        identb = sb("identb", [128, 128], BF16)
        negidb = sb("negidb", [128, 128], BF16)
        ones1b = sb("ones1b", [1, 128], BF16)
        ones128f = sb("ones128f", [128, 1], F32)
        bbase = sb("bbase", [128, KD], F32)
        bd1s = sb("bd1s", [128, E * HC], F32)
        bd2s = sb("bd2s", [128, E * KD], F32)
        bads = sb("bads", [RA, E], F32)
        baus = sb("baus", [128, E * KD], F32)
        convb = sb("convbs", [128, DC2], F32)
        dvec = sb("dvecs", [128, DC2], F32)
        dtbn = sb("dtbn", [128, DC2], F32)
        ppos = sb("ppos", [128, DC2 * DS], F32)
        convdiag = sb("convdiag", [128, DC2 * 4, 128], BF16)
        hb05 = sb("hb05", [128, KD], F32)
        lrow = sb("lrow", [1, E], F32)
        onerow = sb("onerow", [1, E], F32)
        maskrow = sb("maskrow", [1, E], F32)
        tiny = sb("tiny", [1, 8], F32)
        fbuf = sb("fbuf", [1, 4], F32)
        ones1f = sb("ones1f", [1, 128], F32)
        win_s = sb("win_s", [128, KD, DCH], BF16)
        wres_s = sb("wres_s", [128, KD, DCH], BF16)
        wdt_s = sb("wdt_s", [128, DC2, DR], BF16)
        dtw_s = sb("dtw_s", [128, DCH], BF16)
        wbneg_s = sb("wbneg_s", [128, DC2, DS], BF16)
        wc_s = sb("wc_s", [128, DC2, DS], BF16)
        hstate = sb("hstate", [128, DC2, DS], F32)
        selb = sb("selb", [64, 32, 128], BF16)
        wbase_s = sb("wbase_s", [128, KD, D], BF16)
        wad_all = sb("wad_all", [128, KD, E, RA], BF16)
        wau_all = sb("wau_all", [RA, E, D], BF16)

        pfr = [stack.enter_context(nc.psum_tensor(f"pfr{i}", [128, F], F32))
               for i in range(2)]
        pY = stack.enter_context(nc.psum_tensor("pY", [128, 2 * F], F32))
        pB = stack.enter_context(nc.psum_tensor("pB", [128, 2 * F], F32))
        pC = stack.enter_context(nc.psum_tensor("pC", [128, 2 * F], F32))
        pb = [pfr[0][:, :], pfr[1][:, :], pY[:, 0:F], pY[:, F : 2 * F],
              pB[:, 0:F], pB[:, F : 2 * F], pC[:, 0:F], pC[:, F : 2 * F]]

        es_stream = ExitStack()
        stream = b.sb("stream", [128, KD, TC], F32, es_stream)

        with ExitStack() as p0:
            stage = b.sb("stage", [128, 8192], F32, p0)
            stgb = b.sb("stgb", [128, 8192], BF16, p0)

            s_c = b.sem("const")

            @block.sync
            def _(sy):
                s_c.incd(sy.dma_start(out=identf[:], in_=ident_h[:]))
                s_c.incd(sy.dma_start(out=bbase[:], in_=bbase_h[:]))
                s_c.incd(sy.dma_start(out=bd1s[:].rearrange("p (e h) -> p e h", e=E), in_=bd1_h.rearrange("e p h -> p e h")))
                s_c.incd(sy.dma_start(out=bd2s[:].rearrange("p (e k) -> p e k", e=E), in_=bd2_h.rearrange("e p k -> p e k")))
                s_c.incd(sy.dma_start(out=bads[:], in_=bad_h[:]))
                s_c.incd(sy.dma_start(out=baus[:].rearrange("p (e k) -> p e k", e=E), in_=bau_h.rearrange("e p k -> p e k")))
                s_c.incd(sy.dma_start(out=convb[:], in_=convb_h[:]))
                s_c.incd(sy.dma_start(out=dvec[:], in_=dvec_h[:]))
                s_c.incd(sy.dma_start(out=dtbn[:], in_=dtb_h[:]))
                s_c.incd(sy.dma_start(out=ppos[:], in_=alog_h.rearrange("p a b -> p (a b)")))
                s_c.incd(sy.dma_start(out=hb05[:, 0:8], in_=convw_h.rearrange("p a b -> p (a b)")))

            s_c0 = b.sem("c0")

            @block.vector
            def _(v):
                s_c.wait(v)
                v.tensor_copy(out=identb[:], in_=identf[:])
                TSC(v, negidb[:], identf[:], -1.0, ALU.mult)
                v.memset(ones1b[:], 1.0)
                v.memset(ones128f[:], 1.0)
                v.memset(ones1f[:], 1.0)
                v.memset(hstate.rearrange("p a b -> p (a b)"), 0.0)
                v.memset(onerow[:], 1.0)
                TSC(v, dtbn[:], dtbn[:], -1.0, ALU.mult)
                for dc in range(DC2):
                    for i in range(4):
                        TSC(v, convdiag[:, dc * 4 + i, :], identf[:],
                            hb05[:, dc * 4 + i : dc * 4 + i + 1], ALU.mult)
                s_c0.inc(v.tensor_copy(out=tiny[0:1, 0:2], in_=hb05[0:1, 0:2]))

            @block.scalar
            def _(s):
                s_c.wait(s)
                s_c0.inc(s.activation(out=ppos[:], in_=ppos[:], func=ACT.Exp))

            jobs = []

            def wj(src, dst, cache, parts, cols, neg=False):
                assert cols <= 4096
                jobs.append((src, dst, cache, parts, cols, neg))

            for hh in range(2):
                wj(wbase_h[hh * 512 : (hh + 1) * 512].rearrange("(a p) c -> p a c", p=128),
                   wbase_s[:, hh * 4 : (hh + 1) * 4, :].rearrange("p a c -> p (a c)"),
                   None, 128, KD * D // 2)
            wj(win_h.rearrange("(a p) c -> p a c", p=128),
               win_s.rearrange("p a c -> p (a c)"), None, 128, KD * DCH)
            wj(wres_h.rearrange("(a p) c -> p a c", p=128),
               wres_s.rearrange("p a c -> p (a c)"), None, 128, KD * DCH)
            wj(wdt_h.rearrange("(a p) c -> p a c", p=128),
               wdt_s.rearrange("p a c -> p (a c)"), None, 128, DC2 * DR)
            wj(dtw_h[:], dtw_s[:], None, 128, DCH)
            wj(wb_h.rearrange("(a p) c -> p a c", p=128),
               wbneg_s.rearrange("p a c -> p (a c)"), None, 128, DC2 * DS, neg=True)
            wj(wc_h.rearrange("(a p) c -> p a c", p=128),
               wc_s.rearrange("p a c -> p (a c)"), None, 128, DC2 * DS)
            wj(sel_h.rearrange("k n p -> k (n p)"),
               selb.rearrange("k n p -> k (n p)"), None, 64, 32 * 128)
            for e in range(E):
                wj(wad_h[e].rearrange("(a p) r -> p a r", p=128),
                   wad_all[:, :, e, :], None, 128, KD * RA)
                wj(wau_h[e][:], wau_all[:, e, :], None, RA, D)
                wj(wd1_h[e].rearrange("(a p) c -> p a c", p=128), None,
                   wd1c_h[e][:], 128, KD * H)
                wj(wd2_h[e].rearrange("(a p) c -> p a c", p=128), None,
                   wd2c_h[e][:], 128, HC * D)
            for qq in range(4):
                wj(wout_h[qq * 512 : (qq + 1) * 512].rearrange("(a p) c -> p a c", p=128),
                   None, woutc_h[:, qq * 4 * D : (qq + 1) * 4 * D], 128, 4 * D)

            s_wl = DStream(nc, "wload")
            s_wc = b.sem("wcast")
            s_ws = DStream(nc, "wstore")
            ncache = sum(1 for j in jobs if j[2] is not None)

            @block.sync
            def _(sy):
                for i, (src, dst, cache, parts, cols, neg) in enumerate(jobs):
                    if i >= 2:
                        sy.wait_ge(s_wc.h, i - 1)
                    half = (i % 2) * 4096
                    dst_st = stage[0:parts, half : half + cols]
                    if len(src.shape) == 3:
                        dst_st = dst_st.rearrange("p (a c) -> p a c", c=src.shape[2])
                    s_wl.inc(sy.dma_start(out=dst_st, in_=src), i)

            cache_seen = [0]

            @block.vector
            def _(v):
                nstore = 0
                for i, (src, dst, cache, parts, cols, neg) in enumerate(jobs):
                    half = (i % 2) * 4096
                    s_wl.wait(v, i)
                    if cache is not None:
                        if nstore >= 2:
                            s_ws.wait(v, nstore - 2)
                        tgt = stgb[0:parts, (nstore % 2) * 4096 : (nstore % 2) * 4096 + cols]
                        nstore += 1
                    else:
                        tgt = dst
                    if neg:
                        s_wc.inc(TSC(v, tgt, stage[0:parts, half : half + cols], -1.0, ALU.mult))
                    else:
                        s_wc.inc(v.tensor_copy(out=tgt, in_=stage[0:parts, half : half + cols]))

            @block.scalar
            def _(s):
                nstore = 0
                for i, (src, dst, cache, parts, cols, neg) in enumerate(jobs):
                    if cache is None:
                        continue
                    s.wait_ge(s_wc.h, i + 1)
                    src_b = stgb[0:parts, (nstore % 2) * 4096 : (nstore % 2) * 4096 + cols]
                    s_ws.inc(s.dma_start(out=cache, in_=src_b), nstore)
                    nstore += 1

            s_si = DStream(nc, "sin")
            s_tr = b.sem("trp")
            s_se = b.sem("sev")

            @block.sync
            def _(sy):
                sy.wait_ge(s_wc.h, len(jobs))
                for tt in range(TC // 128):
                    half = (tt % 2) * 1024
                    if tt >= 2:
                        sy.wait_ge(s_tr.h, (tt - 1) * KD)
                    s_si.inc(sy.dma_start(out=stage[:, half : half + 1024],
                                          in_=x0_h[tt * 128 : (tt + 1) * 128, :]), tt)

            @block.tensor
            def _(t):
                for tt in range(TC // 128):
                    s_si.wait(t, tt)
                    half = (tt % 2) * 1024
                    for kc in range(KD):
                        j = tt * KD + kc
                        if j >= 2:
                            t.wait_ge(s_se.h, j - 1)
                        s_tr.inc(t.transpose(pb[j % 2][:, 0:128],
                                             stage[:, half + kc * 128 : half + (kc + 1) * 128],
                                             identf[:]))

            @block.vector
            def _(v):
                for tt in range(TC // 128):
                    for kc in range(KD):
                        j = tt * KD + kc
                        v.wait_ge(s_tr.h, j + 1)
                        s_se.inc(v.tensor_copy(
                            out=stream[:, kc, tt * 128 : (tt + 1) * 128],
                            in_=pb[j % 2][:, 0:128]))

            b.fence(fbuf, dma_sems=s_ws.sems + s_si.sems)

        def emit_ln(uout, tag, lp=4):
            s_sq = b.pooled("lnsq")
            s_mm = b.pooled("lnmm")
            s_vs = b.pooled("lnvs")
            s_ro = b.pooled("lnro")
            s_bc = b.pooled("lnbc")
            s_ap = b.pooled("lnap")
            b_sq, b_mm, b_vs = s_sq.n, s_mm.n, s_vs.n
            b_ro, b_bc, b_ap = s_ro.n, s_bc.n, s_ap.n
            with ExitStack() as px:
                lnsq = b.sb(f"ln{tag}s", [128, KD, F], F32, px)
                lnm = b.sb(f"ln{tag}m", [1, TC], F32, px)
                lnr = b.sb(f"ln{tag}r", [1, TC], F32, px)
                lnap = b.sb(f"ln{tag}a", [128, F], F32, px)
                lnq = b.sb(f"ln{tag}q", [1, F], F32, px)
                lnsc = b.sb(f"ln{tag}c", [1, F], F32, px)

                @block.scalar
                def _(sc):
                    for tt in range(NT + 1):
                        if tt < NT:
                            if tt >= 1:
                                sc.wait_ge(s_mm.h, b_mm + 2 * tt)
                            ss = None
                            for kc in range(KD):
                                ss = sc.activation(out=lnsq[:, kc, :],
                                                   in_=stream[:, kc, tt * F : (tt + 1) * F],
                                                   func=ACT.Square)
                            s_sq.inc(ss)
                        if tt >= 1 and lp >= 2:
                            u = tt - 1
                            sc.wait_ge(s_vs.h, b_vs + u + 1)
                            sl = slice(u * F, (u + 1) * F)
                            sc.activation(out=lnr[0:1, sl], in_=lnq[0:1, :], func=ACT.Ln)
                            sc.drain()
                            s_ro.inc(sc.activation(out=lnr[0:1, sl], in_=lnr[0:1, sl],
                                                   func=ACT.Exp, scale=-0.5))

                @block.tensor
                def _(t):
                    for tt in range(NT + 1):
                        if tt < NT:
                            if tt >= 1:
                                t.wait_ge(s_vs.h, b_vs + tt)
                            for kc in range(KD):
                                t.matmul(pb[6][0:1, 0:F], ones128f[:],
                                         stream[:, kc, tt * F : (tt + 1) * F],
                                         start=(kc == 0), stop=(kc == KD - 1))
                            t.wait_ge(s_sq.h, b_sq + tt + 1)
                            mm = None
                            for kc in range(KD):
                                mm = t.matmul(pb[7][0:1, 0:F], ones128f[:], lnsq[:, kc, :],
                                              start=(kc == 0), stop=(kc == KD - 1))
                            s_mm.inc(mm, 2)
                        if tt >= 1 and lp >= 3:
                            u = tt - 1
                            t.wait_ge(s_ro.h, b_ro + u + 1)
                            if u >= 1:
                                t.wait_ge(s_ap.h, b_ap + u * KD)
                            t.matmul(pb[4][:, 0:F], ones1f[:],
                                     lnm[0:1, u * F : (u + 1) * F], start=True, stop=True)
                            s_bc.inc(t.matmul(pb[5][:, 0:F], ones1f[:],
                                              lnr[0:1, u * F : (u + 1) * F],
                                              start=True, stop=True))

                @block.vector
                def _(v):
                    for tt in range(NT + 1):
                        if tt < NT:
                            v.wait_ge(s_mm.h, b_mm + 2 * (tt + 1))
                            if tt >= 1:
                                v.wait_ge(s_ro.h, b_ro + tt)
                            sl = slice(tt * F, (tt + 1) * F)
                            TSC(v, lnm[0:1, sl], pb[6][0:1, 0:F], 1.0 / D, ALU.mult)
                            TSC(v, lnq[0:1, :], pb[7][0:1, 0:F], 1.0 / D, ALU.mult)
                            v.drain()
                            TT(v, lnsc[0:1, :], lnm[0:1, sl], lnm[0:1, sl])
                            v.drain()
                            TT(v, lnq[0:1, :], lnq[0:1, :], lnsc[0:1, :], ALU.subtract)
                            v.drain()
                            s_vs.inc(TSC(v, lnq[0:1, :], lnq[0:1, :], EPS, ALU.add))
                        if tt >= 1 and lp >= 4:
                            u = tt - 1
                            v.wait_ge(s_bc.h, b_bc + u + 1)
                            sl = slice(u * F, (u + 1) * F)
                            for kc in range(KD):
                                STT(v, lnap[:], stream[:, kc, sl], 1.0,
                                    pb[4][:, 0:F], ALU.mult, ALU.subtract)
                                v.drain()
                                s_ap.inc(TT(v, uout[:, kc, sl], lnap[:], pb[5][:, 0:F]))
                                v.drain()

                b.fence(fbuf)

        def emit_ffn(uin, call_idx, parts="all"):
            P = {"experts": 1, "loss": 2, "base": 3, "all": 4}[parts]
            s_w = DStream(nc, "fw", 4)
            s_h = b.pooled("fh")
            s_he = b.pooled("fhe")
            s_r = b.pooled("fr")
            s_sqd = b.pooled("fsq")
            s_ls = b.pooled("fls")
            s_ccd = b.pooled("fcc")
            s_msk = b.pooled("fmsk")
            s_sel = b.pooled("fsel")
            s_bmm = b.pooled("fbmm")
            s_bev = b.pooled("fbev")
            s_amm = b.pooled("famm")
            s_aev = b.pooled("faev")
            s_omm = b.pooled("fomm")
            s_oev = b.pooled("foev")
            b_h, b_he, b_r, b_sqd, b_ls = s_h.n, s_he.n, s_r.n, s_sqd.n, s_ls.n
            b_msk, b_bmm, b_bev = s_msk.n, s_bmm.n, s_bev.n
            b_amm, b_aev, b_omm, b_oev = s_amm.n, s_aev.n, s_omm.n, s_oev.n

            with ExitStack() as px:
                wexp = [b.sb(f"f{call_idx}wex{i}", [128, KD * H + HC * D], BF16, px) for i in range(2)]
                hbuf = b.sb(f"f{call_idx}hbuf", [128, HC, TC], BF16, px)
                abuf = b.sb(f"f{call_idx}abuf", [RA, TC], BF16, px)
                accb = b.sb(f"f{call_idx}accb", [128, E * KD * NT], F32, px)
                lcol = b.sb(f"f{call_idx}lcol", [128, E], F32, px)
                wadsel = b.sb(f"f{call_idx}wadsel", [128, KD, RA], BF16, px)
                wausel = b.sb(f"f{call_idx}wausel", [RA, D], BF16, px)
                badsel = b.sb(f"f{call_idx}badsel", [RA, 1], F32, px)

                @block.sync
                def _(sy):
                    for e in range(E):
                        if e >= 2:
                            sy.wait_ge(s_r.h, b_r + (e - 1) * KD * NT)
                        w = wexp[e % 2]
                        s_w.inc(sy.dma_start(out=w[:, 0 : KD * H], in_=wd1c_h[e][:]), 2 * e)
                        s_w.inc(sy.dma_start(out=w[:, KD * H :], in_=wd2c_h[e][:]), 2 * e + 1)

                @block.tensor
                def _(t):
                    for e in range(E):
                        s_w.wait(t, 2 * e)
                        s_w.wait(t, 2 * e + 1)
                        w = wexp[e % 2]
                        wd1v = w[:, 0 : KD * H].rearrange("p (a c) -> p a c", c=H)
                        wd2v = w[:, KD * H :].rearrange("p (a c) -> p a c", c=D)
                        for hc in range(HC):
                            for tt in range(NT):
                                j = hc * NT + tt
                                bank = pb[j % 2]
                                jj = e * HC * NT + j
                                if jj >= 2:
                                    t.wait_ge(s_he.h, b_he + jj - 1)
                                mm = None
                                for kc in range(KD):
                                    mm = t.matmul(bank[:, 0:F],
                                                  wd1v[:, kc, hc * 128 : (hc + 1) * 128],
                                                  uin[:, kc, tt * F : (tt + 1) * F],
                                                  start=(kc == 0), stop=(kc == KD - 1))
                                s_h.inc(mm)
                        t.wait_ge(s_he.h, b_he + (e + 1) * HC * NT)
                        for pc in range(KD):
                            for tt in range(NT):
                                j = pc * NT + tt
                                bank = pb[2 + j % 2]
                                jj = e * KD * NT + j
                                if jj >= 2:
                                    t.wait_ge(s_sqd.h, b_sqd + jj - 1)
                                for hc in range(HC):
                                    t.matmul(bank[:, 0:F],
                                             wd2v[:, hc, pc * 128 : (pc + 1) * 128],
                                             hbuf[:, hc, tt * F : (tt + 1) * F],
                                             start=(hc == 0), stop=False)
                                s_r.inc(t.matmul(bank[:, 0:F], negidb[:],
                                                 uin[:, pc, tt * F : (tt + 1) * F],
                                                 start=False, stop=True))

                @block.scalar
                def _(s):
                    for e in range(E):
                        for hc in range(HC):
                            for tt in range(NT):
                                j = hc * NT + tt
                                s.wait_ge(s_h.h, b_h + e * HC * NT + j + 1)
                                bank = pb[j % 2]
                                s_he.inc(s.activation(
                                    out=hbuf[:, hc, tt * F : (tt + 1) * F],
                                    in_=bank[:, 0:F], func=ACT.Relu,
                                    bias=bd1s[:, e * HC + hc : e * HC + hc + 1]))
                        for pc in range(KD):
                            for tt in range(NT):
                                j = pc * NT + tt
                                s.wait_ge(s_r.h, b_r + e * KD * NT + j + 1)
                                bank = pb[2 + j % 2]
                                s_sqd.inc(s.activation(
                                    out=bank[:, 0:F], in_=bank[:, 0:F], func=ACT.Square,
                                    bias=bd2s[:, e * KD + pc : e * KD + pc + 1],
                                    accum_out=accb[:, e * KD * NT + j : e * KD * NT + j + 1]))

                @block.tensor
                def _(t):
                    if P < 3:
                        return
                    for pc in range(KD):
                        for tt in range(NT):
                            j = pc * NT + tt
                            bank = pb[4 + j % 2]
                            if j >= 2:
                                t.wait_ge(s_bev.h, b_bev + j - 1)
                            mm = None
                            for kc in range(KD):
                                mm = t.matmul(bank[:, 0:F],
                                              wbase_s[:, kc, pc * 128 : (pc + 1) * 128],
                                              uin[:, kc, tt * F : (tt + 1) * F],
                                              start=(kc == 0), stop=(kc == KD - 1))
                            s_bmm.inc(mm)

                @block.vector
                def _(v):
                    if P < 3:
                        return
                    for pc in range(KD):
                        for tt in range(NT):
                            j = pc * NT + tt
                            v.wait_ge(s_bmm.h, b_bmm + j + 1)
                            bank = pb[4 + j % 2]
                            sl = slice(tt * F, (tt + 1) * F)
                            s_bev.inc(STT(v, stream[:, pc, sl], bank[:, 0:F], 0.5,
                                          stream[:, pc, sl]))

                @block.vector
                def _(v):
                    if P < 2:
                        return
                    for e in range(E):
                        v.wait_ge(s_sqd.h, b_sqd + (e + 1) * KD * NT)
                        s_ls.inc(v.tensor_reduce(out=lcol[:, e : e + 1],
                                                 in_=accb[:, e * KD * NT : (e + 1) * KD * NT], axis=AX.X, op=ALU.add))

                @block.tensor
                def _(t):
                    if P < 2:
                        return
                    t.wait_ge(s_ls.h, b_ls + E)
                    s_ls.inc(t.matmul(pb[6][0:1, 0:E], ones128f[:], lcol[:, :],
                                      start=True, stop=True))

                @block.vector
                def _(v):
                    if P < 2:
                        return
                    v.wait_ge(s_ls.h, b_ls + E + 1)
                    s_ls.inc(v.tensor_copy(out=lrow[:], in_=pb[6][0:1, 0:E]))

                s_cd1 = b.pooled("fcd1")
                s_cd2 = b.pooled("fcd2")
                b_cd1, b_ccd2 = s_cd1.n, s_ccd.n

                @block.gpsimd
                def _(g):
                    if P < 2:
                        return
                    g.wait_ge(s_ls.h, b_ls + E + 2)
                    s_cd1.incd(g.dma_start(out=loss_in_h[call_idx][:], in_=lrow[:]))
                    g.wait_ge(s_cd1.h, b_cd1 + 16)
                    s_ccd.inc(g.collective_compute(
                        "AllReduce", ALU.add, replica_groups=[list(range(NC))],
                        ins=[loss_in_h[call_idx].opt()],
                        outs=[loss_out_h[call_idx].opt()]))
                    g.wait_ge(s_ccd.h, b_ccd2 + 1)
                    s_cd2.incd(g.dma_start(out=lrow[:], in_=loss_out_h[call_idx][:]))

                @block.vector
                def _(v):
                    if P < 2:
                        return
                    s_cd2.wait(v)
                    s_msk.inc(v.tensor_reduce(out=tiny[0:1, 0:1], in_=lrow[:],
                                              axis=AX.X, op=ALU.min))
                    s_msk.wait(v)
                    TSC(v, maskrow[:], lrow[:], tiny[0:1, 0:1], ALU.is_equal)
                    v.drain()
                    s_msk.inc(v.tensor_tensor_scan(out=tiny[0:1, 0:E], data0=onerow[:],
                                                   data1=maskrow[:], initial=0.0,
                                                   op0=ALU.mult, op1=ALU.add))
                    s_msk.wait(v)
                    TSC(v, tiny[0:1, 0:E], tiny[0:1, 0:E], 1.0, ALU.is_equal)
                    v.drain()
                    s_msk.inc(TT(v, maskrow[:], maskrow[:], tiny[0:1, 0:E]))

                @block.tensor
                def _(t):
                    if P < 2:
                        return
                    s_msk.wait(t)
                    s_msk.inc(t.matmul(pb[6][:, 0:E], ones1f[:], maskrow[0:1, :],
                                       start=True, stop=True))

                @block.vector
                def _(v):
                    if P < 4:
                        return
                    s_msk.wait(v)
                    for e in range(E):
                        if e > 0:
                            v.drain()
                        sc = pb[6][:, e : e + 1]
                        scr = pb[6][0:RA, e : e + 1]
                        wadv = wadsel[:, :, :]
                        wallv = wad_all[:, :, e, :]
                        if e == 0:
                            TSC(v, wadv, wallv, sc, ALU.mult)
                            TSC(v, wausel[:], wau_all[:, e, :], scr, ALU.mult)
                            TSC(v, badsel[:], bads[:, e : e + 1], scr, ALU.mult)
                            TSC(v, hb05[:], baus[:, e * KD : (e + 1) * KD], sc, ALU.mult)
                        else:
                            STT(v, wadv, wallv, sc, wadv)
                            STT(v, wausel[:], wau_all[:, e, :], scr, wausel[:])
                            STT(v, badsel[:], bads[:, e : e + 1], scr, badsel[:])
                            STT(v, hb05[:], baus[:, e * KD : (e + 1) * KD], sc, hb05[:])
                    v.drain()
                    STT(v, hb05[:], bbase[:], 1.0, hb05[:], ALU.mult, ALU.add)
                    v.drain()
                    s_sel.inc(TSC(v, hb05[:], hb05[:], 0.5, ALU.mult))

                @block.tensor
                def _(t):
                    if P < 4:
                        return
                    s_sel.wait(t)
                    for tt in range(NT):
                        bank = pb[0 + tt % 2]
                        if tt >= 2:
                            t.wait_ge(s_aev.h, b_aev + tt - 1)
                        mm = None
                        for kc in range(KD):
                            mm = t.matmul(bank[0:RA, 0:F], wadsel[:, kc, :],
                                          uin[:, kc, tt * F : (tt + 1) * F],
                                          start=(kc == 0), stop=(kc == KD - 1))
                        s_amm.inc(mm)

                @block.scalar
                def _(s):
                    if P < 4:
                        return
                    for tt in range(NT):
                        s.wait_ge(s_amm.h, b_amm + tt + 1)
                        bank = pb[0 + tt % 2]
                        s_aev.inc(s.activation(out=abuf[:, tt * F : (tt + 1) * F],
                                               in_=bank[0:RA, 0:F], func=ACT.Relu,
                                               bias=badsel[:]))

                @block.tensor
                def _(t):
                    if P < 4:
                        return
                    for pc in range(KD):
                        for tt in range(NT):
                            j = pc * NT + tt
                            bank = pb[2 + j % 2]
                            t.wait_ge(s_aev.h, b_aev + tt + 1)
                            if j >= 2:
                                t.wait_ge(s_oev.h, b_oev + j - 1)
                            s_omm.inc(t.matmul(bank[:, 0:F],
                                               wausel[:, pc * 128 : (pc + 1) * 128],
                                               abuf[:, tt * F : (tt + 1) * F],
                                               start=True, stop=True))

                @block.vector
                def _(v):
                    if P < 4:
                        return
                    s_bev.wait(v)
                    for pc in range(KD):
                        for tt in range(NT):
                            j = pc * NT + tt
                            v.wait_ge(s_omm.h, b_omm + j + 1)
                            bank = pb[2 + j % 2]
                            sl = slice(tt * F, (tt + 1) * F)
                            STT(v, stream[:, pc, sl], bank[:, 0:F], 0.5, stream[:, pc, sl])
                            v.drain()
                            s_oev.inc(TSC(v, stream[:, pc, sl], stream[:, pc, sl],
                                          hb05[:, pc : pc + 1], ALU.add))

                b.fence(fbuf)

        with ExitStack() as pf:
            u1f = b.sb("u1f", [128, KD, TC], BF16, pf)
            if upto != "p0":
                lnlvl = {"lnA": 1, "lnB": 2, "lnC": 3}.get(upto, 4)
                emit_ln(u1f, "1", lp=lnlvl)
            if upto not in ("ln1", "p0", "lnA", "lnB", "lnC"):
                emit_ffn(u1f, 0, parts=("all" if upto in ("ffn1", "all") else upto))

        if upto in ("ffn1", "ln1", "experts", "loss", "base", "p0", "lnA", "lnB", "lnC"):
            emit_store(nc, block, b, stream, pb, identf, out_h, fbuf)
            es_stream.close()
            return nc

        s_ag = b.sem("ag")
        with ExitStack() as p3:
            u2f = b.sb("u2f", [128, KD, TC], BF16, p3)
            emit_ln(u2f, "2")
            s_u2d = b.sem("u2d")
            s_spl = b.sem("spl")

            @block.sync
            def _(sy):
                s_u2d.incd(sy.dma_start(out=u2_in_h.rearrange("(a p) t -> p a t", p=128),
                                        in_=u2f[:, :, :]))
                s_spl.incd(sy.dma_start(out=sspill_h[:],
                                        in_=stream.rearrange("p a t -> p (a t)")))

            @block.gpsimd
            def _(g):
                g.wait_ge(s_u2d.h, 16)
                s_ag.inc(g.collective_compute(
                    "AllGather", ALU.bypass, replica_groups=[list(range(NC))],
                    ins=[u2_in_h.opt()], outs=[u2_out_h.opt()]))

            b.fence(fbuf, dma_sems=[s_spl, s_u2d])
        es_stream.close()

        with ExitStack() as p4:
            u2t = [b.sb(f"u2t{i}", [128, KD, TS], BF16, p4) for i in range(2)]
            xcv = [b.sb(f"xcq{i}", [128, DC2, TS], F32, p4) for i in range(2)]
            xcb = b.sb("xcbq", [128, DC2, TS], BF16, p4)
            srv = [b.sb(f"srq{i}", [128, DC2, TS], F32, p4) for i in range(2)]
            xconv = [b.sb(f"xcvq{i}", [128, DC2, 3 + TS], BF16, p4) for i in range(2)]
            dpre = [b.sb(f"dpreq{i}", [128, TS], F32, p4) for i in range(2)]
            dpreb = b.sb("dprebq", [128, TS], BF16, p4)
            bcstage = [b.sb(f"bcstq{i}", [64, TS], F32, p4) for i in range(2)]
            bcb = [b.sb(f"bcbq{i}", [64, TS], BF16, p4) for i in range(2)]
            dnegv = b.sb("dnegv", [128, DC2, TS], F32, p4)
            dxn = b.sb("dxnq", [128, DC2, TS], F32, p4)
            scan_a = b.sb("scana", [128, 2, TS], F32, p4)
            scan_u = b.sb("scanu", [128, TS], F32, p4)
            scan_h = b.sb("scanh", [128, TS], F32, p4)
            scan_g = b.sb("scang", [128, 2, TS], BF16, p4)
            ytmp = b.sb("ytmp", [128, F], F32, p4)
            ybuf = b.sb("ybufq", [128, DC2, TS], BF16, p4)

            @block.vector
            def _(v):
                for i in range(2):
                    v.memset(bcb[i][0:64, :], 0.0)

            s_u2l = DStream(nc, "u2l")
            s_fm = b.sem("fm")
            s_fe = b.sem("fe")
            s_cb = b.sem("cb")
            s_gd = DStream(nc, "gd")
            s_ar = b.sem("ar")
            s_ld = DStream(nc, "arld")
            s_dc = b.sem("dcst")
            s_zm = b.sem("zm")
            s_ss = b.sem("sself")
            s_vs2 = b.sem("vself")
            s_bb = b.sem("bb")
            s_bc2 = b.sem("bc2")
            s_ze = b.sem("ze")
            s_dx = b.sem("dx")
            s_ae = b.sem("ae")
            s_ub = b.sem("ub")
            s_sc = b.sem("sc")
            s_hs = b.sem("hs")
            s_cm = b.sem("cm")
            s_ir = b.sem("ir")
            s_yf = b.sem("yf")
            s_ys = DStream(nc, "ys")
            s_sd = b.sem("sd")

            fm_after_res = {}
            fm_c = [0]
            fe_c = [0]

            def front(sy_ops, s):
                bq = s % 2
                @block.sync
                def _(sy):
                    s_ag.wait(sy)
                    if s >= 2:
                        sy.wait_ge(s_fm.h, fm_after_res[s - 2])
                    src = u2_out_h[s // 2].rearrange("(a p) t -> p a t", p=128)
                    s_u2l.inc(sy.dma_start(
                        out=u2t[bq][:, :, :],
                        in_=src[:, :, (s % 2) * TS : (s % 2) * TS + TS]), s)

                jobs = []
                for dc in range(DC2):
                    for tt in range(2):
                        jobs.append(("x", dc, tt))
                for dc in range(DC2):
                    for tt in range(2):
                        jobs.append(("conv", dc, tt))
                for dc in range(DC2):
                    for tt in range(2):
                        jobs.append(("res", dc, tt))
                jobs += [("B", None, 0), ("B", None, 1), ("C", None, 0), ("C", None, 1),
                         ("dpre", None, 0), ("dpre", None, 1)]

                fm0 = fm_c[0]
                fe0 = fe_c[0]

                @block.tensor
                def _(t):
                    for j, (kind, dc, tt) in enumerate(jobs):
                        bank = pb[j % 2]
                        gj = fm0 + j
                        if gj >= 2:
                            t.wait_ge(s_fe.h, gj - 1)
                        if j == 0 and s >= 2:
                            t.wait_ge(s_yf.h, 4 * (s - 1))
                        if kind == "x":
                            if j == 0:
                                s_u2l.wait(t, s)
                            mm = None
                            for kc in range(KD):
                                mm = t.matmul(bank[:, 0:F],
                                              win_s[:, kc, dc * 128 : (dc + 1) * 128],
                                              u2t[bq][:, kc, tt * F : (tt + 1) * F],
                                              start=(kc == 0), stop=(kc == KD - 1))
                        elif kind == "conv":
                            t.wait_ge(s_fe.h, fe0 + dc * 2 + 2)
                            mm = None
                            for i4 in range(4):
                                mm = t.matmul(bank[:, 0:F],
                                              convdiag[:, dc * 4 + i4, :],
                                              xconv[bq][:, dc, tt * F + i4 : tt * F + i4 + F],
                                              start=(i4 == 0), stop=(i4 == 3))
                        elif kind == "res":
                            mm = None
                            for kc in range(KD):
                                mm = t.matmul(bank[:, 0:F],
                                              wres_s[:, kc, dc * 128 : (dc + 1) * 128],
                                              u2t[bq][:, kc, tt * F : (tt + 1) * F],
                                              start=(kc == 0), stop=(kc == KD - 1))
                        elif kind == "B":
                            t.wait_ge(s_cb.h, 2 * s + 2)
                            mm = None
                            for dcc in range(DC2):
                                mm = t.matmul(bank[0:DS, 0:F],
                                              wbneg_s[:, dcc, :],
                                              xcb[:, dcc, tt * F : (tt + 1) * F],
                                              start=(dcc == 0), stop=(dcc == DC2 - 1))
                        elif kind == "C":
                            mm = None
                            for dcc in range(DC2):
                                mm = t.matmul(bank[0:DS, 0:F],
                                              wc_s[:, dcc, :],
                                              xcb[:, dcc, tt * F : (tt + 1) * F],
                                              start=(dcc == 0), stop=(dcc == DC2 - 1))
                        elif kind == "dpre":
                            mm = None
                            for dcc in range(DC2):
                                mm = t.matmul(bank[:, 0:F],
                                              wdt_s[:, dcc, :],
                                              xcb[:, dcc, tt * F : (tt + 1) * F],
                                              start=(dcc == 0), stop=(dcc == DC2 - 1))
                        s_fm.inc(mm)
                fm_c[0] = fm0 + len(jobs)
                fm_after_res[s] = fm0 + 12

                @block.vector
                def _(v):
                    if s >= 1:
                        v.wait_ge(s_fe.h, fe0 - 18 + 4)
                    if s >= 2:
                        v.wait_ge(s_sd.h, s - 1)
                    if s == 0:
                        for dc in range(DC2):
                            v.memset(xconv[bq][:, dc, 0:3], 0.0)
                    else:
                        for dc in range(DC2):
                            v.tensor_copy(out=xconv[bq][:, dc, 0:3],
                                          in_=xconv[1 - bq][:, dc, TS : TS + 3])

                @block.scalar
                def _(sc):
                    if s >= 2:
                        sc.wait_ge(s_sd.h, s - 1)
                    for j in range(12):
                        kind, dc, tt = jobs[j]
                        sc.wait_ge(s_fm.h, fm0 + j + 1)
                        bank = pb[j % 2]
                        if kind == "x":
                            ev = sc.activation(out=xconv[bq][:, dc, 3 + tt * F : 3 + (tt + 1) * F],
                                               in_=bank[:, 0:F], func=ACT.Identity)
                        elif kind == "conv":
                            ev = sc.activation(out=xcv[bq][:, dc, tt * F : (tt + 1) * F],
                                               in_=bank[:, 0:F], func=ACT.Silu,
                                               bias=convb[:, dc : dc + 1])
                        elif kind == "res":
                            ev = sc.activation(out=srv[bq][:, dc, tt * F : (tt + 1) * F],
                                               in_=bank[:, 0:F], func=ACT.Silu)
                        s_fe.inc(ev)

                @block.vector
                def _(v):
                    if s >= 1:
                        v.wait_ge(s_fm.h, fm0 - 18 + 18)
                    for dc in range(DC2):
                        v.wait_ge(s_fe.h, fe0 + 4 + dc * 2 + 2)
                        s_cb.inc(v.tensor_copy(out=xcb[:, dc, :], in_=xcv[bq][:, dc, :]))
                    for j in range(12, 18):
                        kind, dc, tt = jobs[j]
                        v.wait_ge(s_fm.h, fm0 + j + 1)
                        bank = pb[j % 2]
                        if kind == "B":
                            ev = v.tensor_copy(out=bcstage[bq][0:DS, tt * F : (tt + 1) * F],
                                               in_=bank[0:DS, 0:F])
                        elif kind == "C":
                            ev = v.tensor_copy(out=bcstage[bq][32 : 32 + DS, tt * F : (tt + 1) * F],
                                               in_=bank[0:DS, 0:F])
                        elif kind == "dpre":
                            ev = v.tensor_copy(out=dpre[bq][:, tt * F : (tt + 1) * F],
                                               in_=bank[:, 0:F])
                        s_fe.inc(ev)
                fe_c[0] = fe0 + 18

                @block.gpsimd
                def _(g):
                    g.wait_ge(s_fe.h, fe0 + 18)
                    s_gd.inc(g.dma_start(out=seg_in_h[s][0:DR, :], in_=dpre[bq][:]), 2 * s)
                    s_gd.inc(g.dma_start(out=seg_in_h[s][DR : DR + DS, :],
                                         in_=bcstage[bq][0:DS, :]), 2 * s + 1)
                    s_gd.inc(g.dma_start(out=seg_in_h[s][DR + DS : DR + 2 * DS, :],
                                         in_=bcstage[bq][32 : 32 + DS, :]), 2 * s + 1)
                    s_gd.wait(g, 2 * s)
                    s_gd.wait(g, 2 * s + 1)
                    s_ar.inc(g.collective_compute(
                        "AllReduce", ALU.add, replica_groups=[list(range(NC))],
                        ins=[seg_in_h[s].opt()], outs=[seg_out_h[s].opt()]))

                @block.sync
                def _(sy):
                    sy.wait_ge(s_ar.h, s + 1)
                    s_ld.inc(sy.dma_start(out=dpre[bq][:], in_=seg_out_h[s][0:DR, :]), 2 * s)
                    s_ld.inc(sy.dma_start(out=bcstage[bq][0:DS, :],
                                          in_=seg_out_h[s][DR : DR + DS, :]), 2 * s + 1)
                    s_ld.inc(sy.dma_start(out=bcstage[bq][32 : 32 + DS, :],
                                          in_=seg_out_h[s][DR + DS : DR + 2 * DS, :]), 2 * s + 1)

            def scan_seg(s):
                bq = s % 2
                last_seg = (s == NSEG - 1)
                @block.vector
                def _(v):
                    s_ld.wait(v, 2 * s)
                    s_ld.wait(v, 2 * s + 1)
                    v.tensor_copy(out=dpreb[:], in_=dpre[bq][:])
                    v.tensor_copy(out=bcb[bq][0:DS, :], in_=bcstage[bq][0:DS, :])
                    s_dc.inc(v.tensor_copy(out=bcb[bq][32 : 32 + DS, :],
                                           in_=bcstage[bq][32 : 32 + DS, :]), 2)

                zm0 = 4 * s
                @block.tensor
                def _(t):
                    t.wait_ge(s_dc.h, 2 * (s + 1))
                    for j in range(4):
                        dc, tt = j // 2, j % 2
                        bank = pb[2 + j % 2]
                        if j >= 2:
                            t.wait_ge(s_ze.h, zm0 + j - 1)
                        elif s >= 1:
                            t.wait_ge(s_yf.h, 4 * s - 2)
                        mm = t.matmul(bank[:, 0:F], dtw_s[:, dc * 128 : (dc + 1) * 128],
                                      dpreb[:, tt * F : (tt + 1) * F],
                                      start=True, stop=True)
                        s_zm.inc(mm)

                @block.scalar
                def _(sc):
                    for j in range(4):
                        dc, tt = j // 2, j % 2
                        sc.wait_ge(s_zm.h, zm0 + j + 1)
                        bank = pb[2 + j % 2]
                        sg = sc.activation(out=bank[:, 0:F], in_=bank[:, 0:F],
                                           func=ACT.Sigmoid, scale=-1.0,
                                           bias=dtbn[:, dc : dc + 1])
                        s_ss.inc(sg)
                        sc.wait_ge(s_ss.h, s_ss.n)
                        ev = sc.activation(out=dnegv[:, dc, tt * F : (tt + 1) * F],
                                           in_=bank[:, 0:F], func=ACT.Ln)
                        s_ze.inc(ev)

                @block.vector
                def _(v):
                    v.wait_ge(s_ze.h, zm0 + 4)
                    s_dx.inc(TT(v, dxn.rearrange("p a t -> p (a t)"),
                                dnegv.rearrange("p a t -> p (a t)"),
                                xcv[bq].rearrange("p a t -> p (a t)")))

                @block.scalar
                def _(sc):
                    sc.wait_ge(s_ze.h, zm0 + 4)
                    for n in range(DS):
                        for dc in range(DC2):
                            gi = (s * DS + n) * DC2 + dc
                            if gi >= 2:
                                sc.wait_ge(s_sc.h, gi - 1)
                            ev = sc.activation(out=scan_a[:, dc, :],
                                               in_=dnegv[:, dc, :], func=ACT.Exp,
                                               scale=ppos[:, dc * DS + n : dc * DS + n + 1])
                            s_ae.inc(ev)

                @block.tensor
                def _(t):
                    for n in range(DS):
                        ni = s * DS + n
                        if ni >= 1:
                            t.wait_ge(s_ub.h, 2 * ni)
                        for c in range(2):
                            mm = t.matmul(pb[4 + c][:, 0:F], selb[:, n, :],
                                          bcb[bq][:, c * F : (c + 1) * F],
                                          start=True, stop=True)
                        s_bb.inc(mm)
                        if ni >= 1:
                            t.wait_ge(s_cm.h, 2 * ni)
                        for c in range(2):
                            mm = t.matmul(pb[6 + c][:, 0:F], selb[:, DS + n, :],
                                          bcb[bq][:, c * F : (c + 1) * F],
                                          start=True, stop=True)
                        s_bc2.inc(mm)
                        for dc in range(DC2):
                            gi = (s * DS + n) * DC2 + dc
                            t.wait_ge(s_cm.h, gi + 1)
                            if dc == 0:
                                banks = (pb[2], pb[3])
                            else:
                                banks = (pb[0], pb[1])
                                if n == 0:
                                    t.wait_ge(s_fe.h, min(18 * (s + 2), 18 * NSEG))
                            mm = None
                            for c in range(2):
                                mm = t.matmul(banks[c][:, 0:F], identb[:],
                                              scan_g[:, dc, c * F : (c + 1) * F],
                                              start=(n == 0), stop=(n == DS - 1))
                            s_ir.inc(mm)

                @block.vector
                def _(v):
                    v.wait_ge(s_dx.h, s + 1)
                    for n in range(DS):
                        ni = s * DS + n
                        v.wait_ge(s_bb.h, ni + 1)
                        for dc in range(DC2):
                            gi = (s * DS + n) * DC2 + dc
                            if gi >= 1:
                                v.wait_ge(s_sc.h, gi)
                            tv = TT(v, scan_u[:], dxn[:, dc, :], pB[:, :])
                            s_ub.inc(tv)
                            v.wait_ge(s_ub.h, gi + 1)
                            v.wait_ge(s_ae.h, gi + 1)
                            if s >= 1:
                                v.wait_ge(s_hs.h, ((s - 1) * DS + n) * DC2 + dc + 1)
                            tv = v.tensor_tensor_scan(
                                out=scan_h[:], data0=scan_a[:, dc, :],
                                data1=scan_u[:],
                                initial=(0.0 if s == 0 else hstate[:, dc, n : n + 1]),
                                op0=ALU.mult, op1=ALU.add)
                            s_sc.inc(tv)
                            v.wait_ge(s_sc.h, gi + 1)
                            s_hs.inc(v.tensor_copy(out=hstate[:, dc, n : n + 1],
                                                   in_=scan_h[:, TS - 1 : TS]))
                            v.wait_ge(s_bc2.h, ni + 1)
                            if gi >= 2:
                                v.wait_ge(s_ir.h, gi - 1)
                            tv = TT(v, scan_g[:, dc, :], scan_h[:], pC[:, :])
                            s_cm.inc(tv)
                    for dc in range(DC2):
                        v.wait_ge(s_ir.h, (s * DS + DS - 1) * DC2 + dc + 1)
                        if 2 * s + dc - 2 >= 0:
                            s_ys.wait(v, 2 * s + dc - 2)
                        banks = (pb[2], pb[3]) if dc == 0 else (pb[0], pb[1])
                        for c in range(2):
                            tv = STT(v, ytmp[:], xcv[bq][:, dc, c * F : (c + 1) * F],
                                     dvec[:, dc : dc + 1], banks[c][:, 0:F],
                                     ALU.mult, ALU.add)
                            s_vs2.inc(tv)
                            v.wait_ge(s_vs2.h, s_vs2.n)
                            tv = TT(v, ybuf[:, dc, c * F : (c + 1) * F],
                                    ytmp[:], srv[bq][:, dc, c * F : (c + 1) * F])
                            s_yf.inc(tv)
                            v.wait_ge(s_yf.h, s_yf.n)

                @block.sync
                def _(sy):
                    for dc in range(DC2):
                        sy.wait_ge(s_yf.h, 4 * s + 2 * (dc + 1))
                        dst = y_in_h[s // 2, dc * 128 + 0 : dc * 128 + 128,
                                     (s % 2) * TS : (s % 2) * TS + TS]
                        s_ys.inc(sy.dma_start(out=dst, in_=ybuf[:, dc, :]), 2 * s + dc)
                @block.vector
                def _(v):
                    s_sd.inc(v.memset(ytmp[0:1, 0:1], 0.0))

            for s_i in range(NSEG + 1):
                if s_i < NSEG:
                    front(None, s_i)
                if s_i >= 1:
                    scan_seg(s_i - 1)

            @block.sync
            def _(sy):
                for _ss in s_ys.sems:
                    sy.wait_ge(_ss.h, _ss.n)
            b.fence(fbuf, dma_sems=s_ys.sems + s_ld.sems + s_u2l.sems)

        stream = b.sb("stream2", [128, KD, TC], F32, es_stream2 := ExitStack())
        s_a2a = b.sem("a2a")
        with ExitStack() as p5:
            wout_s = b.sb("wout_s", [128, DM // 128, D], BF16, p5)
            ytile = [b.sb(f"ytile{i}", [128, DM // 128, F], BF16, p5) for i in range(2)]
            s_wol = b.sem("wol")
            s_srl = DStream(nc, "srl")
            s_ytl = DStream(nc, "ytl")
            s_om = b.sem("om")
            s_oe = b.sem("oe")

            @block.gpsimd
            def _(g):
                from concourse.aot_env import allow_ici_all_to_all
                with allow_ici_all_to_all():
                    s_a2a.inc(g.collective_compute(
                        "AllToAll", ALU.bypass, replica_groups=[list(range(NC))],
                        ins=[y_in_h.opt()], outs=[y_out_h.opt()]))

            @block.sync
            def _(sy):
                s_wol.incd(sy.dma_start(
                    out=wout_s.rearrange("p a c -> p (a c)"), in_=woutc_h[:]))
                s_wol.incd(sy.dma_start(out=stream.rearrange("p a t -> p (a t)"),
                                        in_=sspill_h[:]))
                s_a2a.wait(sy)
                yv = y_out_h.rearrange("i (c p) t -> p i c t", p=128)
                for tt in range(NT):
                    if tt >= 2:
                        sy.wait_ge(s_om.h, (tt - 1) * KD)
                    s_ytl.inc(sy.dma_start(
                        out=ytile[tt % 2].rearrange("p a f -> p a f"),
                        in_=yv[:, :, :, tt * F : (tt + 1) * F].rearrange("p i c t -> p (i c) t")), tt)

            @block.tensor
            def _(t):
                t.wait_ge(s_wol.h, 16)
                for tt in range(NT):
                    s_ytl.wait(t, tt)
                    for pc in range(KD):
                        j = tt * KD + pc
                        bank = pb[j % 2]
                        if j >= 2:
                            t.wait_ge(s_oe.h, j - 1)
                        mm = None
                        for kc in range(DM // 128):
                            mm = t.matmul(bank[:, 0:F],
                                          wout_s[:, kc, pc * 128 : (pc + 1) * 128],
                                          ytile[tt % 2][:, kc, :],
                                          start=(kc == 0), stop=(kc == DM // 128 - 1))
                        s_om.inc(mm)

            @block.vector
            def _(v):
                v.wait_ge(s_wol.h, 32)
                for tt in range(NT):
                    for pc in range(KD):
                        j = tt * KD + pc
                        v.wait_ge(s_om.h, j + 1)
                        bank = pb[j % 2]
                        s_oe.inc(STT(v, stream[:, pc, tt * F : (tt + 1) * F],
                                     bank[:, 0:F], 1.0,
                                     stream[:, pc, tt * F : (tt + 1) * F]))

            b.fence(fbuf, dma_sems=s_ytl.sems + s_srl.sems)

        with ExitStack() as pf2:
            u3f = b.sb("u3f", [128, KD, TC], BF16, pf2)
            emit_ln(u3f, "3")
            emit_ffn(u3f, 1)

        emit_store(nc, block, b, stream, pb, identf, out_h, fbuf)
        es_stream2.close()
        return nc


def emit_store(nc, block, b, stream, pb, identf, out_h, tiny):
    s_tr = b.sem("otr")
    s_ev = b.sem("oev")
    s_st = DStream(nc, "ost")
    with ExitStack() as px:
        ostage = b.sb("ostage", [128, 2, 1024], F32, px)

        @block.tensor
        def _(t):
            for tt in range(TC // 128):
                for kc in range(KD):
                    j = tt * KD + kc
                    if j >= 2:
                        t.wait_ge(s_ev.h, j - 1)
                    s_tr.inc(t.transpose(pb[j % 2][:, 0:128],
                                         stream[:, kc, tt * 128 : (tt + 1) * 128],
                                         identf[:]))

        @block.vector
        def _(v):
            for tt in range(TC // 128):
                if tt >= 2:
                    s_st.wait(v, tt - 2)
                for kc in range(KD):
                    j = tt * KD + kc
                    v.wait_ge(s_tr.h, j + 1)
                    s_ev.inc(v.tensor_copy(out=ostage[:, tt % 2, kc * 128 : (kc + 1) * 128],
                                           in_=pb[j % 2][:, 0:128]))

        @block.sync
        def _(sy):
            for tt in range(TC // 128):
                sy.wait_ge(s_ev.h, (tt + 1) * KD)
                s_st.inc(sy.dma_start(out=out_h[tt * 128 : (tt + 1) * 128, :],
                                      in_=ostage[:, tt % 2, :]), tt)
            for _ss in s_st.sems:
                sy.wait_ge(_ss.h, _ss.n)


def _prep_inputs(inputs):
    g = lambda k: np.asarray(inputs[k])
    x = g("L0_attn_x_in")[0]
    per_core = []
    ident = np.eye(128, dtype=np.float32)
    sel32 = np.zeros((64, 32, 128), dtype=np.float32)
    for j in range(16):
        sel32[j, j, :] = 1.0
        sel32[32 + j, 16 + j, :] = 1.0
    for c in range(NC):
        ch = slice(c * DCH, (c + 1) * DCH)
        m = {}
        m["x0"] = np.ascontiguousarray(x[c * TC : (c + 1) * TC])
        m["ident"] = ident
        m["sel32"] = sel32
        m["w_base"] = np.ascontiguousarray(g("L0_ffn_W_base"))
        m["b_base"] = np.ascontiguousarray(g("L0_ffn_b_base").reshape(KD, 128).T)
        m["wd1"] = np.ascontiguousarray(g("L0_ffn_Wd1"))
        m["bd1"] = np.ascontiguousarray(g("L0_ffn_bd1").reshape(E, HC, 128).transpose(0, 2, 1))
        m["wd2"] = np.ascontiguousarray(g("L0_ffn_Wd2"))
        m["bd2"] = np.ascontiguousarray(g("L0_ffn_bd2").reshape(E, KD, 128).transpose(0, 2, 1))
        m["wad"] = np.ascontiguousarray(g("L0_ffn_Wa_down"))
        m["bad"] = np.ascontiguousarray(g("L0_ffn_ba_down").T)
        m["wau"] = np.ascontiguousarray(g("L0_ffn_Wa_up"))
        m["bau"] = np.ascontiguousarray(g("L0_ffn_ba_up").reshape(E, KD, 128).transpose(0, 2, 1))
        m["w_in"] = np.ascontiguousarray(g("L0_attn_W_in")[:, ch])
        m["w_res"] = np.ascontiguousarray(g("L0_attn_W_res")[:, ch])
        m["w_out"] = np.ascontiguousarray(g("L0_attn_W_out"))
        m["conv_w"] = np.ascontiguousarray(
            g("L0_attn_conv_w")[ch].reshape(DC2, 128, 4).transpose(1, 0, 2))
        m["conv_b"] = np.ascontiguousarray(
            g("L0_attn_conv_b")[ch].reshape(DC2, 128).T)
        m["a_log"] = np.ascontiguousarray(
            g("L0_attn_A_log")[ch].reshape(DC2, 128, DS).transpose(1, 0, 2))
        m["d_vec"] = np.ascontiguousarray(
            g("L0_attn_D")[ch].reshape(DC2, 128).T)
        m["w_b"] = np.ascontiguousarray(g("L0_attn_W_B")[ch])
        m["w_c"] = np.ascontiguousarray(g("L0_attn_W_C")[ch])
        m["w_dt"] = np.ascontiguousarray(g("L0_attn_W_dt")[ch])
        m["dt_w"] = np.ascontiguousarray(g("L0_attn_dt_w")[:, ch])
        m["dt_b"] = np.ascontiguousarray(
            g("L0_attn_dt_b")[ch].reshape(DC2, 128).T)
        per_core.append(m)
    return per_core


def kernel(**inputs):
    key = "all"
    if key not in _CACHE:
        _CACHE[key] = build_program(upto=key)
    nc = _CACHE[key]
    in_maps = _prep_inputs(inputs)
    res = run_bass_kernel_spmd(nc, in_maps, core_ids=list(range(NC)))
    out = np.concatenate([res.results[c]["out"] for c in range(NC)], axis=0)
    return out[None, :, :]


# version 03
# speedup vs baseline: 1.2731x; 1.1346x over previous
from contextlib import ExitStack

import numpy as np
import concourse.bass as bass
import concourse.mybir as mybir
from concourse.bass_utils import run_bass_kernel_spmd

F32 = mybir.dt.float32
BF16 = mybir.dt.bfloat16
ALU = mybir.AluOpType
ACT = mybir.ActivationFunctionType
AX = mybir.AxisListType

NC = 8
L = 16384
D = 1024
TC = L // NC
DM = 2048
DCH = DM // NC
DC2 = DCH // 128
DS = 16
DR = 128
E = 8
H = 256
HC = H // 128
RA = 64
TS = 1024
NSEG = L // TS
KD = D // 128
F = 512
NT = TC // F
EPS = 1e-5

_CACHE = {}


class Sem:
    __slots__ = ("h", "n")

    def __init__(self, nc, name):
        self.h = nc.alloc_semaphore(name)
        self.n = 0

    def inc(self, instr, k=1):
        instr.then_inc(self.h, k)
        self.n += k
        return instr

    def incd(self, instr):
        return self.inc(instr, 16)

    def wait(self, eng, val=None):
        eng.wait_ge(self.h, self.n if val is None else val)


class DStream:
    _uid = [0]

    def __init__(self, nc, name, n=2):
        DStream._uid[0] += 1
        u = DStream._uid[0]
        self.sems = [Sem(nc, f"d{u}_{name}{i}") for i in range(n)]
        self.n = n

    def inc(self, instr, i):
        self.sems[i % self.n].inc(instr, 16)
        return instr

    def wait(self, eng, i):
        eng.wait_ge(self.sems[i % self.n].h, 16 * (i // self.n + 1))


class B:

    def __init__(self, nc, block, stack):
        self.nc = nc
        self.block = block
        self.stack = stack
        self._semi = 0
        self.pV = self.sem("pV")
        self.pS = self.sem("pS")
        self.pT = self.sem("pT")
        self.pG = self.sem("pG")

    def sem(self, tag=""):
        self._semi += 1
        return Sem(self.nc, f"s{self._semi}_{tag}")

    def pooled(self, tag):
        if not hasattr(self, "_pool"):
            self._pool = {}
        if tag not in self._pool:
            self._pool[tag] = self.sem(tag)
        return self._pool[tag]

    def sb(self, name, shape, dt, stack=None):
        return (stack or self.stack).enter_context(self.nc.sbuf_tensor(name, shape, dt))

    def fence(self, scratch, dma_sems=()):
        blk = self.block
        pV, pS, pT, pG = self.pV, self.pS, self.pT, self.pG

        @blk.vector
        def _(v):
            v.wait_ge(pV.h, pV.n)
            pV.inc(v.memset(scratch[0:1, 0:1], 0.0))

        @blk.scalar
        def _(s):
            s.wait_ge(pS.h, pS.n)
            pS.inc(s.memzero(scratch[0:1, 1:2]))

        @blk.gpsimd
        def _(g):
            g.wait_ge(pG.h, pG.n)
            pG.inc(g.memset(scratch[0:1, 2:3], 0.0))

        @blk.tensor
        def _(t):
            pT.inc(t.nop())

        marks = [self.pV, self.pS, self.pT, self.pG] + list(dma_sems)
        for mk in marks:
            @blk.vector
            def _(v, mk=mk):
                mk.wait(v)

            @blk.scalar
            def _(s, mk=mk):
                mk.wait(s)

            @blk.tensor
            def _(t, mk=mk):
                mk.wait(t)

            @blk.gpsimd
            def _(g, mk=mk):
                mk.wait(g)

            @blk.sync
            def _(sy, mk=mk):
                mk.wait(sy)


def TT(eng, out, a, bb, op=ALU.mult):
    return eng.tensor_tensor(out=out, in0=a, in1=bb, op=op)


def TSC(eng, out, a, s1, op0, s2=None, op1=ALU.add):
    if s2 is None:
        return eng.tensor_scalar(out=out, in0=a, scalar1=s1, scalar2=None,
                                 op0=op0, op1=ALU.bypass)
    return eng.tensor_scalar(out=out, in0=a, scalar1=s1, scalar2=s2, op0=op0, op1=op1)


def STT(eng, out, a, s, bb, op0=ALU.mult, op1=ALU.add):
    return eng.scalar_tensor_tensor(out=out, in0=a, scalar=s, in1=bb, op0=op0, op1=op1)


def build_program(upto="all", parts="all"):
    nc = bass.Bass("TRN2", target_bir_lowering=False, detect_race_conditions=False)

    def inp(name, shape):
        return nc.dram_tensor(name, shape, F32, kind="ExternalInput").ap()

    x0_h = inp("x0", [TC, D])
    ident_h = inp("ident", [128, 128])
    sel_h = inp("sel32", [64, 32, 128])
    wbase_h = inp("w_base", [D, D])
    bbase_h = inp("b_base", [128, KD])
    wd1_h = inp("wd1", [E, D, H])
    bd1_h = inp("bd1", [E, 128, HC])
    wd2_h = inp("wd2", [E, H, D])
    bd2_h = inp("bd2", [E, 128, KD])
    wad_h = inp("wad", [E, D, RA])
    bad_h = inp("bad", [RA, E])
    wau_h = inp("wau", [E, RA, D])
    bau_h = inp("bau", [E, 128, KD])
    win_h = inp("w_in", [D, DCH])
    wres_h = inp("w_res", [D, DCH])
    wout_h = inp("w_out", [DM, D])
    convw_h = inp("conv_w", [128, DC2, 4])
    convb_h = inp("conv_b", [128, DC2])
    alog_h = inp("a_log", [128, DC2, DS])
    dvec_h = inp("d_vec", [128, DC2])
    wb_h = inp("w_b", [DCH, DS])
    wc_h = inp("w_c", [DCH, DS])
    wdt_h = inp("w_dt", [DCH, DR])
    dtw_h = inp("dt_w", [DR, DCH])
    dtb_h = inp("dt_b", [128, DC2])

    out_h = nc.dram_tensor("out", [TC, D], F32, kind="ExternalOutput").ap()

    def dram(name, shape, dt=F32):
        return nc.dram_tensor(name, shape, dt).ap()

    wd1c_h = [dram(f"wd1c{e}", [128, KD * H], BF16) for e in range(E)]
    wd2c_h = [dram(f"wd2c{e}", [128, HC * D], BF16) for e in range(E)]
    woutc_h = dram("woutc", [128, (DM // 128) * D], BF16)
    u2_in_h = dram("u2_in", [D, TC], BF16)
    u2_out_h = dram("u2_out", [NC, D, TC], BF16)
    seg_in_h = [dram(f"seg_in{s}", [DR + 2 * DS, TS]) for s in range(NSEG)]
    seg_out_h = [dram(f"seg_out{s}", [DR + 2 * DS, TS]) for s in range(NSEG)]
    loss_in_h = [dram(f"loss_in{c}", [1, E]) for c in range(2)]
    loss_out_h = [dram(f"loss_out{c}", [1, E]) for c in range(2)]
    y_in_h = dram("y_in", [NC, DCH, TC], BF16)
    y_out_h = dram("y_out", [NC, DCH, TC], BF16)
    sspill_h = dram("sspill", [128, KD * TC])

    with ExitStack() as stack, nc.Block() as block:
        b = B(nc, block, stack)
        sb = b.sb

        identf = sb("identf", [128, 128], F32)
        identb = sb("identb", [128, 128], BF16)
        negidb = sb("negidb", [128, 128], BF16)
        ones1b = sb("ones1b", [1, 128], BF16)
        ones128f = sb("ones128f", [128, 1], F32)
        bbase = sb("bbase", [128, KD], F32)
        bd1s = sb("bd1s", [128, E * HC], F32)
        bd2s = sb("bd2s", [128, E * KD], F32)
        bads = sb("bads", [RA, E], F32)
        baus = sb("baus", [128, E * KD], F32)
        convb = sb("convbs", [128, DC2], F32)
        dvec = sb("dvecs", [128, DC2], F32)
        dtbn = sb("dtbn", [128, DC2], F32)
        ppos = sb("ppos", [128, DC2 * DS], F32)
        convdiag = sb("convdiag", [128, DC2 * 4, 128], BF16)
        hb05 = sb("hb05", [128, KD], F32)
        lrow = sb("lrow", [1, E], F32)
        onerow = sb("onerow", [1, E], F32)
        maskrow = sb("maskrow", [1, E], F32)
        tiny = sb("tiny", [1, 8], F32)
        fbuf = sb("fbuf", [1, 4], F32)
        ones1f = sb("ones1f", [1, 128], F32)
        win_s = sb("win_s", [128, KD, DCH], BF16)
        wres_s = sb("wres_s", [128, KD, DCH], BF16)
        wdt_s = sb("wdt_s", [128, DC2, DR], BF16)
        dtw_s = sb("dtw_s", [128, DCH], BF16)
        wbneg_s = sb("wbneg_s", [128, DC2, DS], BF16)
        wc_s = sb("wc_s", [128, DC2, DS], BF16)
        hstate = sb("hstate", [128, DC2, DS], F32)
        selb = sb("selb", [64, 32, 128], BF16)
        wbase_s = sb("wbase_s", [128, KD, D], BF16)
        wad_all = sb("wad_all", [128, KD, E, RA], BF16)
        wau_all = sb("wau_all", [RA, E, D], BF16)

        pfr = [stack.enter_context(nc.psum_tensor(f"pfr{i}", [128, F], F32))
               for i in range(2)]
        pY = stack.enter_context(nc.psum_tensor("pY", [128, 2 * F], F32))
        pB = stack.enter_context(nc.psum_tensor("pB", [128, 2 * F], F32))
        pC = stack.enter_context(nc.psum_tensor("pC", [128, 2 * F], F32))
        pb = [pfr[0][:, :], pfr[1][:, :], pY[:, 0:F], pY[:, F : 2 * F],
              pB[:, 0:F], pB[:, F : 2 * F], pC[:, 0:F], pC[:, F : 2 * F]]

        es_stream = ExitStack()
        stream = b.sb("stream", [128, KD, TC], F32, es_stream)

        with ExitStack() as p0:
            stage = b.sb("stage", [128, 8192], F32, p0)
            stgb = b.sb("stgb", [128, 8192], BF16, p0)

            s_c = b.sem("const")

            @block.sync
            def _(sy):
                s_c.incd(sy.dma_start(out=identf[:], in_=ident_h[:]))
                s_c.incd(sy.dma_start(out=bbase[:], in_=bbase_h[:]))
                s_c.incd(sy.dma_start(out=bd1s[:].rearrange("p (e h) -> p e h", e=E), in_=bd1_h.rearrange("e p h -> p e h")))
                s_c.incd(sy.dma_start(out=bd2s[:].rearrange("p (e k) -> p e k", e=E), in_=bd2_h.rearrange("e p k -> p e k")))
                s_c.incd(sy.dma_start(out=bads[:], in_=bad_h[:]))
                s_c.incd(sy.dma_start(out=baus[:].rearrange("p (e k) -> p e k", e=E), in_=bau_h.rearrange("e p k -> p e k")))
                s_c.incd(sy.dma_start(out=convb[:], in_=convb_h[:]))
                s_c.incd(sy.dma_start(out=dvec[:], in_=dvec_h[:]))
                s_c.incd(sy.dma_start(out=dtbn[:], in_=dtb_h[:]))
                s_c.incd(sy.dma_start(out=ppos[:], in_=alog_h.rearrange("p a b -> p (a b)")))
                s_c.incd(sy.dma_start(out=hb05[:, 0:8], in_=convw_h.rearrange("p a b -> p (a b)")))

            s_c0 = b.sem("c0")

            @block.vector
            def _(v):
                s_c.wait(v)
                v.tensor_copy(out=identb[:], in_=identf[:])
                TSC(v, negidb[:], identf[:], -1.0, ALU.mult)
                v.memset(ones1b[:], 1.0)
                v.memset(ones128f[:], 1.0)
                v.memset(ones1f[:], 1.0)
                v.memset(hstate.rearrange("p a b -> p (a b)"), 0.0)
                v.memset(onerow[:], 1.0)
                TSC(v, dtbn[:], dtbn[:], -1.0, ALU.mult)
                for dc in range(DC2):
                    for i in range(4):
                        TSC(v, convdiag[:, dc * 4 + i, :], identf[:],
                            hb05[:, dc * 4 + i : dc * 4 + i + 1], ALU.mult)
                s_c0.inc(v.tensor_copy(out=tiny[0:1, 0:2], in_=hb05[0:1, 0:2]))

            @block.scalar
            def _(s):
                s_c.wait(s)
                s_c0.inc(s.activation(out=ppos[:], in_=ppos[:], func=ACT.Exp))

            jobs = []

            def wj(src, dst, cache, parts, cols, neg=False):
                assert cols <= 4096
                jobs.append((src, dst, cache, parts, cols, neg))

            for hh in range(2):
                wj(wbase_h[hh * 512 : (hh + 1) * 512].rearrange("(a p) c -> p a c", p=128),
                   wbase_s[:, hh * 4 : (hh + 1) * 4, :].rearrange("p a c -> p (a c)"),
                   None, 128, KD * D // 2)
            wj(win_h.rearrange("(a p) c -> p a c", p=128),
               win_s.rearrange("p a c -> p (a c)"), None, 128, KD * DCH)
            wj(wres_h.rearrange("(a p) c -> p a c", p=128),
               wres_s.rearrange("p a c -> p (a c)"), None, 128, KD * DCH)
            wj(wdt_h.rearrange("(a p) c -> p a c", p=128),
               wdt_s.rearrange("p a c -> p (a c)"), None, 128, DC2 * DR)
            wj(dtw_h[:], dtw_s[:], None, 128, DCH)
            wj(wb_h.rearrange("(a p) c -> p a c", p=128),
               wbneg_s.rearrange("p a c -> p (a c)"), None, 128, DC2 * DS, neg=True)
            wj(wc_h.rearrange("(a p) c -> p a c", p=128),
               wc_s.rearrange("p a c -> p (a c)"), None, 128, DC2 * DS)
            wj(sel_h.rearrange("k n p -> k (n p)"),
               selb.rearrange("k n p -> k (n p)"), None, 64, 32 * 128)
            for e in range(E):
                wj(wad_h[e].rearrange("(a p) r -> p a r", p=128),
                   wad_all[:, :, e, :], None, 128, KD * RA)
                wj(wau_h[e][:], wau_all[:, e, :], None, RA, D)
                wj(wd1_h[e].rearrange("(a p) c -> p a c", p=128), None,
                   wd1c_h[e][:], 128, KD * H)
                wj(wd2_h[e].rearrange("(a p) c -> p a c", p=128), None,
                   wd2c_h[e][:], 128, HC * D)
            for qq in range(4):
                wj(wout_h[qq * 512 : (qq + 1) * 512].rearrange("(a p) c -> p a c", p=128),
                   None, woutc_h[:, qq * 4 * D : (qq + 1) * 4 * D], 128, 4 * D)

            s_wl = DStream(nc, "wload")
            s_wc = b.sem("wcast")
            s_ws = DStream(nc, "wstore")
            ncache = sum(1 for j in jobs if j[2] is not None)

            @block.sync
            def _(sy):
                for i, (src, dst, cache, parts, cols, neg) in enumerate(jobs):
                    if i >= 2:
                        sy.wait_ge(s_wc.h, i - 1)
                    half = (i % 2) * 4096
                    dst_st = stage[0:parts, half : half + cols]
                    if len(src.shape) == 3:
                        dst_st = dst_st.rearrange("p (a c) -> p a c", c=src.shape[2])
                    s_wl.inc(sy.dma_start(out=dst_st, in_=src), i)

            cache_seen = [0]

            @block.vector
            def _(v):
                nstore = 0
                for i, (src, dst, cache, parts, cols, neg) in enumerate(jobs):
                    half = (i % 2) * 4096
                    s_wl.wait(v, i)
                    if cache is not None:
                        if nstore >= 2:
                            s_ws.wait(v, nstore - 2)
                        tgt = stgb[0:parts, (nstore % 2) * 4096 : (nstore % 2) * 4096 + cols]
                        nstore += 1
                    else:
                        tgt = dst
                    if neg:
                        s_wc.inc(TSC(v, tgt, stage[0:parts, half : half + cols], -1.0, ALU.mult))
                    else:
                        s_wc.inc(v.tensor_copy(out=tgt, in_=stage[0:parts, half : half + cols]))

            @block.scalar
            def _(s):
                nstore = 0
                for i, (src, dst, cache, parts, cols, neg) in enumerate(jobs):
                    if cache is None:
                        continue
                    s.wait_ge(s_wc.h, i + 1)
                    src_b = stgb[0:parts, (nstore % 2) * 4096 : (nstore % 2) * 4096 + cols]
                    s_ws.inc(s.dma_start(out=cache, in_=src_b), nstore)
                    nstore += 1

            s_si = DStream(nc, "sin")
            s_tr = b.sem("trp")
            s_se = b.sem("sev")

            @block.sync
            def _(sy):
                sy.wait_ge(s_wc.h, len(jobs))
                for tt in range(TC // 128):
                    half = (tt % 2) * 1024
                    if tt >= 2:
                        sy.wait_ge(s_tr.h, (tt - 1) * KD)
                    s_si.inc(sy.dma_start(out=stage[:, half : half + 1024],
                                          in_=x0_h[tt * 128 : (tt + 1) * 128, :]), tt)

            @block.tensor
            def _(t):
                for tt in range(TC // 128):
                    s_si.wait(t, tt)
                    half = (tt % 2) * 1024
                    for kc in range(KD):
                        j = tt * KD + kc
                        if j >= 2:
                            t.wait_ge(s_se.h, j - 1)
                        s_tr.inc(t.transpose(pb[j % 2][:, 0:128],
                                             stage[:, half + kc * 128 : half + (kc + 1) * 128],
                                             identf[:]))

            @block.vector
            def _(v):
                for tt in range(TC // 128):
                    for kc in range(KD):
                        j = tt * KD + kc
                        v.wait_ge(s_tr.h, j + 1)
                        s_se.inc(v.tensor_copy(
                            out=stream[:, kc, tt * 128 : (tt + 1) * 128],
                            in_=pb[j % 2][:, 0:128]))

            b.fence(fbuf, dma_sems=s_ws.sems + s_si.sems)

        def emit_ln(uout, tag, lp=4):
            s_sq = b.pooled("lnsq")
            s_mm = b.pooled("lnmm")
            s_vs = b.pooled("lnvs")
            s_ro = b.pooled("lnro")
            s_bc = b.pooled("lnbc")
            s_ap = b.pooled("lnap")
            b_sq, b_mm, b_vs = s_sq.n, s_mm.n, s_vs.n
            b_ro, b_bc, b_ap = s_ro.n, s_bc.n, s_ap.n
            with ExitStack() as px:
                lnsq = b.sb(f"ln{tag}s", [128, KD, F], F32, px)
                lnm = b.sb(f"ln{tag}m", [1, TC], F32, px)
                lnr = b.sb(f"ln{tag}r", [1, TC], F32, px)
                lnap = b.sb(f"ln{tag}a", [128, F], F32, px)
                lnq = b.sb(f"ln{tag}q", [1, F], F32, px)
                lnsc = b.sb(f"ln{tag}c", [1, F], F32, px)

                @block.scalar
                def _(sc):
                    for tt in range(NT + 1):
                        if tt < NT:
                            if tt >= 1:
                                sc.wait_ge(s_mm.h, b_mm + 2 * tt)
                            ss = None
                            for kc in range(KD):
                                ss = sc.activation(out=lnsq[:, kc, :],
                                                   in_=stream[:, kc, tt * F : (tt + 1) * F],
                                                   func=ACT.Square)
                            s_sq.inc(ss)
                        if tt >= 1 and lp >= 2:
                            u = tt - 1
                            sc.wait_ge(s_vs.h, b_vs + u + 1)
                            sl = slice(u * F, (u + 1) * F)
                            sc.activation(out=lnr[0:1, sl], in_=lnq[0:1, :], func=ACT.Ln)
                            sc.drain()
                            s_ro.inc(sc.activation(out=lnr[0:1, sl], in_=lnr[0:1, sl],
                                                   func=ACT.Exp, scale=-0.5))

                @block.tensor
                def _(t):
                    for tt in range(NT + 1):
                        if tt < NT:
                            if tt >= 1:
                                t.wait_ge(s_vs.h, b_vs + tt)
                            for kc in range(KD):
                                t.matmul(pb[6][0:1, 0:F], ones128f[:],
                                         stream[:, kc, tt * F : (tt + 1) * F],
                                         start=(kc == 0), stop=(kc == KD - 1))
                            t.wait_ge(s_sq.h, b_sq + tt + 1)
                            mm = None
                            for kc in range(KD):
                                mm = t.matmul(pb[7][0:1, 0:F], ones128f[:], lnsq[:, kc, :],
                                              start=(kc == 0), stop=(kc == KD - 1))
                            s_mm.inc(mm, 2)
                        if tt >= 1 and lp >= 3:
                            u = tt - 1
                            t.wait_ge(s_ro.h, b_ro + u + 1)
                            if u >= 1:
                                t.wait_ge(s_ap.h, b_ap + u * KD)
                            t.matmul(pb[4][:, 0:F], ones1f[:],
                                     lnm[0:1, u * F : (u + 1) * F], start=True, stop=True)
                            s_bc.inc(t.matmul(pb[5][:, 0:F], ones1f[:],
                                              lnr[0:1, u * F : (u + 1) * F],
                                              start=True, stop=True))

                @block.vector
                def _(v):
                    for tt in range(NT + 1):
                        if tt < NT:
                            v.wait_ge(s_mm.h, b_mm + 2 * (tt + 1))
                            if tt >= 1:
                                v.wait_ge(s_ro.h, b_ro + tt)
                            sl = slice(tt * F, (tt + 1) * F)
                            TSC(v, lnm[0:1, sl], pb[6][0:1, 0:F], 1.0 / D, ALU.mult)
                            TSC(v, lnq[0:1, :], pb[7][0:1, 0:F], 1.0 / D, ALU.mult)
                            v.drain()
                            TT(v, lnsc[0:1, :], lnm[0:1, sl], lnm[0:1, sl])
                            v.drain()
                            TT(v, lnq[0:1, :], lnq[0:1, :], lnsc[0:1, :], ALU.subtract)
                            v.drain()
                            s_vs.inc(TSC(v, lnq[0:1, :], lnq[0:1, :], EPS, ALU.add))
                        if tt >= 1 and lp >= 4:
                            u = tt - 1
                            v.wait_ge(s_bc.h, b_bc + u + 1)
                            sl = slice(u * F, (u + 1) * F)
                            for kc in range(KD):
                                STT(v, lnap[:], stream[:, kc, sl], 1.0,
                                    pb[4][:, 0:F], ALU.mult, ALU.subtract)
                                v.drain()
                                s_ap.inc(TT(v, uout[:, kc, sl], lnap[:], pb[5][:, 0:F]))
                                v.drain()

                b.fence(fbuf)

        def emit_ffn(uin, call_idx, parts="all"):
            P = {"experts": 1, "loss": 2, "base": 3, "all": 4}[parts]
            s_w = DStream(nc, "fw", 4)
            s_h = b.pooled("fh")
            s_he = b.pooled("fhe")
            s_r = b.pooled("fr")
            s_sqd = b.pooled("fsq")
            s_ls = b.pooled("fls")
            s_ccd = b.pooled("fcc")
            s_msk = b.pooled("fmsk")
            s_sel = b.pooled("fsel")
            s_bmm = b.pooled("fbmm")
            s_bev = b.pooled("fbev")
            s_amm = b.pooled("famm")
            s_aev = b.pooled("faev")
            s_omm = b.pooled("fomm")
            s_oev = b.pooled("foev")
            b_h, b_he, b_r, b_sqd, b_ls = s_h.n, s_he.n, s_r.n, s_sqd.n, s_ls.n
            b_msk, b_bmm, b_bev = s_msk.n, s_bmm.n, s_bev.n
            b_amm, b_aev, b_omm, b_oev = s_amm.n, s_aev.n, s_omm.n, s_oev.n

            with ExitStack() as px:
                wexp = [b.sb(f"f{call_idx}wex{i}", [128, KD * H + HC * D], BF16, px) for i in range(2)]
                hbuf = b.sb(f"f{call_idx}hbuf", [128, HC, TC], BF16, px)
                abuf = b.sb(f"f{call_idx}abuf", [RA, TC], BF16, px)
                accb = b.sb(f"f{call_idx}accb", [128, E * KD * NT], F32, px)
                lcol = b.sb(f"f{call_idx}lcol", [128, E], F32, px)
                wadsel = b.sb(f"f{call_idx}wadsel", [128, KD, RA], BF16, px)
                wausel = b.sb(f"f{call_idx}wausel", [RA, D], BF16, px)
                badsel = b.sb(f"f{call_idx}badsel", [RA, 1], F32, px)

                @block.sync
                def _(sy):
                    for e in range(E):
                        if e >= 2:
                            sy.wait_ge(s_r.h, b_r + (e - 1) * KD * NT)
                        w = wexp[e % 2]
                        s_w.inc(sy.dma_start(out=w[:, 0 : KD * H], in_=wd1c_h[e][:]), 2 * e)
                        s_w.inc(sy.dma_start(out=w[:, KD * H :], in_=wd2c_h[e][:]), 2 * e + 1)

                @block.tensor
                def _(t):
                    for e in range(E):
                        s_w.wait(t, 2 * e)
                        s_w.wait(t, 2 * e + 1)
                        w = wexp[e % 2]
                        wd1v = w[:, 0 : KD * H].rearrange("p (a c) -> p a c", c=H)
                        wd2v = w[:, KD * H :].rearrange("p (a c) -> p a c", c=D)
                        for hc in range(HC):
                            for tt in range(NT):
                                j = hc * NT + tt
                                bank = pb[j % 2]
                                jj = e * HC * NT + j
                                if jj >= 2:
                                    t.wait_ge(s_he.h, b_he + jj - 1)
                                mm = None
                                for kc in range(KD):
                                    mm = t.matmul(bank[:, 0:F],
                                                  wd1v[:, kc, hc * 128 : (hc + 1) * 128],
                                                  uin[:, kc, tt * F : (tt + 1) * F],
                                                  start=(kc == 0), stop=(kc == KD - 1))
                                s_h.inc(mm)
                        t.wait_ge(s_he.h, b_he + (e + 1) * HC * NT)
                        for pc in range(KD):
                            for tt in range(NT):
                                j = pc * NT + tt
                                bank = pb[2 + j % 2]
                                jj = e * KD * NT + j
                                if jj >= 2:
                                    t.wait_ge(s_sqd.h, b_sqd + jj - 1)
                                for hc in range(HC):
                                    t.matmul(bank[:, 0:F],
                                             wd2v[:, hc, pc * 128 : (pc + 1) * 128],
                                             hbuf[:, hc, tt * F : (tt + 1) * F],
                                             start=(hc == 0), stop=False)
                                s_r.inc(t.matmul(bank[:, 0:F], negidb[:],
                                                 uin[:, pc, tt * F : (tt + 1) * F],
                                                 start=False, stop=True))

                @block.scalar
                def _(s):
                    for e in range(E):
                        for hc in range(HC):
                            for tt in range(NT):
                                j = hc * NT + tt
                                s.wait_ge(s_h.h, b_h + e * HC * NT + j + 1)
                                bank = pb[j % 2]
                                s_he.inc(s.activation(
                                    out=hbuf[:, hc, tt * F : (tt + 1) * F],
                                    in_=bank[:, 0:F], func=ACT.Relu,
                                    bias=bd1s[:, e * HC + hc : e * HC + hc + 1]))
                        for pc in range(KD):
                            for tt in range(NT):
                                j = pc * NT + tt
                                s.wait_ge(s_r.h, b_r + e * KD * NT + j + 1)
                                bank = pb[2 + j % 2]
                                s_sqd.inc(s.activation(
                                    out=bank[:, 0:F], in_=bank[:, 0:F], func=ACT.Square,
                                    bias=bd2s[:, e * KD + pc : e * KD + pc + 1],
                                    accum_out=accb[:, e * KD * NT + j : e * KD * NT + j + 1]))

                @block.tensor
                def _(t):
                    if P < 3:
                        return
                    for pc in range(KD):
                        for tt in range(NT):
                            j = pc * NT + tt
                            bank = pb[4 + j % 2]
                            if j >= 2:
                                t.wait_ge(s_bev.h, b_bev + j - 1)
                            mm = None
                            for kc in range(KD):
                                mm = t.matmul(bank[:, 0:F],
                                              wbase_s[:, kc, pc * 128 : (pc + 1) * 128],
                                              uin[:, kc, tt * F : (tt + 1) * F],
                                              start=(kc == 0), stop=(kc == KD - 1))
                            s_bmm.inc(mm)

                @block.vector
                def _(v):
                    if P < 3:
                        return
                    for pc in range(KD):
                        for tt in range(NT):
                            j = pc * NT + tt
                            v.wait_ge(s_bmm.h, b_bmm + j + 1)
                            bank = pb[4 + j % 2]
                            sl = slice(tt * F, (tt + 1) * F)
                            s_bev.inc(STT(v, stream[:, pc, sl], bank[:, 0:F], 0.5,
                                          stream[:, pc, sl]))

                @block.vector
                def _(v):
                    if P < 2:
                        return
                    for e in range(E):
                        v.wait_ge(s_sqd.h, b_sqd + (e + 1) * KD * NT)
                        s_ls.inc(v.tensor_reduce(out=lcol[:, e : e + 1],
                                                 in_=accb[:, e * KD * NT : (e + 1) * KD * NT], axis=AX.X, op=ALU.add))

                @block.tensor
                def _(t):
                    if P < 2:
                        return
                    t.wait_ge(s_ls.h, b_ls + E)
                    s_ls.inc(t.matmul(pb[6][0:1, 0:E], ones128f[:], lcol[:, :],
                                      start=True, stop=True))

                @block.vector
                def _(v):
                    if P < 2:
                        return
                    v.wait_ge(s_ls.h, b_ls + E + 1)
                    s_ls.inc(v.tensor_copy(out=lrow[:], in_=pb[6][0:1, 0:E]))

                s_cd1 = b.pooled("fcd1")
                s_cd2 = b.pooled("fcd2")
                b_cd1, b_ccd2 = s_cd1.n, s_ccd.n

                @block.gpsimd
                def _(g):
                    if P < 2:
                        return
                    g.wait_ge(s_ls.h, b_ls + E + 2)
                    s_cd1.incd(g.dma_start(out=loss_in_h[call_idx][:], in_=lrow[:]))
                    g.wait_ge(s_cd1.h, b_cd1 + 16)
                    s_ccd.inc(g.collective_compute(
                        "AllReduce", ALU.add, replica_groups=[list(range(NC))],
                        ins=[loss_in_h[call_idx].opt()],
                        outs=[loss_out_h[call_idx].opt()]))
                    g.wait_ge(s_ccd.h, b_ccd2 + 1)
                    s_cd2.incd(g.dma_start(out=lrow[:], in_=loss_out_h[call_idx][:]))

                @block.vector
                def _(v):
                    if P < 2:
                        return
                    s_cd2.wait(v)
                    s_msk.inc(v.tensor_reduce(out=tiny[0:1, 0:1], in_=lrow[:],
                                              axis=AX.X, op=ALU.min))
                    s_msk.wait(v)
                    TSC(v, maskrow[:], lrow[:], tiny[0:1, 0:1], ALU.is_equal)
                    v.drain()
                    s_msk.inc(v.tensor_tensor_scan(out=tiny[0:1, 0:E], data0=onerow[:],
                                                   data1=maskrow[:], initial=0.0,
                                                   op0=ALU.mult, op1=ALU.add))
                    s_msk.wait(v)
                    TSC(v, tiny[0:1, 0:E], tiny[0:1, 0:E], 1.0, ALU.is_equal)
                    v.drain()
                    s_msk.inc(TT(v, maskrow[:], maskrow[:], tiny[0:1, 0:E]))

                @block.tensor
                def _(t):
                    if P < 2:
                        return
                    s_msk.wait(t)
                    s_msk.inc(t.matmul(pb[6][:, 0:E], ones1f[:], maskrow[0:1, :],
                                       start=True, stop=True))

                @block.vector
                def _(v):
                    if P < 4:
                        return
                    s_msk.wait(v)
                    for e in range(E):
                        if e > 0:
                            v.drain()
                        sc = pb[6][:, e : e + 1]
                        scr = pb[6][0:RA, e : e + 1]
                        wadv = wadsel[:, :, :]
                        wallv = wad_all[:, :, e, :]
                        if e == 0:
                            TSC(v, wadv, wallv, sc, ALU.mult)
                            TSC(v, wausel[:], wau_all[:, e, :], scr, ALU.mult)
                            TSC(v, badsel[:], bads[:, e : e + 1], scr, ALU.mult)
                            TSC(v, hb05[:], baus[:, e * KD : (e + 1) * KD], sc, ALU.mult)
                        else:
                            STT(v, wadv, wallv, sc, wadv)
                            STT(v, wausel[:], wau_all[:, e, :], scr, wausel[:])
                            STT(v, badsel[:], bads[:, e : e + 1], scr, badsel[:])
                            STT(v, hb05[:], baus[:, e * KD : (e + 1) * KD], sc, hb05[:])
                    v.drain()
                    STT(v, hb05[:], bbase[:], 1.0, hb05[:], ALU.mult, ALU.add)
                    v.drain()
                    s_sel.inc(TSC(v, hb05[:], hb05[:], 0.5, ALU.mult))

                @block.tensor
                def _(t):
                    if P < 4:
                        return
                    s_sel.wait(t)
                    for tt in range(NT):
                        bank = pb[0 + tt % 2]
                        if tt >= 2:
                            t.wait_ge(s_aev.h, b_aev + tt - 1)
                        mm = None
                        for kc in range(KD):
                            mm = t.matmul(bank[0:RA, 0:F], wadsel[:, kc, :],
                                          uin[:, kc, tt * F : (tt + 1) * F],
                                          start=(kc == 0), stop=(kc == KD - 1))
                        s_amm.inc(mm)

                @block.scalar
                def _(s):
                    if P < 4:
                        return
                    for tt in range(NT):
                        s.wait_ge(s_amm.h, b_amm + tt + 1)
                        bank = pb[0 + tt % 2]
                        s_aev.inc(s.activation(out=abuf[:, tt * F : (tt + 1) * F],
                                               in_=bank[0:RA, 0:F], func=ACT.Relu,
                                               bias=badsel[:]))

                @block.tensor
                def _(t):
                    if P < 4:
                        return
                    for pc in range(KD):
                        for tt in range(NT):
                            j = pc * NT + tt
                            bank = pb[2 + j % 2]
                            t.wait_ge(s_aev.h, b_aev + tt + 1)
                            if j >= 2:
                                t.wait_ge(s_oev.h, b_oev + j - 1)
                            s_omm.inc(t.matmul(bank[:, 0:F],
                                               wausel[:, pc * 128 : (pc + 1) * 128],
                                               abuf[:, tt * F : (tt + 1) * F],
                                               start=True, stop=True))

                @block.vector
                def _(v):
                    if P < 4:
                        return
                    s_bev.wait(v)
                    for pc in range(KD):
                        for tt in range(NT):
                            j = pc * NT + tt
                            v.wait_ge(s_omm.h, b_omm + j + 1)
                            bank = pb[2 + j % 2]
                            sl = slice(tt * F, (tt + 1) * F)
                            STT(v, stream[:, pc, sl], bank[:, 0:F], 0.5, stream[:, pc, sl])
                            v.drain()
                            s_oev.inc(TSC(v, stream[:, pc, sl], stream[:, pc, sl],
                                          hb05[:, pc : pc + 1], ALU.add))

                b.fence(fbuf)

        with ExitStack() as pf:
            u1f = b.sb("u1f", [128, KD, TC], BF16, pf)
            if upto != "p0":
                lnlvl = {"lnA": 1, "lnB": 2, "lnC": 3}.get(upto, 4)
                emit_ln(u1f, "1", lp=lnlvl)
            if upto not in ("ln1", "p0", "lnA", "lnB", "lnC"):
                emit_ffn(u1f, 0, parts=("all" if upto in ("ffn1", "all") else upto))

        if upto in ("ffn1", "ln1", "experts", "loss", "base", "p0", "lnA", "lnB", "lnC"):
            emit_store(nc, block, b, stream, pb, identf, out_h, fbuf)
            es_stream.close()
            return nc

        s_ag = b.sem("ag")
        with ExitStack() as p3:
            u2f = b.sb("u2f", [128, KD, TC], BF16, p3)
            emit_ln(u2f, "2")
            s_u2d = b.sem("u2d")
            s_spl = b.sem("spl")

            @block.sync
            def _(sy):
                s_u2d.incd(sy.dma_start(out=u2_in_h.rearrange("(a p) t -> p a t", p=128),
                                        in_=u2f[:, :, :]))
                s_spl.incd(sy.dma_start(out=sspill_h[:],
                                        in_=stream.rearrange("p a t -> p (a t)")))

            @block.gpsimd
            def _(g):
                g.wait_ge(s_u2d.h, 16)
                s_ag.inc(g.collective_compute(
                    "AllGather", ALU.bypass, replica_groups=[list(range(NC))],
                    ins=[u2_in_h.opt()], outs=[u2_out_h.opt()]))

            b.fence(fbuf, dma_sems=[s_spl, s_u2d])
        es_stream.close()

        with ExitStack() as p4:
            u2t = [b.sb(f"u2t{i}", [128, KD, TS], BF16, p4) for i in range(2)]
            xcv = [b.sb(f"xcq{i}", [128, DC2, TS], F32, p4) for i in range(2)]
            xcb = b.sb("xcbq", [128, DC2, TS], BF16, p4)
            srv = [b.sb(f"srq{i}", [128, DC2, TS], F32, p4) for i in range(2)]
            xconv = [b.sb(f"xcvq{i}", [128, DC2, 3 + TS], BF16, p4) for i in range(2)]
            dpre = [b.sb(f"dpreq{i}", [128, TS], F32, p4) for i in range(2)]
            dpreb = b.sb("dprebq", [128, TS], BF16, p4)
            bcstage = [b.sb(f"bcstq{i}", [64, TS], F32, p4) for i in range(2)]
            bcb = [b.sb(f"bcbq{i}", [64, TS], BF16, p4) for i in range(2)]
            dnegv = b.sb("dnegv", [128, DC2, TS], F32, p4)
            dxn = b.sb("dxnq", [128, DC2, TS], F32, p4)
            scan_a = b.sb("scana", [128, 2, TS], F32, p4)
            scan_u = b.sb("scanu", [128, TS], F32, p4)
            scan_h = b.sb("scanh", [128, TS], F32, p4)
            scan_g = b.sb("scang", [128, 2, TS], BF16, p4)
            ytmp = b.sb("ytmp", [128, F], F32, p4)
            ybuf = b.sb("ybufq", [128, DC2, TS], BF16, p4)

            @block.vector
            def _(v):
                for i in range(2):
                    v.memset(bcb[i][0:64, :], 0.0)

            s_u2l = DStream(nc, "u2l")
            s_fm = b.sem("fm")
            s_fe = b.sem("fe")
            s_cb = b.sem("cb")
            s_gd = DStream(nc, "gd")
            s_ar = b.sem("ar")
            s_ld = DStream(nc, "arld")
            s_dc = b.sem("dcst")
            s_zm = b.sem("zm")
            s_ss = b.sem("sself")
            s_vs2 = b.sem("vself")
            s_bb = b.sem("bb")
            s_bc2 = b.sem("bc2")
            s_ze = b.sem("ze")
            s_dx = b.sem("dx")
            s_ae = b.sem("ae")
            s_ub = b.sem("ub")
            s_sc = b.sem("sc")
            s_hs = b.sem("hs")
            s_cm = b.sem("cm")
            s_ir = b.sem("ir")
            s_yf = b.sem("yf")
            s_ys = DStream(nc, "ys")
            s_sd = b.sem("sd")

            fm_after_res = {}
            fm_c = [0]
            fe_c = [0]

            def front_sync(s):
                bq = s % 2

                @block.sync
                def _(sy):
                    s_ag.wait(sy)
                    if s >= 2:
                        sy.wait_ge(s_fm.h, 18 * (s - 2) + 8)
                    src = u2_out_h[s // 2].rearrange("(a p) t -> p a t", p=128)
                    s_u2l.inc(sy.dma_start(
                        out=u2t[bq][:, :, :],
                        in_=src[:, :, (s % 2) * TS : (s % 2) * TS + TS]), s)

            FJOBS = []
            for dc_ in range(DC2):
                for tt_ in range(2):
                    FJOBS.append(("x", dc_, tt_))
            for dc_ in range(DC2):
                for tt_ in range(2):
                    FJOBS.append(("res", dc_, tt_))
            for dc_ in range(DC2):
                for tt_ in range(2):
                    FJOBS.append(("conv", dc_, tt_))
            FJOBS += [("B", None, 0), ("B", None, 1), ("C", None, 0), ("C", None, 1),
                      ("dpre", None, 0), ("dpre", None, 1)]

            def emit_front_job(t, s, j):
                bq = s % 2
                kind, dc, tt = FJOBS[j]
                bank = pb[j % 2]
                fm0 = 18 * s
                gj = fm0 + j
                if gj >= 2:
                    t.wait_ge(s_fe.h, gj - 1)
                if kind == "x":
                    if j == 0:
                        s_u2l.wait(t, s)
                    mm = None
                    for kc in range(KD):
                        mm = t.matmul(bank[:, 0:F],
                                      win_s[:, kc, dc * 128 : (dc + 1) * 128],
                                      u2t[bq][:, kc, tt * F : (tt + 1) * F],
                                      start=(kc == 0), stop=(kc == KD - 1))
                elif kind == "res":
                    mm = None
                    for kc in range(KD):
                        mm = t.matmul(bank[:, 0:F],
                                      wres_s[:, kc, dc * 128 : (dc + 1) * 128],
                                      u2t[bq][:, kc, tt * F : (tt + 1) * F],
                                      start=(kc == 0), stop=(kc == KD - 1))
                elif kind == "conv":
                    t.wait_ge(s_fe.h, fm0 + dc * 2 + 2)
                    t.wait_ge(s_fe.h, fm0 - 18 + 18) if False else None
                    mm = None
                    for i4 in range(4):
                        mm = t.matmul(bank[:, 0:F],
                                      convdiag[:, dc * 4 + i4, :],
                                      xconv[bq][:, dc, tt * F + i4 : tt * F + i4 + F],
                                      start=(i4 == 0), stop=(i4 == 3))
                elif kind == "B":
                    t.wait_ge(s_cb.h, 2 * s + 2)
                    mm = None
                    for dcc in range(DC2):
                        mm = t.matmul(bank[0:DS, 0:F],
                                      wbneg_s[:, dcc, :],
                                      xcb[:, dcc, tt * F : (tt + 1) * F],
                                      start=(dcc == 0), stop=(dcc == DC2 - 1))
                elif kind == "C":
                    mm = None
                    for dcc in range(DC2):
                        mm = t.matmul(bank[0:DS, 0:F],
                                      wc_s[:, dcc, :],
                                      xcb[:, dcc, tt * F : (tt + 1) * F],
                                      start=(dcc == 0), stop=(dcc == DC2 - 1))
                elif kind == "dpre":
                    mm = None
                    for dcc in range(DC2):
                        mm = t.matmul(bank[:, 0:F],
                                      wdt_s[:, dcc, :],
                                      xcb[:, dcc, tt * F : (tt + 1) * F],
                                      start=(dcc == 0), stop=(dcc == DC2 - 1))
                s_fm.inc(mm)

            def emit_front_evac(sc, s, j):
                bq = s % 2
                kind, dc, tt = FJOBS[j]
                fm0 = 18 * s
                sc.wait_ge(s_fm.h, fm0 + j + 1)
                bank = pb[j % 2]
                if kind == "x":
                    ev = sc.activation(out=xconv[bq][:, dc, 3 + tt * F : 3 + (tt + 1) * F],
                                       in_=bank[:, 0:F], func=ACT.Identity)
                elif kind == "res":
                    ev = sc.activation(out=srv[bq][:, dc, tt * F : (tt + 1) * F],
                                       in_=bank[:, 0:F], func=ACT.Silu)
                elif kind == "conv":
                    ev = sc.activation(out=xcv[bq][:, dc, tt * F : (tt + 1) * F],
                                       in_=bank[:, 0:F], func=ACT.Silu,
                                       bias=convb[:, dc : dc + 1])
                elif kind == "B":
                    ev = sc.activation(out=bcstage[bq][0:DS, tt * F : (tt + 1) * F],
                                       in_=bank[0:DS, 0:F], func=ACT.Identity)
                elif kind == "C":
                    ev = sc.activation(out=bcstage[bq][32 : 32 + DS, tt * F : (tt + 1) * F],
                                       in_=bank[0:DS, 0:F], func=ACT.Identity)
                elif kind == "dpre":
                    ev = sc.activation(out=dpre[bq][:, tt * F : (tt + 1) * F],
                                       in_=bank[:, 0:F], func=ACT.Identity)
                s_fe.inc(ev)
                if kind == "conv" and tt == 1:
                    cv = sc.activation(out=xcb[:, dc, :], in_=xcv[bq][:, dc, :],
                                       func=ACT.Identity)
                    s_cb.inc(cv)

            def front_g(s):
                bq = s % 2

                @block.gpsimd
                def _(g):
                    g.wait_ge(s_fe.h, 18 * (s + 1))
                    s_gd.inc(g.dma_start(out=seg_in_h[s][0:DR, :], in_=dpre[bq][:]), 2 * s)
                    s_gd.inc(g.dma_start(out=seg_in_h[s][DR : DR + DS, :],
                                         in_=bcstage[bq][0:DS, :]), 2 * s + 1)
                    s_gd.inc(g.dma_start(out=seg_in_h[s][DR + DS : DR + 2 * DS, :],
                                         in_=bcstage[bq][32 : 32 + DS, :]), 2 * s + 1)
                    s_gd.wait(g, 2 * s)
                    s_gd.wait(g, 2 * s + 1)
                    s_ar.inc(g.collective_compute(
                        "AllReduce", ALU.add, replica_groups=[list(range(NC))],
                        ins=[seg_in_h[s].opt()], outs=[seg_out_h[s].opt()]))

                @block.sync
                def _(sy):
                    sy.wait_ge(s_ar.h, s + 1)
                    s_ld.inc(sy.dma_start(out=dpre[bq][:], in_=seg_out_h[s][0:DR, :]), 2 * s)
                    s_ld.inc(sy.dma_start(out=bcstage[bq][0:DS, :],
                                          in_=seg_out_h[s][DR : DR + DS, :]), 2 * s + 1)
                    s_ld.inc(sy.dma_start(out=bcstage[bq][32 : 32 + DS, :],
                                          in_=seg_out_h[s][DR + DS : DR + 2 * DS, :]), 2 * s + 1)

            def emit_z_mms(t, s):
                t.wait_ge(s_dc.h, 2 * (s + 1))
                if s >= 1:
                    t.wait_ge(s_yf.h, 4 * s)
                for j in range(4):
                    dc, tt = j // 2, j % 2
                    bank = pb[2 + j % 2]
                    if j >= 2:
                        t.wait_ge(s_ze.h, 4 * s + j - 1)
                    mm = t.matmul(bank[:, 0:F], dtw_s[:, dc * 128 : (dc + 1) * 128],
                                  dpreb[:, tt * F : (tt + 1) * F],
                                  start=True, stop=True)
                    s_zm.inc(mm)

            def emit_z_chain_s(sc, s):
                bq = s % 2
                s_ld.wait(sc, 2 * s)
                s_ld.wait(sc, 2 * s + 1)
                sc.activation(out=dpreb[:], in_=dpre[bq][:], func=ACT.Identity)
                sc.activation(out=bcb[bq][0:DS, :], in_=bcstage[bq][0:DS, :],
                              func=ACT.Identity)
                s_dc.inc(sc.activation(out=bcb[bq][32 : 32 + DS, :],
                                       in_=bcstage[bq][32 : 32 + DS, :],
                                       func=ACT.Identity), 2)
                for j in range(4):
                    dc, tt = j // 2, j % 2
                    sc.wait_ge(s_zm.h, 4 * s + j + 1)
                    bank = pb[2 + j % 2]
                    sg = sc.activation(out=bank[:, 0:F], in_=bank[:, 0:F],
                                       func=ACT.Sigmoid, scale=-1.0,
                                       bias=dtbn[:, dc : dc + 1])
                    s_ss.inc(sg)
                    sc.wait_ge(s_ss.h, s_ss.n)
                    ev = sc.activation(out=dnegv[:, dc, tt * F : (tt + 1) * F],
                                       in_=bank[:, 0:F], func=ACT.Ln)
                    s_ze.inc(ev)

            def scan_te(s, with_front):
                bq = s % 2
                fq = list(range(18)) if with_front else []

                def ir_group(t, l):
                    dcl, n = l // DS, l % DS
                    li = (s * DC2 + dcl) * DS + n
                    t.wait_ge(s_cm.h, li + 1)
                    if n == 0:
                        if dcl == 0:
                            t.wait_ge(s_ze.h, 4 * (s + 1))
                        else:
                            t.wait_ge(s_yf.h, 4 * s + 2)
                    mm = None
                    for c in range(2):
                        mm = t.matmul(pb[2 + c][:, 0:F], identb[:],
                                      scan_g[:, dcl, c * F : (c + 1) * F],
                                      start=(n == 0), stop=(n == DS - 1))
                    s_ir.inc(mm)

                @block.tensor
                def _(t):
                    for l in range(DC2 * DS):
                        dcl, n = l // DS, l % DS
                        li = (s * DC2 + dcl) * DS + n
                        if li >= 1:
                            t.wait_ge(s_ub.h, li)
                        for c in range(2):
                            mm = t.matmul(pb[4 + c][:, 0:F], selb[:, n, :],
                                          bcb[bq][:, c * F : (c + 1) * F],
                                          start=True, stop=True)
                        s_bb.inc(mm)
                        if l >= 1:
                            ir_group(t, l - 1)
                        if li >= 1:
                            t.wait_ge(s_cm.h, li)
                        for c in range(2):
                            mm = t.matmul(pb[6 + c][:, 0:F], selb[:, DS + n, :],
                                          bcb[bq][:, c * F : (c + 1) * F],
                                          start=True, stop=True)
                        s_bc2.inc(mm)
                        if fq:
                            emit_front_job(t, s + 1, fq.pop(0))
                    ir_group(t, DC2 * DS - 1)
                    while fq:
                        emit_front_job(t, s + 1, fq.pop(0))
                    if s + 1 < NSEG:
                        emit_z_mms(t, s + 1)

            def scan_s(s, with_front):
                bq = s % 2

                @block.scalar
                def _(sc):
                    if with_front:
                        bq1 = (s + 1) % 2
                        if s + 1 >= 2:
                            sc.wait_ge(s_sd.h, s)
                        for dc in range(DC2):
                            sc.activation(out=xconv[bq1][:, dc, 0:3],
                                          in_=xconv[1 - bq1][:, dc, TS : TS + 3],
                                          func=ACT.Identity)
                    fq = list(range(18)) if with_front else []
                    for dcl in range(DC2):
                        for n in range(DS):
                            li = (s * DC2 + dcl) * DS + n
                            if li >= 2:
                                sc.wait_ge(s_sc.h, li - 1)
                            ev = sc.activation(out=scan_a[:, dcl, :],
                                               in_=dnegv[:, dcl, :], func=ACT.Exp,
                                               scale=ppos[:, dcl * DS + n : dcl * DS + n + 1])
                            s_ae.inc(ev)
                            if fq:
                                emit_front_evac(sc, s + 1, fq.pop(0))
                    while fq:
                        emit_front_evac(sc, s + 1, fq.pop(0))
                    if s + 1 < NSEG:
                        emit_z_chain_s(sc, s + 1)

            def scan_v(s):
                bq = s % 2

                @block.vector
                def _(v):
                    v.wait_ge(s_ze.h, 4 * s + 4)
                    s_dx.inc(TT(v, dxn.rearrange("p a t -> p (a t)"),
                                dnegv.rearrange("p a t -> p (a t)"),
                                xcv[bq].rearrange("p a t -> p (a t)")))
                    for dcl in range(DC2):
                        for n in range(DS):
                            li = (s * DC2 + dcl) * DS + n
                            v.wait_ge(s_bb.h, li + 1)
                            if li >= 1:
                                v.wait_ge(s_sc.h, li)
                            tv = TT(v, scan_u[:], dxn[:, dcl, :], pB[:, :])
                            s_ub.inc(tv)
                            v.wait_ge(s_ub.h, li + 1)
                            v.wait_ge(s_ae.h, li + 1)
                            if s >= 1:
                                v.wait_ge(s_hs.h, ((s - 1) * DC2 + dcl) * DS + n + 1)
                            tv = v.tensor_tensor_scan(
                                out=scan_h[:], data0=scan_a[:, dcl, :],
                                data1=scan_u[:],
                                initial=(0.0 if s == 0 else hstate[:, dcl, n : n + 1]),
                                op0=ALU.mult, op1=ALU.add)
                            s_sc.inc(tv)
                            v.wait_ge(s_sc.h, li + 1)
                            s_hs.inc(v.tensor_copy(out=hstate[:, dcl, n : n + 1],
                                                   in_=scan_h[:, TS - 1 : TS]))
                            v.wait_ge(s_bc2.h, li + 1)
                            tv = TT(v, scan_g[:, dcl, :], scan_h[:], pC[:, :])
                            s_cm.inc(tv)
                        v.wait_ge(s_ir.h, (s * DC2 + dcl) * DS + DS)
                        if 2 * s + dcl - 2 >= 0:
                            s_ys.wait(v, 2 * s + dcl - 2)
                        for c in range(2):
                            tv = STT(v, ytmp[:], xcv[bq][:, dcl, c * F : (c + 1) * F],
                                     dvec[:, dcl : dcl + 1], pb[2 + c][:, 0:F],
                                     ALU.mult, ALU.add)
                            s_vs2.inc(tv)
                            v.wait_ge(s_vs2.h, s_vs2.n)
                            tv = TT(v, ybuf[:, dcl, c * F : (c + 1) * F],
                                    ytmp[:], srv[bq][:, dcl, c * F : (c + 1) * F])
                            s_yf.inc(tv)
                            v.wait_ge(s_yf.h, s_yf.n)
                    s_sd.inc(v.memset(ytmp[0:1, 0:1], 0.0))

                @block.sync
                def _(sy):
                    for dcl in range(DC2):
                        sy.wait_ge(s_yf.h, 4 * s + 2 * (dcl + 1))
                        dst = y_in_h[s // 2, dcl * 128 + 0 : dcl * 128 + 128,
                                     (s % 2) * TS : (s % 2) * TS + TS]
                        s_ys.inc(sy.dma_start(out=dst, in_=ybuf[:, dcl, :]), 2 * s + dcl)

            front_sync(0)

            @block.tensor
            def _(t):
                for j in range(18):
                    emit_front_job(t, 0, j)

            @block.scalar
            def _(sc):
                bq1 = 0
                for dc in range(DC2):
                    sc.activation(out=xconv[0][:, dc, 0:3],
                                  in_=xconv[0][:, dc, 0:3], func=ACT.Identity,
                                  scale=0.0)
                for j in range(18):
                    emit_front_evac(sc, 0, j)
            front_g(0)

            @block.scalar
            def _(sc):
                pass

            @block.tensor
            def _(t):
                pass

            @block.scalar
            def _(sc):
                s_ld.wait(sc, 0)
                s_ld.wait(sc, 1)
                sc.activation(out=dpreb[:], in_=dpre[0][:], func=ACT.Identity)
                sc.activation(out=bcb[0][0:DS, :], in_=bcstage[0][0:DS, :],
                              func=ACT.Identity)
                s_dc.inc(sc.activation(out=bcb[0][32 : 32 + DS, :],
                                       in_=bcstage[0][32 : 32 + DS, :],
                                       func=ACT.Identity), 2)

            @block.tensor
            def _(t):
                emit_z_mms(t, 0)

            @block.scalar
            def _(sc):
                for j in range(4):
                    dc, tt = j // 2, j % 2
                    sc.wait_ge(s_zm.h, j + 1)
                    bank = pb[2 + j % 2]
                    sg = sc.activation(out=bank[:, 0:F], in_=bank[:, 0:F],
                                       func=ACT.Sigmoid, scale=-1.0,
                                       bias=dtbn[:, dc : dc + 1])
                    s_ss.inc(sg)
                    sc.wait_ge(s_ss.h, s_ss.n)
                    ev = sc.activation(out=dnegv[:, dc, tt * F : (tt + 1) * F],
                                       in_=bank[:, 0:F], func=ACT.Ln)
                    s_ze.inc(ev)

            for s_i in range(NSEG):
                if s_i + 1 < NSEG:
                    front_sync(s_i + 1)
                scan_te(s_i, with_front=(s_i + 1 < NSEG))
                scan_s(s_i, with_front=(s_i + 1 < NSEG))
                scan_v(s_i)
                if s_i + 1 < NSEG:
                    front_g(s_i + 1)

            @block.sync
            def _(sy):
                for _ss in s_ys.sems:
                    sy.wait_ge(_ss.h, _ss.n)
            b.fence(fbuf, dma_sems=s_ys.sems + s_ld.sems + s_u2l.sems)

        stream = b.sb("stream2", [128, KD, TC], F32, es_stream2 := ExitStack())
        s_a2a = b.sem("a2a")
        with ExitStack() as p5:
            wout_s = b.sb("wout_s", [128, DM // 128, D], BF16, p5)
            ytile = [b.sb(f"ytile{i}", [128, DM // 128, F], BF16, p5) for i in range(2)]
            s_wol = b.sem("wol")
            s_srl = DStream(nc, "srl")
            s_ytl = DStream(nc, "ytl")
            s_om = b.sem("om")
            s_oe = b.sem("oe")

            @block.gpsimd
            def _(g):
                from concourse.aot_env import allow_ici_all_to_all
                with allow_ici_all_to_all():
                    s_a2a.inc(g.collective_compute(
                        "AllToAll", ALU.bypass, replica_groups=[list(range(NC))],
                        ins=[y_in_h.opt()], outs=[y_out_h.opt()]))

            @block.sync
            def _(sy):
                s_wol.incd(sy.dma_start(
                    out=wout_s.rearrange("p a c -> p (a c)"), in_=woutc_h[:]))
                s_wol.incd(sy.dma_start(out=stream.rearrange("p a t -> p (a t)"),
                                        in_=sspill_h[:]))
                s_a2a.wait(sy)
                yv = y_out_h.rearrange("i (c p) t -> p i c t", p=128)
                for tt in range(NT):
                    if tt >= 2:
                        sy.wait_ge(s_om.h, (tt - 1) * KD)
                    s_ytl.inc(sy.dma_start(
                        out=ytile[tt % 2].rearrange("p a f -> p a f"),
                        in_=yv[:, :, :, tt * F : (tt + 1) * F].rearrange("p i c t -> p (i c) t")), tt)

            @block.tensor
            def _(t):
                t.wait_ge(s_wol.h, 16)
                for tt in range(NT):
                    s_ytl.wait(t, tt)
                    for pc in range(KD):
                        j = tt * KD + pc
                        bank = pb[j % 2]
                        if j >= 2:
                            t.wait_ge(s_oe.h, j - 1)
                        mm = None
                        for kc in range(DM // 128):
                            mm = t.matmul(bank[:, 0:F],
                                          wout_s[:, kc, pc * 128 : (pc + 1) * 128],
                                          ytile[tt % 2][:, kc, :],
                                          start=(kc == 0), stop=(kc == DM // 128 - 1))
                        s_om.inc(mm)

            @block.vector
            def _(v):
                v.wait_ge(s_wol.h, 32)
                for tt in range(NT):
                    for pc in range(KD):
                        j = tt * KD + pc
                        v.wait_ge(s_om.h, j + 1)
                        bank = pb[j % 2]
                        s_oe.inc(STT(v, stream[:, pc, tt * F : (tt + 1) * F],
                                     bank[:, 0:F], 1.0,
                                     stream[:, pc, tt * F : (tt + 1) * F]))

            b.fence(fbuf, dma_sems=s_ytl.sems + s_srl.sems)

        with ExitStack() as pf2:
            u3f = b.sb("u3f", [128, KD, TC], BF16, pf2)
            emit_ln(u3f, "3")
            emit_ffn(u3f, 1)

        emit_store(nc, block, b, stream, pb, identf, out_h, fbuf)
        es_stream2.close()
        return nc


def emit_store(nc, block, b, stream, pb, identf, out_h, tiny):
    s_tr = b.sem("otr")
    s_ev = b.sem("oev")
    s_st = DStream(nc, "ost")
    with ExitStack() as px:
        ostage = b.sb("ostage", [128, 2, 1024], F32, px)

        @block.tensor
        def _(t):
            for tt in range(TC // 128):
                for kc in range(KD):
                    j = tt * KD + kc
                    if j >= 2:
                        t.wait_ge(s_ev.h, j - 1)
                    s_tr.inc(t.transpose(pb[j % 2][:, 0:128],
                                         stream[:, kc, tt * 128 : (tt + 1) * 128],
                                         identf[:]))

        @block.vector
        def _(v):
            for tt in range(TC // 128):
                if tt >= 2:
                    s_st.wait(v, tt - 2)
                for kc in range(KD):
                    j = tt * KD + kc
                    v.wait_ge(s_tr.h, j + 1)
                    s_ev.inc(v.tensor_copy(out=ostage[:, tt % 2, kc * 128 : (kc + 1) * 128],
                                           in_=pb[j % 2][:, 0:128]))

        @block.sync
        def _(sy):
            for tt in range(TC // 128):
                sy.wait_ge(s_ev.h, (tt + 1) * KD)
                s_st.inc(sy.dma_start(out=out_h[tt * 128 : (tt + 1) * 128, :],
                                      in_=ostage[:, tt % 2, :]), tt)
            for _ss in s_st.sems:
                sy.wait_ge(_ss.h, _ss.n)


def _prep_inputs(inputs):
    g = lambda k: np.asarray(inputs[k])
    x = g("L0_attn_x_in")[0]
    per_core = []
    ident = np.eye(128, dtype=np.float32)
    sel32 = np.zeros((64, 32, 128), dtype=np.float32)
    for j in range(16):
        sel32[j, j, :] = 1.0
        sel32[32 + j, 16 + j, :] = 1.0
    for c in range(NC):
        ch = slice(c * DCH, (c + 1) * DCH)
        m = {}
        m["x0"] = np.ascontiguousarray(x[c * TC : (c + 1) * TC])
        m["ident"] = ident
        m["sel32"] = sel32
        m["w_base"] = np.ascontiguousarray(g("L0_ffn_W_base"))
        m["b_base"] = np.ascontiguousarray(g("L0_ffn_b_base").reshape(KD, 128).T)
        m["wd1"] = np.ascontiguousarray(g("L0_ffn_Wd1"))
        m["bd1"] = np.ascontiguousarray(g("L0_ffn_bd1").reshape(E, HC, 128).transpose(0, 2, 1))
        m["wd2"] = np.ascontiguousarray(g("L0_ffn_Wd2"))
        m["bd2"] = np.ascontiguousarray(g("L0_ffn_bd2").reshape(E, KD, 128).transpose(0, 2, 1))
        m["wad"] = np.ascontiguousarray(g("L0_ffn_Wa_down"))
        m["bad"] = np.ascontiguousarray(g("L0_ffn_ba_down").T)
        m["wau"] = np.ascontiguousarray(g("L0_ffn_Wa_up"))
        m["bau"] = np.ascontiguousarray(g("L0_ffn_ba_up").reshape(E, KD, 128).transpose(0, 2, 1))
        m["w_in"] = np.ascontiguousarray(g("L0_attn_W_in")[:, ch])
        m["w_res"] = np.ascontiguousarray(g("L0_attn_W_res")[:, ch])
        m["w_out"] = np.ascontiguousarray(g("L0_attn_W_out"))
        m["conv_w"] = np.ascontiguousarray(
            g("L0_attn_conv_w")[ch].reshape(DC2, 128, 4).transpose(1, 0, 2))
        m["conv_b"] = np.ascontiguousarray(
            g("L0_attn_conv_b")[ch].reshape(DC2, 128).T)
        m["a_log"] = np.ascontiguousarray(
            g("L0_attn_A_log")[ch].reshape(DC2, 128, DS).transpose(1, 0, 2))
        m["d_vec"] = np.ascontiguousarray(
            g("L0_attn_D")[ch].reshape(DC2, 128).T)
        m["w_b"] = np.ascontiguousarray(g("L0_attn_W_B")[ch])
        m["w_c"] = np.ascontiguousarray(g("L0_attn_W_C")[ch])
        m["w_dt"] = np.ascontiguousarray(g("L0_attn_W_dt")[ch])
        m["dt_w"] = np.ascontiguousarray(g("L0_attn_dt_w")[:, ch])
        m["dt_b"] = np.ascontiguousarray(
            g("L0_attn_dt_b")[ch].reshape(DC2, 128).T)
        per_core.append(m)
    return per_core


def kernel(**inputs):
    key = "all"
    if key not in _CACHE:
        _CACHE[key] = build_program(upto=key)
    nc = _CACHE[key]
    in_maps = _prep_inputs(inputs)
    res = run_bass_kernel_spmd(nc, in_maps, core_ids=list(range(NC)))
    out = np.concatenate([res.results[c]["out"] for c in range(NC)], axis=0)
    return out[None, :, :]


# version 04
# speedup vs baseline: 1.2970x; 1.0188x over previous
from contextlib import ExitStack

import numpy as np
import concourse.bass as bass
import concourse.mybir as mybir
from concourse.bass_utils import run_bass_kernel_spmd

F32 = mybir.dt.float32
BF16 = mybir.dt.bfloat16
ALU = mybir.AluOpType
ACT = mybir.ActivationFunctionType
AX = mybir.AxisListType

NC = 8
L = 16384
D = 1024
TC = L // NC
DM = 2048
DCH = DM // NC
DC2 = DCH // 128
DS = 16
DR = 128
E = 8
H = 256
HC = H // 128
RA = 64
TS = 1024
NSEG = L // TS
KD = D // 128
F = 512
NT = TC // F
EPS = 1e-5

_CACHE = {}


class Sem:
    __slots__ = ("h", "n")

    def __init__(self, nc, name):
        self.h = nc.alloc_semaphore(name)
        self.n = 0

    def inc(self, instr, k=1):
        instr.then_inc(self.h, k)
        self.n += k
        return instr

    def incd(self, instr):
        return self.inc(instr, 16)

    def wait(self, eng, val=None):
        eng.wait_ge(self.h, self.n if val is None else val)


class DStream:
    _uid = [0]

    def __init__(self, nc, name, n=2):
        DStream._uid[0] += 1
        u = DStream._uid[0]
        self.sems = [Sem(nc, f"d{u}_{name}{i}") for i in range(n)]
        self.n = n

    def inc(self, instr, i):
        self.sems[i % self.n].inc(instr, 16)
        return instr

    def wait(self, eng, i):
        eng.wait_ge(self.sems[i % self.n].h, 16 * (i // self.n + 1))


class B:

    def __init__(self, nc, block, stack):
        self.nc = nc
        self.block = block
        self.stack = stack
        self._semi = 0
        self.pV = self.sem("pV")
        self.pS = self.sem("pS")
        self.pT = self.sem("pT")
        self.pG = self.sem("pG")

    def sem(self, tag=""):
        self._semi += 1
        return Sem(self.nc, f"s{self._semi}_{tag}")

    def pooled(self, tag):
        if not hasattr(self, "_pool"):
            self._pool = {}
        if tag not in self._pool:
            self._pool[tag] = self.sem(tag)
        return self._pool[tag]

    def sb(self, name, shape, dt, stack=None):
        return (stack or self.stack).enter_context(self.nc.sbuf_tensor(name, shape, dt))

    def fence(self, scratch, dma_sems=()):
        blk = self.block
        pV, pS, pT, pG = self.pV, self.pS, self.pT, self.pG

        @blk.vector
        def _(v):
            v.wait_ge(pV.h, pV.n)
            pV.inc(v.memset(scratch[0:1, 0:1], 0.0))

        @blk.scalar
        def _(s):
            s.wait_ge(pS.h, pS.n)
            pS.inc(s.memzero(scratch[0:1, 1:2]))

        @blk.gpsimd
        def _(g):
            g.wait_ge(pG.h, pG.n)
            pG.inc(g.memset(scratch[0:1, 2:3], 0.0))

        @blk.tensor
        def _(t):
            pT.inc(t.nop())

        marks = [self.pV, self.pS, self.pT, self.pG] + list(dma_sems)
        for mk in marks:
            @blk.vector
            def _(v, mk=mk):
                mk.wait(v)

            @blk.scalar
            def _(s, mk=mk):
                mk.wait(s)

            @blk.tensor
            def _(t, mk=mk):
                mk.wait(t)

            @blk.gpsimd
            def _(g, mk=mk):
                mk.wait(g)

            @blk.sync
            def _(sy, mk=mk):
                mk.wait(sy)


def TT(eng, out, a, bb, op=ALU.mult):
    return eng.tensor_tensor(out=out, in0=a, in1=bb, op=op)


def TSC(eng, out, a, s1, op0, s2=None, op1=ALU.add):
    if s2 is None:
        return eng.tensor_scalar(out=out, in0=a, scalar1=s1, scalar2=None,
                                 op0=op0, op1=ALU.bypass)
    return eng.tensor_scalar(out=out, in0=a, scalar1=s1, scalar2=s2, op0=op0, op1=op1)


def STT(eng, out, a, s, bb, op0=ALU.mult, op1=ALU.add):
    return eng.scalar_tensor_tensor(out=out, in0=a, scalar=s, in1=bb, op0=op0, op1=op1)


def build_program(upto="all", parts="all"):
    nc = bass.Bass("TRN2", target_bir_lowering=False, detect_race_conditions=False)

    def inp(name, shape):
        return nc.dram_tensor(name, shape, F32, kind="ExternalInput").ap()

    x0_h = inp("x0", [TC, D])
    ident_h = inp("ident", [128, 128])
    sel_h = inp("sel32", [64, 32, 128])
    wbase_h = inp("w_base", [D, D])
    bbase_h = inp("b_base", [128, KD])
    wd1_h = inp("wd1", [E, D, H])
    bd1_h = inp("bd1", [E, 128, HC])
    wd2_h = inp("wd2", [E, H, D])
    bd2_h = inp("bd2", [E, 128, KD])
    wad_h = inp("wad", [E, D, RA])
    bad_h = inp("bad", [RA, E])
    wau_h = inp("wau", [E, RA, D])
    bau_h = inp("bau", [E, 128, KD])
    win_h = inp("w_in", [D, DCH])
    wres_h = inp("w_res", [D, DCH])
    wout_h = inp("w_out", [DM, D])
    convw_h = inp("conv_w", [128, DC2, 4])
    convb_h = inp("conv_b", [128, DC2])
    alog_h = inp("a_log", [128, DC2, DS])
    dvec_h = inp("d_vec", [128, DC2])
    wb_h = inp("w_b", [DCH, DS])
    wc_h = inp("w_c", [DCH, DS])
    wdt_h = inp("w_dt", [DCH, DR])
    dtw_h = inp("dt_w", [DR, DCH])
    dtb_h = inp("dt_b", [128, DC2])

    out_h = nc.dram_tensor("out", [TC, D], F32, kind="ExternalOutput").ap()

    def dram(name, shape, dt=F32):
        return nc.dram_tensor(name, shape, dt).ap()

    wd1c_h = [dram(f"wd1c{e}", [128, KD * H], BF16) for e in range(E)]
    wd2c_h = [dram(f"wd2c{e}", [128, HC * D], BF16) for e in range(E)]
    woutc_h = dram("woutc", [128, (DM // 128) * D], BF16)
    u2_in_h = dram("u2_in", [D, TC], BF16)
    u2_out_h = dram("u2_out", [NC, D, TC], BF16)
    seg_in_h = [dram(f"seg_in{s}", [DR + 2 * DS, TS]) for s in range(NSEG)]
    seg_out_h = [dram(f"seg_out{s}", [DR + 2 * DS, TS]) for s in range(NSEG)]
    loss_in_h = [dram(f"loss_in{c}", [1, E]) for c in range(2)]
    loss_out_h = [dram(f"loss_out{c}", [1, E]) for c in range(2)]
    y_in_h = dram("y_in", [NC, DCH, TC], BF16)
    y_out_h = dram("y_out", [NC, DCH, TC], BF16)
    sspill_h = dram("sspill", [128, KD * TC])

    with ExitStack() as stack, nc.Block() as block:
        b = B(nc, block, stack)
        sb = b.sb

        identf = sb("identf", [128, 128], F32)
        identb = sb("identb", [128, 128], BF16)
        negidb = sb("negidb", [128, 128], BF16)
        ones1b = sb("ones1b", [1, 128], BF16)
        ones128f = sb("ones128f", [128, 1], F32)
        bbase = sb("bbase", [128, KD], F32)
        bd1s = sb("bd1s", [128, E * HC], F32)
        bd2s = sb("bd2s", [128, E * KD], F32)
        bads = sb("bads", [RA, E], F32)
        baus = sb("baus", [128, E * KD], F32)
        convb = sb("convbs", [128, DC2], F32)
        dvec = sb("dvecs", [128, DC2], F32)
        dtbn = sb("dtbn", [128, DC2], F32)
        ppos = sb("ppos", [128, DC2 * DS], F32)
        convdiag = sb("convdiag", [128, DC2 * 4, 128], BF16)
        hb05 = sb("hb05", [128, KD], F32)
        lrow = sb("lrow", [1, E], F32)
        onerow = sb("onerow", [1, E], F32)
        maskrow = sb("maskrow", [1, E], F32)
        tiny = sb("tiny", [1, 8], F32)
        fbuf = sb("fbuf", [1, 4], F32)
        ones1f = sb("ones1f", [1, 128], F32)
        win_s = sb("win_s", [128, KD, DCH], BF16)
        wres_s = sb("wres_s", [128, KD, DCH], BF16)
        wdt_s = sb("wdt_s", [128, DC2, DR], BF16)
        dtw_s = sb("dtw_s", [128, DCH], BF16)
        wbneg_s = sb("wbneg_s", [128, DC2, DS], BF16)
        wc_s = sb("wc_s", [128, DC2, DS], BF16)
        hstate = sb("hstate", [128, DC2, DS], F32)
        selb = sb("selb", [64, 32, 128], BF16)
        wbase_s = sb("wbase_s", [128, KD, D], BF16)
        wad_all = sb("wad_all", [128, KD, E, RA], BF16)
        wau_all = sb("wau_all", [RA, E, D], BF16)

        pfr = [stack.enter_context(nc.psum_tensor(f"pfr{i}", [128, F], F32))
               for i in range(2)]
        pY = stack.enter_context(nc.psum_tensor("pY", [128, 2 * F], F32))
        pB = stack.enter_context(nc.psum_tensor("pB", [128, 2 * F], F32))
        pC = stack.enter_context(nc.psum_tensor("pC", [128, 2 * F], F32))
        pb = [pfr[0][:, :], pfr[1][:, :], pY[:, 0:F], pY[:, F : 2 * F],
              pB[:, 0:F], pB[:, F : 2 * F], pC[:, 0:F], pC[:, F : 2 * F]]

        es_stream = ExitStack()
        stream = b.sb("stream", [128, KD, TC], F32, es_stream)

        with ExitStack() as p0:
            stage = b.sb("stage", [128, 8192], F32, p0)
            stgb = b.sb("stgb", [128, 8192], BF16, p0)

            s_c = b.sem("const")

            @block.sync
            def _(sy):
                s_c.incd(sy.dma_start(out=identf[:], in_=ident_h[:]))
                s_c.incd(sy.dma_start(out=bbase[:], in_=bbase_h[:]))
                s_c.incd(sy.dma_start(out=bd1s[:].rearrange("p (e h) -> p e h", e=E), in_=bd1_h.rearrange("e p h -> p e h")))
                s_c.incd(sy.dma_start(out=bd2s[:].rearrange("p (e k) -> p e k", e=E), in_=bd2_h.rearrange("e p k -> p e k")))
                s_c.incd(sy.dma_start(out=bads[:], in_=bad_h[:]))
                s_c.incd(sy.dma_start(out=baus[:].rearrange("p (e k) -> p e k", e=E), in_=bau_h.rearrange("e p k -> p e k")))
                s_c.incd(sy.dma_start(out=convb[:], in_=convb_h[:]))
                s_c.incd(sy.dma_start(out=dvec[:], in_=dvec_h[:]))
                s_c.incd(sy.dma_start(out=dtbn[:], in_=dtb_h[:]))
                s_c.incd(sy.dma_start(out=ppos[:], in_=alog_h.rearrange("p a b -> p (a b)")))
                s_c.incd(sy.dma_start(out=hb05[:, 0:8], in_=convw_h.rearrange("p a b -> p (a b)")))

            s_c0 = b.sem("c0")

            @block.vector
            def _(v):
                s_c.wait(v)
                v.tensor_copy(out=identb[:], in_=identf[:])
                TSC(v, negidb[:], identf[:], -1.0, ALU.mult)
                v.memset(ones1b[:], 1.0)
                v.memset(ones128f[:], 1.0)
                v.memset(ones1f[:], 1.0)
                v.memset(hstate.rearrange("p a b -> p (a b)"), 0.0)
                v.memset(onerow[:], 1.0)
                for dc in range(DC2):
                    for i in range(4):
                        TSC(v, convdiag[:, dc * 4 + i, :], identf[:],
                            hb05[:, dc * 4 + i : dc * 4 + i + 1], ALU.mult)
                s_c0.inc(v.tensor_copy(out=tiny[0:1, 0:2], in_=hb05[0:1, 0:2]))

            @block.scalar
            def _(s):
                s_c.wait(s)
                s_c0.inc(s.activation(out=ppos[:], in_=ppos[:], func=ACT.Exp))
                s.wait_ge(s_c0.h, 2)
                s_c0.inc(s.activation(out=ppos[:], in_=ppos[:],
                                      func=ACT.Identity, scale=-1.0))

            jobs = []

            def wj(src, dst, cache, parts, cols, neg=False):
                assert cols <= 4096
                jobs.append((src, dst, cache, parts, cols, neg))

            for hh in range(2):
                wj(wbase_h[hh * 512 : (hh + 1) * 512].rearrange("(a p) c -> p a c", p=128),
                   wbase_s[:, hh * 4 : (hh + 1) * 4, :].rearrange("p a c -> p (a c)"),
                   None, 128, KD * D // 2)
            wj(win_h.rearrange("(a p) c -> p a c", p=128),
               win_s.rearrange("p a c -> p (a c)"), None, 128, KD * DCH)
            wj(wres_h.rearrange("(a p) c -> p a c", p=128),
               wres_s.rearrange("p a c -> p (a c)"), None, 128, KD * DCH)
            wj(wdt_h.rearrange("(a p) c -> p a c", p=128),
               wdt_s.rearrange("p a c -> p (a c)"), None, 128, DC2 * DR)
            wj(dtw_h[:], dtw_s[:], None, 128, DCH)
            wj(wb_h.rearrange("(a p) c -> p a c", p=128),
               wbneg_s.rearrange("p a c -> p (a c)"), None, 128, DC2 * DS)
            wj(wc_h.rearrange("(a p) c -> p a c", p=128),
               wc_s.rearrange("p a c -> p (a c)"), None, 128, DC2 * DS)
            wj(sel_h.rearrange("k n p -> k (n p)"),
               selb.rearrange("k n p -> k (n p)"), None, 64, 32 * 128)
            for e in range(E):
                wj(wad_h[e].rearrange("(a p) r -> p a r", p=128),
                   wad_all[:, :, e, :], None, 128, KD * RA)
                wj(wau_h[e][:], wau_all[:, e, :], None, RA, D)
                wj(wd1_h[e].rearrange("(a p) c -> p a c", p=128), None,
                   wd1c_h[e][:], 128, KD * H)
                wj(wd2_h[e].rearrange("(a p) c -> p a c", p=128), None,
                   wd2c_h[e][:], 128, HC * D)
            for qq in range(4):
                wj(wout_h[qq * 512 : (qq + 1) * 512].rearrange("(a p) c -> p a c", p=128),
                   None, woutc_h[:, qq * 4 * D : (qq + 1) * 4 * D], 128, 4 * D)

            s_wl = DStream(nc, "wload")
            s_wc = b.sem("wcast")
            s_ws = DStream(nc, "wstore")
            ncache = sum(1 for j in jobs if j[2] is not None)

            @block.sync
            def _(sy):
                for i, (src, dst, cache, parts, cols, neg) in enumerate(jobs):
                    if i >= 2:
                        sy.wait_ge(s_wc.h, i - 1)
                    half = (i % 2) * 4096
                    dst_st = stage[0:parts, half : half + cols]
                    if len(src.shape) == 3:
                        dst_st = dst_st.rearrange("p (a c) -> p a c", c=src.shape[2])
                    s_wl.inc(sy.dma_start(out=dst_st, in_=src), i)

            cache_seen = [0]

            @block.vector
            def _(v):
                nstore = 0
                for i, (src, dst, cache, parts, cols, neg) in enumerate(jobs):
                    half = (i % 2) * 4096
                    s_wl.wait(v, i)
                    if cache is not None:
                        if nstore >= 2:
                            s_ws.wait(v, nstore - 2)
                        tgt = stgb[0:parts, (nstore % 2) * 4096 : (nstore % 2) * 4096 + cols]
                        nstore += 1
                    else:
                        tgt = dst
                    if neg:
                        s_wc.inc(TSC(v, tgt, stage[0:parts, half : half + cols], -1.0, ALU.mult))
                    else:
                        s_wc.inc(v.tensor_copy(out=tgt, in_=stage[0:parts, half : half + cols]))

            @block.scalar
            def _(s):
                nstore = 0
                for i, (src, dst, cache, parts, cols, neg) in enumerate(jobs):
                    if cache is None:
                        continue
                    s.wait_ge(s_wc.h, i + 1)
                    src_b = stgb[0:parts, (nstore % 2) * 4096 : (nstore % 2) * 4096 + cols]
                    s_ws.inc(s.dma_start(out=cache, in_=src_b), nstore)
                    nstore += 1

            s_si = DStream(nc, "sin")
            s_tr = b.sem("trp")
            s_se = b.sem("sev")

            @block.sync
            def _(sy):
                sy.wait_ge(s_wc.h, len(jobs))
                for tt in range(TC // 128):
                    half = (tt % 2) * 1024
                    if tt >= 2:
                        sy.wait_ge(s_tr.h, (tt - 1) * KD)
                    s_si.inc(sy.dma_start(out=stage[:, half : half + 1024],
                                          in_=x0_h[tt * 128 : (tt + 1) * 128, :]), tt)

            @block.tensor
            def _(t):
                for tt in range(TC // 128):
                    s_si.wait(t, tt)
                    half = (tt % 2) * 1024
                    for kc in range(KD):
                        j = tt * KD + kc
                        if j >= 2:
                            t.wait_ge(s_se.h, j - 1)
                        s_tr.inc(t.transpose(pb[j % 2][:, 0:128],
                                             stage[:, half + kc * 128 : half + (kc + 1) * 128],
                                             identf[:]))

            @block.vector
            def _(v):
                for tt in range(TC // 128):
                    for kc in range(KD):
                        j = tt * KD + kc
                        v.wait_ge(s_tr.h, j + 1)
                        s_se.inc(v.tensor_copy(
                            out=stream[:, kc, tt * 128 : (tt + 1) * 128],
                            in_=pb[j % 2][:, 0:128]))

            b.fence(fbuf, dma_sems=s_ws.sems + s_si.sems)

        def emit_ln(uout, tag, lp=4):
            s_sq = b.pooled("lnsq")
            s_mm = b.pooled("lnmm")
            s_vs = b.pooled("lnvs")
            s_ro = b.pooled("lnro")
            s_bc = b.pooled("lnbc")
            s_ap = b.pooled("lnap")
            b_sq, b_mm, b_vs = s_sq.n, s_mm.n, s_vs.n
            b_ro, b_bc, b_ap = s_ro.n, s_bc.n, s_ap.n
            with ExitStack() as px:
                lnsq = b.sb(f"ln{tag}s", [128, KD, F], F32, px)
                lnm = b.sb(f"ln{tag}m", [1, TC], F32, px)
                lnr = b.sb(f"ln{tag}r", [1, TC], F32, px)
                lnap = b.sb(f"ln{tag}a", [128, F], F32, px)
                lnq = b.sb(f"ln{tag}q", [1, F], F32, px)
                lnsc = b.sb(f"ln{tag}c", [1, F], F32, px)

                @block.scalar
                def _(sc):
                    for tt in range(NT + 1):
                        if tt < NT:
                            if tt >= 1:
                                sc.wait_ge(s_mm.h, b_mm + 2 * tt)
                            ss = None
                            for kc in range(KD):
                                ss = sc.activation(out=lnsq[:, kc, :],
                                                   in_=stream[:, kc, tt * F : (tt + 1) * F],
                                                   func=ACT.Square)
                            s_sq.inc(ss)
                        if tt >= 1 and lp >= 2:
                            u = tt - 1
                            sc.wait_ge(s_vs.h, b_vs + u + 1)
                            sl = slice(u * F, (u + 1) * F)
                            sc.activation(out=lnr[0:1, sl], in_=lnq[0:1, :], func=ACT.Ln)
                            sc.drain()
                            s_ro.inc(sc.activation(out=lnr[0:1, sl], in_=lnr[0:1, sl],
                                                   func=ACT.Exp, scale=-0.5))

                @block.tensor
                def _(t):
                    for tt in range(NT + 1):
                        if tt < NT:
                            if tt >= 1:
                                t.wait_ge(s_vs.h, b_vs + tt)
                            for kc in range(KD):
                                t.matmul(pb[6][0:1, 0:F], ones128f[:],
                                         stream[:, kc, tt * F : (tt + 1) * F],
                                         start=(kc == 0), stop=(kc == KD - 1))
                            t.wait_ge(s_sq.h, b_sq + tt + 1)
                            mm = None
                            for kc in range(KD):
                                mm = t.matmul(pb[7][0:1, 0:F], ones128f[:], lnsq[:, kc, :],
                                              start=(kc == 0), stop=(kc == KD - 1))
                            s_mm.inc(mm, 2)
                        if tt >= 1 and lp >= 3:
                            u = tt - 1
                            t.wait_ge(s_ro.h, b_ro + u + 1)
                            if u >= 1:
                                t.wait_ge(s_ap.h, b_ap + u * KD)
                            t.matmul(pb[4][:, 0:F], ones1f[:],
                                     lnm[0:1, u * F : (u + 1) * F], start=True, stop=True)
                            s_bc.inc(t.matmul(pb[5][:, 0:F], ones1f[:],
                                              lnr[0:1, u * F : (u + 1) * F],
                                              start=True, stop=True))

                @block.vector
                def _(v):
                    for tt in range(NT + 1):
                        if tt < NT:
                            v.wait_ge(s_mm.h, b_mm + 2 * (tt + 1))
                            if tt >= 1:
                                v.wait_ge(s_ro.h, b_ro + tt)
                            sl = slice(tt * F, (tt + 1) * F)
                            TSC(v, lnm[0:1, sl], pb[6][0:1, 0:F], 1.0 / D, ALU.mult)
                            TSC(v, lnq[0:1, :], pb[7][0:1, 0:F], 1.0 / D, ALU.mult)
                            v.drain()
                            TT(v, lnsc[0:1, :], lnm[0:1, sl], lnm[0:1, sl])
                            v.drain()
                            TT(v, lnq[0:1, :], lnq[0:1, :], lnsc[0:1, :], ALU.subtract)
                            v.drain()
                            s_vs.inc(TSC(v, lnq[0:1, :], lnq[0:1, :], EPS, ALU.add))
                        if tt >= 1 and lp >= 4:
                            u = tt - 1
                            v.wait_ge(s_bc.h, b_bc + u + 1)
                            sl = slice(u * F, (u + 1) * F)
                            for kc in range(KD):
                                STT(v, lnap[:], stream[:, kc, sl], 1.0,
                                    pb[4][:, 0:F], ALU.mult, ALU.subtract)
                                v.drain()
                                s_ap.inc(TT(v, uout[:, kc, sl], lnap[:], pb[5][:, 0:F]))
                                v.drain()

                b.fence(fbuf)

        def emit_ffn(uin, call_idx, parts="all"):
            P = {"experts": 1, "loss": 2, "base": 3, "all": 4}[parts]
            s_w = DStream(nc, "fw", 4)
            s_h = b.pooled("fh")
            s_he = b.pooled("fhe")
            s_r = b.pooled("fr")
            s_sqd = b.pooled("fsq")
            s_ls = b.pooled("fls")
            s_ccd = b.pooled("fcc")
            s_msk = b.pooled("fmsk")
            s_sel = b.pooled("fsel")
            s_bmm = b.pooled("fbmm")
            s_bev = b.pooled("fbev")
            s_amm = b.pooled("famm")
            s_aev = b.pooled("faev")
            s_omm = b.pooled("fomm")
            s_oev = b.pooled("foev")
            b_h, b_he, b_r, b_sqd, b_ls = s_h.n, s_he.n, s_r.n, s_sqd.n, s_ls.n
            b_msk, b_bmm, b_bev = s_msk.n, s_bmm.n, s_bev.n
            b_amm, b_aev, b_omm, b_oev = s_amm.n, s_aev.n, s_omm.n, s_oev.n

            with ExitStack() as px:
                wexp = [b.sb(f"f{call_idx}wex{i}", [128, KD * H + HC * D], BF16, px) for i in range(2)]
                hbuf = b.sb(f"f{call_idx}hbuf", [128, HC, TC], BF16, px)
                abuf = b.sb(f"f{call_idx}abuf", [RA, TC], BF16, px)
                accb = b.sb(f"f{call_idx}accb", [128, E * KD * NT], F32, px)
                lcol = b.sb(f"f{call_idx}lcol", [128, E], F32, px)
                wadsel = b.sb(f"f{call_idx}wadsel", [128, KD, RA], BF16, px)
                wausel = b.sb(f"f{call_idx}wausel", [RA, D], BF16, px)
                badsel = b.sb(f"f{call_idx}badsel", [RA, 1], F32, px)

                @block.sync
                def _(sy):
                    for e in range(E):
                        if e >= 2:
                            sy.wait_ge(s_r.h, b_r + (e - 1) * KD * NT)
                        w = wexp[e % 2]
                        s_w.inc(sy.dma_start(out=w[:, 0 : KD * H], in_=wd1c_h[e][:]), 2 * e)
                        s_w.inc(sy.dma_start(out=w[:, KD * H :], in_=wd2c_h[e][:]), 2 * e + 1)

                @block.tensor
                def _(t):
                    for e in range(E):
                        s_w.wait(t, 2 * e)
                        s_w.wait(t, 2 * e + 1)
                        w = wexp[e % 2]
                        wd1v = w[:, 0 : KD * H].rearrange("p (a c) -> p a c", c=H)
                        wd2v = w[:, KD * H :].rearrange("p (a c) -> p a c", c=D)
                        for hc in range(HC):
                            for tt in range(NT):
                                j = hc * NT + tt
                                bank = pb[j % 2]
                                jj = e * HC * NT + j
                                if jj >= 2:
                                    t.wait_ge(s_he.h, b_he + jj - 1)
                                mm = None
                                for kc in range(KD):
                                    mm = t.matmul(bank[:, 0:F],
                                                  wd1v[:, kc, hc * 128 : (hc + 1) * 128],
                                                  uin[:, kc, tt * F : (tt + 1) * F],
                                                  start=(kc == 0), stop=(kc == KD - 1))
                                s_h.inc(mm)
                        t.wait_ge(s_he.h, b_he + (e + 1) * HC * NT)
                        for pc in range(KD):
                            for tt in range(NT):
                                j = pc * NT + tt
                                bank = pb[2 + j % 2]
                                jj = e * KD * NT + j
                                if jj >= 2:
                                    t.wait_ge(s_sqd.h, b_sqd + jj - 1)
                                for hc in range(HC):
                                    t.matmul(bank[:, 0:F],
                                             wd2v[:, hc, pc * 128 : (pc + 1) * 128],
                                             hbuf[:, hc, tt * F : (tt + 1) * F],
                                             start=(hc == 0), stop=False)
                                s_r.inc(t.matmul(bank[:, 0:F], negidb[:],
                                                 uin[:, pc, tt * F : (tt + 1) * F],
                                                 start=False, stop=True))

                @block.scalar
                def _(s):
                    for e in range(E):
                        for hc in range(HC):
                            for tt in range(NT):
                                j = hc * NT + tt
                                s.wait_ge(s_h.h, b_h + e * HC * NT + j + 1)
                                bank = pb[j % 2]
                                s_he.inc(s.activation(
                                    out=hbuf[:, hc, tt * F : (tt + 1) * F],
                                    in_=bank[:, 0:F], func=ACT.Relu,
                                    bias=bd1s[:, e * HC + hc : e * HC + hc + 1]))
                        for pc in range(KD):
                            for tt in range(NT):
                                j = pc * NT + tt
                                s.wait_ge(s_r.h, b_r + e * KD * NT + j + 1)
                                bank = pb[2 + j % 2]
                                s_sqd.inc(s.activation(
                                    out=bank[:, 0:F], in_=bank[:, 0:F], func=ACT.Square,
                                    bias=bd2s[:, e * KD + pc : e * KD + pc + 1],
                                    accum_out=accb[:, e * KD * NT + j : e * KD * NT + j + 1]))

                @block.tensor
                def _(t):
                    if P < 3:
                        return
                    for pc in range(KD):
                        for tt in range(NT):
                            j = pc * NT + tt
                            bank = pb[4 + j % 2]
                            if j >= 2:
                                t.wait_ge(s_bev.h, b_bev + j - 1)
                            mm = None
                            for kc in range(KD):
                                mm = t.matmul(bank[:, 0:F],
                                              wbase_s[:, kc, pc * 128 : (pc + 1) * 128],
                                              uin[:, kc, tt * F : (tt + 1) * F],
                                              start=(kc == 0), stop=(kc == KD - 1))
                            s_bmm.inc(mm)

                @block.vector
                def _(v):
                    if P < 3:
                        return
                    for pc in range(KD):
                        for tt in range(NT):
                            j = pc * NT + tt
                            v.wait_ge(s_bmm.h, b_bmm + j + 1)
                            bank = pb[4 + j % 2]
                            sl = slice(tt * F, (tt + 1) * F)
                            s_bev.inc(STT(v, stream[:, pc, sl], bank[:, 0:F], 0.5,
                                          stream[:, pc, sl]))

                @block.vector
                def _(v):
                    if P < 2:
                        return
                    for e in range(E):
                        v.wait_ge(s_sqd.h, b_sqd + (e + 1) * KD * NT)
                        s_ls.inc(v.tensor_reduce(out=lcol[:, e : e + 1],
                                                 in_=accb[:, e * KD * NT : (e + 1) * KD * NT], axis=AX.X, op=ALU.add))

                @block.tensor
                def _(t):
                    if P < 2:
                        return
                    t.wait_ge(s_ls.h, b_ls + E)
                    s_ls.inc(t.matmul(pb[6][0:1, 0:E], ones128f[:], lcol[:, :],
                                      start=True, stop=True))

                @block.vector
                def _(v):
                    if P < 2:
                        return
                    v.wait_ge(s_ls.h, b_ls + E + 1)
                    s_ls.inc(v.tensor_copy(out=lrow[:], in_=pb[6][0:1, 0:E]))

                s_cd1 = b.pooled("fcd1")
                s_cd2 = b.pooled("fcd2")
                b_cd1, b_ccd2 = s_cd1.n, s_ccd.n

                @block.gpsimd
                def _(g):
                    if P < 2:
                        return
                    g.wait_ge(s_ls.h, b_ls + E + 2)
                    s_cd1.incd(g.dma_start(out=loss_in_h[call_idx][:], in_=lrow[:]))
                    g.wait_ge(s_cd1.h, b_cd1 + 16)
                    s_ccd.inc(g.collective_compute(
                        "AllReduce", ALU.add, replica_groups=[list(range(NC))],
                        ins=[loss_in_h[call_idx].opt()],
                        outs=[loss_out_h[call_idx].opt()]))
                    g.wait_ge(s_ccd.h, b_ccd2 + 1)
                    s_cd2.incd(g.dma_start(out=lrow[:], in_=loss_out_h[call_idx][:]))

                @block.vector
                def _(v):
                    if P < 2:
                        return
                    s_cd2.wait(v)
                    s_msk.inc(v.tensor_reduce(out=tiny[0:1, 0:1], in_=lrow[:],
                                              axis=AX.X, op=ALU.min))
                    s_msk.wait(v)
                    TSC(v, maskrow[:], lrow[:], tiny[0:1, 0:1], ALU.is_equal)
                    v.drain()
                    s_msk.inc(v.tensor_tensor_scan(out=tiny[0:1, 0:E], data0=onerow[:],
                                                   data1=maskrow[:], initial=0.0,
                                                   op0=ALU.mult, op1=ALU.add))
                    s_msk.wait(v)
                    TSC(v, tiny[0:1, 0:E], tiny[0:1, 0:E], 1.0, ALU.is_equal)
                    v.drain()
                    s_msk.inc(TT(v, maskrow[:], maskrow[:], tiny[0:1, 0:E]))

                @block.tensor
                def _(t):
                    if P < 2:
                        return
                    s_msk.wait(t)
                    s_msk.inc(t.matmul(pb[6][:, 0:E], ones1f[:], maskrow[0:1, :],
                                       start=True, stop=True))

                @block.vector
                def _(v):
                    if P < 4:
                        return
                    s_msk.wait(v)
                    for e in range(E):
                        if e > 0:
                            v.drain()
                        sc = pb[6][:, e : e + 1]
                        scr = pb[6][0:RA, e : e + 1]
                        wadv = wadsel[:, :, :]
                        wallv = wad_all[:, :, e, :]
                        if e == 0:
                            TSC(v, wadv, wallv, sc, ALU.mult)
                            TSC(v, wausel[:], wau_all[:, e, :], scr, ALU.mult)
                            TSC(v, badsel[:], bads[:, e : e + 1], scr, ALU.mult)
                            TSC(v, hb05[:], baus[:, e * KD : (e + 1) * KD], sc, ALU.mult)
                        else:
                            STT(v, wadv, wallv, sc, wadv)
                            STT(v, wausel[:], wau_all[:, e, :], scr, wausel[:])
                            STT(v, badsel[:], bads[:, e : e + 1], scr, badsel[:])
                            STT(v, hb05[:], baus[:, e * KD : (e + 1) * KD], sc, hb05[:])
                    v.drain()
                    STT(v, hb05[:], bbase[:], 1.0, hb05[:], ALU.mult, ALU.add)
                    v.drain()
                    s_sel.inc(TSC(v, hb05[:], hb05[:], 0.5, ALU.mult))

                @block.tensor
                def _(t):
                    if P < 4:
                        return
                    s_sel.wait(t)
                    for tt in range(NT):
                        bank = pb[0 + tt % 2]
                        if tt >= 2:
                            t.wait_ge(s_aev.h, b_aev + tt - 1)
                        mm = None
                        for kc in range(KD):
                            mm = t.matmul(bank[0:RA, 0:F], wadsel[:, kc, :],
                                          uin[:, kc, tt * F : (tt + 1) * F],
                                          start=(kc == 0), stop=(kc == KD - 1))
                        s_amm.inc(mm)

                @block.scalar
                def _(s):
                    if P < 4:
                        return
                    for tt in range(NT):
                        s.wait_ge(s_amm.h, b_amm + tt + 1)
                        bank = pb[0 + tt % 2]
                        s_aev.inc(s.activation(out=abuf[:, tt * F : (tt + 1) * F],
                                               in_=bank[0:RA, 0:F], func=ACT.Relu,
                                               bias=badsel[:]))

                @block.tensor
                def _(t):
                    if P < 4:
                        return
                    for pc in range(KD):
                        for tt in range(NT):
                            j = pc * NT + tt
                            bank = pb[2 + j % 2]
                            t.wait_ge(s_aev.h, b_aev + tt + 1)
                            if j >= 2:
                                t.wait_ge(s_oev.h, b_oev + j - 1)
                            s_omm.inc(t.matmul(bank[:, 0:F],
                                               wausel[:, pc * 128 : (pc + 1) * 128],
                                               abuf[:, tt * F : (tt + 1) * F],
                                               start=True, stop=True))

                @block.vector
                def _(v):
                    if P < 4:
                        return
                    s_bev.wait(v)
                    for pc in range(KD):
                        for tt in range(NT):
                            j = pc * NT + tt
                            v.wait_ge(s_omm.h, b_omm + j + 1)
                            bank = pb[2 + j % 2]
                            sl = slice(tt * F, (tt + 1) * F)
                            STT(v, stream[:, pc, sl], bank[:, 0:F], 0.5, stream[:, pc, sl])
                            v.drain()
                            s_oev.inc(TSC(v, stream[:, pc, sl], stream[:, pc, sl],
                                          hb05[:, pc : pc + 1], ALU.add))

                b.fence(fbuf)

        with ExitStack() as pf:
            u1f = b.sb("u1f", [128, KD, TC], BF16, pf)
            if upto != "p0":
                lnlvl = {"lnA": 1, "lnB": 2, "lnC": 3}.get(upto, 4)
                emit_ln(u1f, "1", lp=lnlvl)
            if upto not in ("ln1", "p0", "lnA", "lnB", "lnC"):
                emit_ffn(u1f, 0, parts=("all" if upto in ("ffn1", "all") else upto))

        if upto in ("ffn1", "ln1", "experts", "loss", "base", "p0", "lnA", "lnB", "lnC"):
            emit_store(nc, block, b, stream, pb, identf, out_h, fbuf)
            es_stream.close()
            return nc

        s_ag = b.sem("ag")
        with ExitStack() as p3:
            u2f = b.sb("u2f", [128, KD, TC], BF16, p3)
            emit_ln(u2f, "2")
            s_u2d = b.sem("u2d")
            s_spl = b.sem("spl")

            @block.sync
            def _(sy):
                s_u2d.incd(sy.dma_start(out=u2_in_h.rearrange("(a p) t -> p a t", p=128),
                                        in_=u2f[:, :, :]))
                s_spl.incd(sy.dma_start(out=sspill_h[:],
                                        in_=stream.rearrange("p a t -> p (a t)")))

            @block.gpsimd
            def _(g):
                g.wait_ge(s_u2d.h, 16)
                s_ag.inc(g.collective_compute(
                    "AllGather", ALU.bypass, replica_groups=[list(range(NC))],
                    ins=[u2_in_h.opt()], outs=[u2_out_h.opt()]))

            b.fence(fbuf, dma_sems=[s_spl, s_u2d])
        es_stream.close()

        with ExitStack() as p4:
            u2t = [b.sb(f"u2t{i}", [128, KD, TS], BF16, p4) for i in range(2)]
            xcv = [b.sb(f"xcq{i}", [128, DC2, TS], F32, p4) for i in range(2)]
            xcb = b.sb("xcbq", [128, DC2, TS], BF16, p4)
            srv = [b.sb(f"srq{i}", [128, DC2, TS], F32, p4) for i in range(2)]
            xconv = [b.sb(f"xcvq{i}", [128, DC2, 3 + TS], BF16, p4) for i in range(2)]
            dpre = [b.sb(f"dpreq{i}", [128, TS], F32, p4) for i in range(2)]
            dpreb = b.sb("dprebq", [128, TS], BF16, p4)
            bcstage = [b.sb(f"bcstq{i}", [64, TS], F32, p4) for i in range(2)]
            bcb = [b.sb(f"bcbq{i}", [64, TS], BF16, p4) for i in range(2)]
            dnegv = b.sb("dnegv", [128, DC2, TS], F32, p4)
            dxn = b.sb("dxnq", [128, DC2, TS], F32, p4)
            scan_a = b.sb("scana", [128, 2, TS], F32, p4)
            scan_u = b.sb("scanu", [128, TS], F32, p4)
            scan_h = b.sb("scanh", [128, TS], F32, p4)
            scan_g = b.sb("scang", [128, 2, TS], BF16, p4)
            ytmp = b.sb("ytmp", [128, F], F32, p4)
            ybuf = b.sb("ybufq", [128, DC2, TS], BF16, p4)

            s_bcz = b.sem("bcz")

            @block.vector
            def _(v):
                for i in range(2):
                    s_bcz.inc(v.memset(bcb[i][0:64, :], 0.0))

            s_u2l = DStream(nc, "u2l")
            s_fm = b.sem("fm")
            s_fe = b.sem("fe")
            s_cb = b.sem("cb")
            s_gd = DStream(nc, "gd", n=4)
            s_ar = b.sem("ar")
            s_ld = DStream(nc, "arld", n=4)
            s_dc = b.sem("dcst")
            s_zm = b.sem("zm")
            s_ss = b.sem("sself")
            s_vs2 = b.sem("vself")
            s_bb = b.sem("bb")
            s_bc2 = b.sem("bc2")
            s_ze = b.sem("ze")
            s_dx = b.sem("dx")
            s_ae = b.sem("ae")
            s_ub = b.sem("ub")
            s_sc = b.sem("sc")
            s_hs = b.sem("hs")
            s_cm = b.sem("cm")
            s_ir = b.sem("ir")
            s_yf = b.sem("yf")
            s_ys = DStream(nc, "ys")
            s_sd = b.sem("sd")

            fm_after_res = {}
            fm_c = [0]
            fe_c = [0]

            def front_sync(s):
                bq = s % 2

                @block.sync
                def _(sy):
                    s_ag.wait(sy)
                    if s >= 2:
                        sy.wait_ge(s_fm.h, 18 * (s - 2) + 8)
                    src = u2_out_h[s // 2].rearrange("(a p) t -> p a t", p=128)
                    s_u2l.inc(sy.dma_start(
                        out=u2t[bq][:, :, :],
                        in_=src[:, :, (s % 2) * TS : (s % 2) * TS + TS]), s)

            FJOBS = []
            for dc_ in range(DC2):
                for tt_ in range(2):
                    FJOBS.append(("res", dc_, tt_))
            for dc_ in range(DC2):
                for tt_ in range(2):
                    FJOBS.append(("x", dc_, tt_))
            for dc_ in range(DC2):
                for tt_ in range(2):
                    FJOBS.append(("conv", dc_, tt_))
            FJOBS += [("B", None, 0), ("B", None, 1), ("C", None, 0), ("C", None, 1),
                      ("dpre", None, 0), ("dpre", None, 1)]
            ESCHED = {1: [0], 2: [1], 3: [2], 4: [3], 5: [4], 6: [5], 7: [6],
                      8: [7], 11: [8, 9], 13: [10, 11], 14: [12], 15: [13],
                      16: [14], 17: [15], 18: [16], 19: [17]}

            def emit_front_job(t, s, j):
                bq = s % 2
                kind, dc, tt = FJOBS[j]
                bank = pb[j % 2]
                fm0 = 18 * s
                gj = fm0 + j
                if gj >= 2:
                    t.wait_ge(s_fe.h, gj - 1)
                if kind == "x":
                    mm = None
                    for kc in range(KD):
                        mm = t.matmul(bank[:, 0:F],
                                      win_s[:, kc, dc * 128 : (dc + 1) * 128],
                                      u2t[bq][:, kc, tt * F : (tt + 1) * F],
                                      start=(kc == 0), stop=(kc == KD - 1))
                elif kind == "res":
                    if j == 0:
                        s_u2l.wait(t, s)
                    mm = None
                    for kc in range(KD):
                        mm = t.matmul(bank[:, 0:F],
                                      wres_s[:, kc, dc * 128 : (dc + 1) * 128],
                                      u2t[bq][:, kc, tt * F : (tt + 1) * F],
                                      start=(kc == 0), stop=(kc == KD - 1))
                elif kind == "conv":
                    t.wait_ge(s_fe.h, fm0 + 4 + dc * 2 + 2)
                    mm = None
                    for i4 in range(4):
                        mm = t.matmul(bank[:, 0:F],
                                      convdiag[:, dc * 4 + i4, :],
                                      xconv[bq][:, dc, tt * F + i4 : tt * F + i4 + F],
                                      start=(i4 == 0), stop=(i4 == 3))
                elif kind == "B":
                    t.wait_ge(s_cb.h, 2 * s + 2)
                    mm = None
                    for dcc in range(DC2):
                        mm = t.matmul(bank[0:DS, 0:F],
                                      wbneg_s[:, dcc, :],
                                      xcb[:, dcc, tt * F : (tt + 1) * F],
                                      start=(dcc == 0), stop=(dcc == DC2 - 1))
                elif kind == "C":
                    mm = None
                    for dcc in range(DC2):
                        mm = t.matmul(bank[0:DS, 0:F],
                                      wc_s[:, dcc, :],
                                      xcb[:, dcc, tt * F : (tt + 1) * F],
                                      start=(dcc == 0), stop=(dcc == DC2 - 1))
                elif kind == "dpre":
                    mm = None
                    for dcc in range(DC2):
                        mm = t.matmul(bank[:, 0:F],
                                      wdt_s[:, dcc, :],
                                      xcb[:, dcc, tt * F : (tt + 1) * F],
                                      start=(dcc == 0), stop=(dcc == DC2 - 1))
                s_fm.inc(mm)

            def emit_front_evac(sc, s, j):
                bq = s % 2
                kind, dc, tt = FJOBS[j]
                fm0 = 18 * s
                sc.wait_ge(s_fm.h, fm0 + j + 1)
                bank = pb[j % 2]
                if kind == "x":
                    ev = sc.activation(out=xconv[bq][:, dc, 3 + tt * F : 3 + (tt + 1) * F],
                                       in_=bank[:, 0:F], func=ACT.Identity)
                elif kind == "res":
                    ev = sc.activation(out=srv[bq][:, dc, tt * F : (tt + 1) * F],
                                       in_=bank[:, 0:F], func=ACT.Identity)
                elif kind == "conv":
                    ev = sc.activation(out=xcv[bq][:, dc, tt * F : (tt + 1) * F],
                                       in_=bank[:, 0:F], func=ACT.Silu,
                                       bias=convb[:, dc : dc + 1])
                elif kind == "B":
                    ev = sc.activation(out=bcstage[bq][0:DS, tt * F : (tt + 1) * F],
                                       in_=bank[0:DS, 0:F], func=ACT.Identity)
                elif kind == "C":
                    ev = sc.activation(out=bcstage[bq][32 : 32 + DS, tt * F : (tt + 1) * F],
                                       in_=bank[0:DS, 0:F], func=ACT.Identity)
                elif kind == "dpre":
                    ev = sc.activation(out=dpre[bq][:, tt * F : (tt + 1) * F],
                                       in_=bank[:, 0:F], func=ACT.Identity)
                s_fe.inc(ev)
                if kind == "conv" and tt == 1:
                    sc.wait_ge(s_fe.h, fm0 + j + 1)
                    cv = sc.activation(out=xcb[:, dc, :], in_=xcv[bq][:, dc, :],
                                       func=ACT.Identity)
                    s_cb.inc(cv)

            def front_g(s):
                bq = s % 2

                @block.gpsimd
                def _(g):
                    g.wait_ge(s_fe.h, 18 * (s + 1))
                    s_gd.inc(g.dma_start(out=seg_in_h[s][0:DR, :], in_=dpre[bq][:]), 4 * s)
                    s_gd.inc(g.dma_start(out=seg_in_h[s][DR : DR + DS, :],
                                         in_=bcstage[bq][0:DS, :]), 4 * s + 1)
                    s_gd.inc(g.dma_start(out=seg_in_h[s][DR + DS : DR + 2 * DS, :],
                                         in_=bcstage[bq][32 : 32 + DS, :]), 4 * s + 2)
                    s_gd.wait(g, 4 * s)
                    s_gd.wait(g, 4 * s + 1)
                    s_gd.wait(g, 4 * s + 2)
                    s_ar.inc(g.collective_compute(
                        "AllReduce", ALU.add, replica_groups=[list(range(NC))],
                        ins=[seg_in_h[s].opt()], outs=[seg_out_h[s].opt()]))

                @block.sync
                def _(sy):
                    sy.wait_ge(s_ar.h, s + 1)
                    s_ld.inc(sy.dma_start(out=dpre[bq][:], in_=seg_out_h[s][0:DR, :]), 4 * s)
                    s_ld.inc(sy.dma_start(out=bcstage[bq][0:DS, :],
                                          in_=seg_out_h[s][DR : DR + DS, :]), 4 * s + 1)
                    s_ld.inc(sy.dma_start(out=bcstage[bq][32 : 32 + DS, :],
                                          in_=seg_out_h[s][DR + DS : DR + 2 * DS, :]), 4 * s + 2)

            def emit_z_mms(t, s):
                t.wait_ge(s_dc.h, 2 * (s + 1))
                if s >= 1:
                    t.wait_ge(s_ub.h, 32 * s)
                for j in range(4):
                    dc, tt = j // 2, j % 2
                    bank = pb[4 + j % 2]
                    if j >= 2:
                        t.wait_ge(s_ze.h, 4 * s + j - 1)
                    mm = t.matmul(bank[:, 0:F], dtw_s[:, dc * 128 : (dc + 1) * 128],
                                  dpreb[:, tt * F : (tt + 1) * F],
                                  start=True, stop=True)
                    s_zm.inc(mm)

            def emit_z_chain_s(sc, s):
                bq = s % 2
                if s <= 1:
                    sc.wait_ge(s_bcz.h, 2)
                s_ld.wait(sc, 4 * s)
                s_ld.wait(sc, 4 * s + 1)
                s_ld.wait(sc, 4 * s + 2)
                sc.activation(out=dpreb[:], in_=dpre[bq][:], func=ACT.Identity)
                sc.activation(out=bcb[bq][0:DS, :], in_=bcstage[bq][0:DS, :],
                              func=ACT.Identity)
                s_dc.inc(sc.activation(out=bcb[bq][32 : 32 + DS, :],
                                       in_=bcstage[bq][32 : 32 + DS, :],
                                       func=ACT.Identity), 2)
                for j in range(4):
                    dc, tt = j // 2, j % 2
                    sc.wait_ge(s_zm.h, 4 * s + j + 1)
                    bank = pb[4 + j % 2]
                    sg = sc.activation(out=bank[:, 0:F], in_=bank[:, 0:F],
                                       func=ACT.Exp, bias=dtbn[:, dc : dc + 1])
                    s_ss.inc(sg)
                    sc.wait_ge(s_ss.h, s_ss.n)
                    ev = sc.activation(out=dnegv[:, dc, tt * F : (tt + 1) * F],
                                       in_=bank[:, 0:F], func=ACT.Ln, bias=1.0)
                    s_ze.inc(ev)

            def scan_te(s, with_front):
                bq = s % 2
                fq = list(range(18)) if with_front else []

                def ir_group(t, l):
                    dcl, n = l // DS, l % DS
                    li = (s * DC2 + dcl) * DS + n
                    t.wait_ge(s_cm.h, li + 1)
                    if n == 0:
                        if dcl == 0:
                            t.wait_ge(s_ze.h, 4 * (s + 1))
                        else:
                            t.wait_ge(s_yf.h, 4 * s + 2)
                    mm = None
                    for c in range(2):
                        mm = t.matmul(pb[2 + c][:, 0:F], identb[:],
                                      scan_g[:, dcl, c * F : (c + 1) * F],
                                      start=(n == 0), stop=(n == DS - 1))
                    s_ir.inc(mm)

                @block.tensor
                def _(t):
                    for l in range(DC2 * DS):
                        dcl, n = l // DS, l % DS
                        li = (s * DC2 + dcl) * DS + n
                        if l == 0:
                            t.wait_ge(s_ze.h, 4 * s + 4)
                        if li >= 1:
                            t.wait_ge(s_ub.h, li)
                        for c in range(2):
                            mm = t.matmul(pb[4 + c][:, 0:F], selb[:, n, :],
                                          bcb[bq][:, c * F : (c + 1) * F],
                                          start=True, stop=True)
                        s_bb.inc(mm)
                        if l >= 1:
                            ir_group(t, l - 1)
                        if li >= 1:
                            t.wait_ge(s_cm.h, li)
                        for c in range(2):
                            mm = t.matmul(pb[6 + c][:, 0:F], selb[:, DS + n, :],
                                          bcb[bq][:, c * F : (c + 1) * F],
                                          start=True, stop=True)
                        s_bc2.inc(mm)
                        if fq:
                            emit_front_job(t, s + 1, fq.pop(0))
                    ir_group(t, DC2 * DS - 1)
                    while fq:
                        emit_front_job(t, s + 1, fq.pop(0))
                    if s + 1 < NSEG:
                        emit_z_mms(t, s + 1)

            def scan_s(s, with_front):
                bq = s % 2

                @block.scalar
                def _(sc):
                    if with_front:
                        bq1 = (s + 1) % 2
                        if s + 1 >= 2:
                            sc.wait_ge(s_sd.h, s)
                        for dc in range(DC2):
                            sc.activation(out=xconv[bq1][:, dc, 0:3],
                                          in_=xconv[1 - bq1][:, dc, TS : TS + 3],
                                          func=ACT.Identity)
                    for dcl in range(DC2):
                        for n in range(DS):
                            i = dcl * DS + n
                            li = (s * DC2 + dcl) * DS + n
                            if li >= 1:
                                sc.wait_ge(s_sc.h, li)
                            ev = sc.activation(out=scan_a[:, dcl, :],
                                               in_=dnegv[:, dcl, :], func=ACT.Exp,
                                               scale=ppos[:, dcl * DS + n : dcl * DS + n + 1])
                            s_ae.inc(ev)
                            if with_front:
                                for j in ESCHED.get(i, []):
                                    emit_front_evac(sc, s + 1, j)
                    if with_front:
                        bq1 = (s + 1) % 2
                        sc.wait_ge(s_fe.h, 18 * (s + 1) + 4)
                        sc.activation(out=srv[bq1].rearrange("p a t -> p (a t)"),
                                      in_=srv[bq1].rearrange("p a t -> p (a t)"),
                                      func=ACT.Silu)
                    if s + 1 < NSEG:
                        emit_z_chain_s(sc, s + 1)

            def scan_v(s):
                bq = s % 2

                @block.vector
                def _(v):
                    v.wait_ge(s_ze.h, 4 * s + 4)
                    s_dx.inc(TT(v, dxn.rearrange("p a t -> p (a t)"),
                                dnegv.rearrange("p a t -> p (a t)"),
                                xcv[bq].rearrange("p a t -> p (a t)")))
                    v.wait_ge(s_dx.h, s_dx.n)
                    for dcl in range(DC2):
                        for n in range(DS):
                            li = (s * DC2 + dcl) * DS + n
                            v.wait_ge(s_bb.h, li + 1)
                            if li >= 1:
                                v.wait_ge(s_sc.h, li)
                            tv = TT(v, scan_u[:], dxn[:, dcl, :], pB[:, :])
                            s_ub.inc(tv)
                            v.wait_ge(s_ub.h, li + 1)
                            v.wait_ge(s_ae.h, li + 1)
                            if s >= 1:
                                v.wait_ge(s_hs.h, ((s - 1) * DC2 + dcl) * DS + n + 1)
                            tv = v.tensor_tensor_scan(
                                out=scan_h[:], data0=scan_a[:, dcl, :],
                                data1=scan_u[:],
                                initial=(0.0 if s == 0 else hstate[:, dcl, n : n + 1]),
                                op0=ALU.mult, op1=ALU.add)
                            s_sc.inc(tv)
                            v.wait_ge(s_sc.h, li + 1)
                            s_hs.inc(v.tensor_copy(out=hstate[:, dcl, n : n + 1],
                                                   in_=scan_h[:, TS - 1 : TS]))
                            v.wait_ge(s_bc2.h, li + 1)
                            tv = TT(v, scan_g[:, dcl, :], scan_h[:], pC[:, :])
                            s_cm.inc(tv)
                        v.wait_ge(s_ir.h, (s * DC2 + dcl) * DS + DS)
                        if 2 * s + dcl - 2 >= 0:
                            s_ys.wait(v, 2 * s + dcl - 2)
                        for c in range(2):
                            tv = STT(v, ytmp[:], xcv[bq][:, dcl, c * F : (c + 1) * F],
                                     dvec[:, dcl : dcl + 1], pb[2 + c][:, 0:F],
                                     ALU.mult, ALU.add)
                            s_vs2.inc(tv)
                            v.wait_ge(s_vs2.h, s_vs2.n)
                            tv = TT(v, ybuf[:, dcl, c * F : (c + 1) * F],
                                    ytmp[:], srv[bq][:, dcl, c * F : (c + 1) * F])
                            s_yf.inc(tv)
                            v.wait_ge(s_yf.h, s_yf.n)
                    s_sd.inc(v.memset(ytmp[0:1, 0:1], 0.0))

                @block.sync
                def _(sy):
                    for dcl in range(DC2):
                        sy.wait_ge(s_yf.h, 4 * s + 2 * (dcl + 1))
                        dst = y_in_h[s // 2, dcl * 128 + 0 : dcl * 128 + 128,
                                     (s % 2) * TS : (s % 2) * TS + TS]
                        s_ys.inc(sy.dma_start(out=dst, in_=ybuf[:, dcl, :]), 2 * s + dcl)

            front_sync(0)

            @block.tensor
            def _(t):
                for j in range(18):
                    emit_front_job(t, 0, j)

            @block.scalar
            def _(sc):
                bq1 = 0
                for dc in range(DC2):
                    sc.activation(out=xconv[0][:, dc, 0:3],
                                  in_=xconv[0][:, dc, 0:3], func=ACT.Identity,
                                  scale=0.0)
                for j in range(18):
                    emit_front_evac(sc, 0, j)
                sc.wait_ge(s_fe.h, 4)
                sc.activation(out=srv[0].rearrange("p a t -> p (a t)"),
                              in_=srv[0].rearrange("p a t -> p (a t)"),
                              func=ACT.Silu)
            front_g(0)

            @block.scalar
            def _(sc):
                pass

            @block.tensor
            def _(t):
                pass

            @block.scalar
            def _(sc):
                sc.wait_ge(s_bcz.h, 2)
                s_ld.wait(sc, 0)
                s_ld.wait(sc, 1)
                s_ld.wait(sc, 2)
                sc.activation(out=dpreb[:], in_=dpre[0][:], func=ACT.Identity)
                sc.activation(out=bcb[0][0:DS, :], in_=bcstage[0][0:DS, :],
                              func=ACT.Identity)
                s_dc.inc(sc.activation(out=bcb[0][32 : 32 + DS, :],
                                       in_=bcstage[0][32 : 32 + DS, :],
                                       func=ACT.Identity), 2)

            @block.tensor
            def _(t):
                emit_z_mms(t, 0)

            @block.scalar
            def _(sc):
                for j in range(4):
                    dc, tt = j // 2, j % 2
                    sc.wait_ge(s_zm.h, j + 1)
                    bank = pb[4 + j % 2]
                    sg = sc.activation(out=bank[:, 0:F], in_=bank[:, 0:F],
                                       func=ACT.Exp, bias=dtbn[:, dc : dc + 1])
                    s_ss.inc(sg)
                    sc.wait_ge(s_ss.h, s_ss.n)
                    ev = sc.activation(out=dnegv[:, dc, tt * F : (tt + 1) * F],
                                       in_=bank[:, 0:F], func=ACT.Ln, bias=1.0)
                    s_ze.inc(ev)

            for s_i in range(NSEG):
                if s_i + 1 < NSEG:
                    front_sync(s_i + 1)
                scan_te(s_i, with_front=(s_i + 1 < NSEG))
                scan_s(s_i, with_front=(s_i + 1 < NSEG))
                scan_v(s_i)
                if s_i + 1 < NSEG:
                    front_g(s_i + 1)

            @block.sync
            def _(sy):
                for _ss in s_ys.sems:
                    sy.wait_ge(_ss.h, _ss.n)
            b.fence(fbuf, dma_sems=s_ys.sems + s_ld.sems + s_u2l.sems)

        stream = b.sb("stream2", [128, KD, TC], F32, es_stream2 := ExitStack())
        s_a2a = b.sem("a2a")
        with ExitStack() as p5:
            wout_s = b.sb("wout_s", [128, DM // 128, D], BF16, p5)
            ytile = [b.sb(f"ytile{i}", [128, DM // 128, F], BF16, p5) for i in range(2)]
            s_wol = b.sem("wol")
            s_srl2 = b.sem("srl2")
            s_srl = DStream(nc, "srl")
            s_ytl = DStream(nc, "ytl")
            s_om = b.sem("om")
            s_oe = b.sem("oe")

            @block.gpsimd
            def _(g):
                from concourse.aot_env import allow_ici_all_to_all
                with allow_ici_all_to_all():
                    s_a2a.inc(g.collective_compute(
                        "AllToAll", ALU.bypass, replica_groups=[list(range(NC))],
                        ins=[y_in_h.opt()], outs=[y_out_h.opt()]))

            @block.sync
            def _(sy):
                s_wol.incd(sy.dma_start(
                    out=wout_s.rearrange("p a c -> p (a c)"), in_=woutc_h[:]))
                s_srl2.incd(sy.dma_start(out=stream.rearrange("p a t -> p (a t)"),
                                         in_=sspill_h[:]))
                s_a2a.wait(sy)
                yv = y_out_h.rearrange("i (c p) t -> p i c t", p=128)
                for tt in range(NT):
                    if tt >= 2:
                        sy.wait_ge(s_om.h, (tt - 1) * KD)
                    s_ytl.inc(sy.dma_start(
                        out=ytile[tt % 2].rearrange("p a f -> p a f"),
                        in_=yv[:, :, :, tt * F : (tt + 1) * F].rearrange("p i c t -> p (i c) t")), tt)

            @block.tensor
            def _(t):
                t.wait_ge(s_wol.h, 16)
                for tt in range(NT):
                    s_ytl.wait(t, tt)
                    for pc in range(KD):
                        j = tt * KD + pc
                        bank = pb[j % 2]
                        if j >= 2:
                            t.wait_ge(s_oe.h, j - 1)
                        mm = None
                        for kc in range(DM // 128):
                            mm = t.matmul(bank[:, 0:F],
                                          wout_s[:, kc, pc * 128 : (pc + 1) * 128],
                                          ytile[tt % 2][:, kc, :],
                                          start=(kc == 0), stop=(kc == DM // 128 - 1))
                        s_om.inc(mm)

            @block.vector
            def _(v):
                v.wait_ge(s_wol.h, 16)
                v.wait_ge(s_srl2.h, 16)
                for tt in range(NT):
                    for pc in range(KD):
                        j = tt * KD + pc
                        v.wait_ge(s_om.h, j + 1)
                        bank = pb[j % 2]
                        s_oe.inc(STT(v, stream[:, pc, tt * F : (tt + 1) * F],
                                     bank[:, 0:F], 1.0,
                                     stream[:, pc, tt * F : (tt + 1) * F]))

            b.fence(fbuf, dma_sems=s_ytl.sems + s_srl.sems)

        with ExitStack() as pf2:
            u3f = b.sb("u3f", [128, KD, TC], BF16, pf2)
            emit_ln(u3f, "3")
            emit_ffn(u3f, 1)

        emit_store(nc, block, b, stream, pb, identf, out_h, fbuf)
        es_stream2.close()
        return nc


def emit_store(nc, block, b, stream, pb, identf, out_h, tiny):
    s_tr = b.sem("otr")
    s_ev = b.sem("oev")
    s_st = DStream(nc, "ost")
    with ExitStack() as px:
        ostage = b.sb("ostage", [128, 2, 1024], F32, px)

        @block.tensor
        def _(t):
            for tt in range(TC // 128):
                for kc in range(KD):
                    j = tt * KD + kc
                    if j >= 2:
                        t.wait_ge(s_ev.h, j - 1)
                    s_tr.inc(t.transpose(pb[j % 2][:, 0:128],
                                         stream[:, kc, tt * 128 : (tt + 1) * 128],
                                         identf[:]))

        @block.vector
        def _(v):
            for tt in range(TC // 128):
                if tt >= 2:
                    s_st.wait(v, tt - 2)
                for kc in range(KD):
                    j = tt * KD + kc
                    v.wait_ge(s_tr.h, j + 1)
                    s_ev.inc(v.tensor_copy(out=ostage[:, tt % 2, kc * 128 : (kc + 1) * 128],
                                           in_=pb[j % 2][:, 0:128]))

        @block.sync
        def _(sy):
            for tt in range(TC // 128):
                sy.wait_ge(s_ev.h, (tt + 1) * KD)
                s_st.inc(sy.dma_start(out=out_h[tt * 128 : (tt + 1) * 128, :],
                                      in_=ostage[:, tt % 2, :]), tt)
            for _ss in s_st.sems:
                sy.wait_ge(_ss.h, _ss.n)


def _prep_inputs(inputs):
    g = lambda k: np.asarray(inputs[k])
    x = g("L0_attn_x_in")[0]
    per_core = []
    ident = np.eye(128, dtype=np.float32)
    sel32 = np.zeros((64, 32, 128), dtype=np.float32)
    for j in range(16):
        sel32[j, j, :] = 1.0
        sel32[32 + j, 16 + j, :] = 1.0
    for c in range(NC):
        ch = slice(c * DCH, (c + 1) * DCH)
        m = {}
        m["x0"] = np.ascontiguousarray(x[c * TC : (c + 1) * TC])
        m["ident"] = ident
        m["sel32"] = sel32
        m["w_base"] = np.ascontiguousarray(g("L0_ffn_W_base"))
        m["b_base"] = np.ascontiguousarray(g("L0_ffn_b_base").reshape(KD, 128).T)
        m["wd1"] = np.ascontiguousarray(g("L0_ffn_Wd1"))
        m["bd1"] = np.ascontiguousarray(g("L0_ffn_bd1").reshape(E, HC, 128).transpose(0, 2, 1))
        m["wd2"] = np.ascontiguousarray(g("L0_ffn_Wd2"))
        m["bd2"] = np.ascontiguousarray(g("L0_ffn_bd2").reshape(E, KD, 128).transpose(0, 2, 1))
        m["wad"] = np.ascontiguousarray(g("L0_ffn_Wa_down"))
        m["bad"] = np.ascontiguousarray(g("L0_ffn_ba_down").T)
        m["wau"] = np.ascontiguousarray(g("L0_ffn_Wa_up"))
        m["bau"] = np.ascontiguousarray(g("L0_ffn_ba_up").reshape(E, KD, 128).transpose(0, 2, 1))
        m["w_in"] = np.ascontiguousarray(g("L0_attn_W_in")[:, ch])
        m["w_res"] = np.ascontiguousarray(g("L0_attn_W_res")[:, ch])
        m["w_out"] = np.ascontiguousarray(g("L0_attn_W_out"))
        m["conv_w"] = np.ascontiguousarray(
            g("L0_attn_conv_w")[ch].reshape(DC2, 128, 4).transpose(1, 0, 2))
        m["conv_b"] = np.ascontiguousarray(
            g("L0_attn_conv_b")[ch].reshape(DC2, 128).T)
        m["a_log"] = np.ascontiguousarray(
            g("L0_attn_A_log")[ch].reshape(DC2, 128, DS).transpose(1, 0, 2))
        m["d_vec"] = np.ascontiguousarray(
            g("L0_attn_D")[ch].reshape(DC2, 128).T)
        m["w_b"] = np.ascontiguousarray(g("L0_attn_W_B")[ch])
        m["w_c"] = np.ascontiguousarray(g("L0_attn_W_C")[ch])
        m["w_dt"] = np.ascontiguousarray(g("L0_attn_W_dt")[ch])
        m["dt_w"] = np.ascontiguousarray(g("L0_attn_dt_w")[:, ch])
        m["dt_b"] = np.ascontiguousarray(
            g("L0_attn_dt_b")[ch].reshape(DC2, 128).T)
        per_core.append(m)
    return per_core


def kernel(**inputs):
    key = "all"
    if key not in _CACHE:
        _CACHE[key] = build_program(upto=key)
    nc = _CACHE[key]
    in_maps = _prep_inputs(inputs)
    res = run_bass_kernel_spmd(nc, in_maps, core_ids=list(range(NC)))
    out = np.concatenate([res.results[c]["out"] for c in range(NC)], axis=0)
    return out[None, :, :]


# version 05 (final)
# speedup vs baseline: 1.3428x; 1.0353x over previous
from contextlib import ExitStack

import numpy as np
import concourse.bass as bass
import concourse.mybir as mybir
from concourse.bass_utils import run_bass_kernel_spmd

F32 = mybir.dt.float32
BF16 = mybir.dt.bfloat16
ALU = mybir.AluOpType
ACT = mybir.ActivationFunctionType
AX = mybir.AxisListType

NC = 8
L = 16384
D = 1024
TC = L // NC
DM = 2048
DCH = DM // NC
DC2 = DCH // 128
DS = 16
DR = 128
E = 8
H = 256
HC = H // 128
RA = 64
TS = 1024
NSEG = L // TS
KD = D // 128
F = 512
NT = TC // F
EPS = 1e-5

_CACHE = {}


class Sem:
    __slots__ = ("h", "n")

    def __init__(self, nc, name):
        self.h = nc.alloc_semaphore(name)
        self.n = 0

    def inc(self, instr, k=1):
        instr.then_inc(self.h, k)
        self.n += k
        return instr

    def incd(self, instr):
        return self.inc(instr, 16)

    def wait(self, eng, val=None):
        eng.wait_ge(self.h, self.n if val is None else val)


class DStream:
    _uid = [0]

    def __init__(self, nc, name, n=2):
        DStream._uid[0] += 1
        u = DStream._uid[0]
        self.sems = [Sem(nc, f"d{u}_{name}{i}") for i in range(n)]
        self.n = n

    def inc(self, instr, i):
        self.sems[i % self.n].inc(instr, 16)
        return instr

    def wait(self, eng, i):
        eng.wait_ge(self.sems[i % self.n].h, 16 * (i // self.n + 1))


class B:

    def __init__(self, nc, block, stack):
        self.nc = nc
        self.block = block
        self.stack = stack
        self._semi = 0
        self.pV = self.sem("pV")
        self.pS = self.sem("pS")
        self.pT = self.sem("pT")
        self.pG = self.sem("pG")

    def sem(self, tag=""):
        self._semi += 1
        return Sem(self.nc, f"s{self._semi}_{tag}")

    def pooled(self, tag):
        if not hasattr(self, "_pool"):
            self._pool = {}
        if tag not in self._pool:
            self._pool[tag] = self.sem(tag)
        return self._pool[tag]

    def sb(self, name, shape, dt, stack=None):
        return (stack or self.stack).enter_context(self.nc.sbuf_tensor(name, shape, dt))

    def fence(self, scratch, dma_sems=()):
        blk = self.block
        pV, pS, pT, pG = self.pV, self.pS, self.pT, self.pG

        @blk.vector
        def _(v):
            v.wait_ge(pV.h, pV.n)
            pV.inc(v.memset(scratch[0:1, 0:1], 0.0))

        @blk.scalar
        def _(s):
            s.wait_ge(pS.h, pS.n)
            pS.inc(s.memzero(scratch[0:1, 1:2]))

        @blk.gpsimd
        def _(g):
            g.wait_ge(pG.h, pG.n)
            pG.inc(g.memset(scratch[0:1, 2:3], 0.0))

        @blk.tensor
        def _(t):
            pT.inc(t.nop())

        marks = [self.pV, self.pS, self.pT, self.pG] + list(dma_sems)
        for mk in marks:
            @blk.vector
            def _(v, mk=mk):
                mk.wait(v)

            @blk.scalar
            def _(s, mk=mk):
                mk.wait(s)

            @blk.tensor
            def _(t, mk=mk):
                mk.wait(t)

            @blk.gpsimd
            def _(g, mk=mk):
                mk.wait(g)

            @blk.sync
            def _(sy, mk=mk):
                mk.wait(sy)


def TT(eng, out, a, bb, op=ALU.mult):
    return eng.tensor_tensor(out=out, in0=a, in1=bb, op=op)


def TSC(eng, out, a, s1, op0, s2=None, op1=ALU.add):
    if s2 is None:
        return eng.tensor_scalar(out=out, in0=a, scalar1=s1, scalar2=None,
                                 op0=op0, op1=ALU.bypass)
    return eng.tensor_scalar(out=out, in0=a, scalar1=s1, scalar2=s2, op0=op0, op1=op1)


def STT(eng, out, a, s, bb, op0=ALU.mult, op1=ALU.add):
    return eng.scalar_tensor_tensor(out=out, in0=a, scalar=s, in1=bb, op0=op0, op1=op1)


def build_program(upto="all", parts="all"):
    nc = bass.Bass("TRN2", target_bir_lowering=False, detect_race_conditions=False)

    def inp(name, shape):
        return nc.dram_tensor(name, shape, F32, kind="ExternalInput").ap()

    x0_h = inp("x0", [TC, D])
    ident_h = inp("ident", [128, 128])
    sel_h = inp("sel32", [64, 32, 128])
    wbase_h = inp("w_base", [D, D])
    bbase_h = inp("b_base", [128, KD])
    wd1_h = inp("wd1", [E, D, H])
    bd1_h = inp("bd1", [E, 128, HC])
    wd2_h = inp("wd2", [E, H, D])
    bd2_h = inp("bd2", [E, 128, KD])
    wad_h = inp("wad", [E, D, RA])
    bad_h = inp("bad", [RA, E])
    wau_h = inp("wau", [E, RA, D])
    bau_h = inp("bau", [E, 128, KD])
    win_h = inp("w_in", [D, DCH])
    wres_h = inp("w_res", [D, DCH])
    wout_h = inp("w_out", [DM, D])
    convw_h = inp("conv_w", [128, DC2, 4])
    convb_h = inp("conv_b", [128, DC2])
    alog_h = inp("a_log", [128, DC2, DS])
    dvec_h = inp("d_vec", [128, DC2])
    wb_h = inp("w_b", [DCH, DS])
    wc_h = inp("w_c", [DCH, DS])
    wdt_h = inp("w_dt", [DCH, DR])
    dtw_h = inp("dt_w", [DR, DCH])
    dtb_h = inp("dt_b", [128, DC2])

    out_h = nc.dram_tensor("out", [TC, D], F32, kind="ExternalOutput").ap()

    def dram(name, shape, dt=F32):
        return nc.dram_tensor(name, shape, dt).ap()

    wd1c_h = [dram(f"wd1c{e}", [128, KD * H], BF16) for e in range(E)]
    wd2c_h = [dram(f"wd2c{e}", [128, HC * D], BF16) for e in range(E)]
    woutc_h = dram("woutc", [128, (DM // 128) * D], BF16)
    u2_in_h = dram("u2_in", [D, TC], BF16)
    u2_out_h = dram("u2_out", [NC, D, TC], BF16)
    seg_in_h = [dram(f"seg_in{s}", [DR + 2 * DS, TS]) for s in range(NSEG)]
    seg_out_h = [dram(f"seg_out{s}", [DR + 2 * DS, TS]) for s in range(NSEG)]
    loss_in_h = [dram(f"loss_in{c}", [1, E]) for c in range(2)]
    loss_out_h = [dram(f"loss_out{c}", [1, E]) for c in range(2)]
    y_in_h = dram("y_in", [NC, DCH, TC], BF16)
    y_out_h = dram("y_out", [NC, DCH, TC], BF16)
    sspill_h = dram("sspill", [128, KD * TC])

    with ExitStack() as stack, nc.Block() as block:
        b = B(nc, block, stack)
        sb = b.sb

        identf = sb("identf", [128, 128], F32)
        identb = sb("identb", [128, 128], BF16)
        negidb = sb("negidb", [128, 128], BF16)
        ones1b = sb("ones1b", [1, 128], BF16)
        ones128f = sb("ones128f", [128, 1], F32)
        bbase = sb("bbase", [128, KD], F32)
        bd1s = sb("bd1s", [128, E * HC], F32)
        bd2s = sb("bd2s", [128, E * KD], F32)
        bads = sb("bads", [RA, E], F32)
        baus = sb("baus", [128, E * KD], F32)
        convb = sb("convbs", [128, DC2], F32)
        dvec = sb("dvecs", [128, DC2], F32)
        dtbn = sb("dtbn", [128, DC2], F32)
        ppos = sb("ppos", [128, DC2 * DS], F32)
        convdiag = sb("convdiag", [128, DC2 * 4, 128], BF16)
        hb05 = sb("hb05", [128, KD], F32)
        lrow = sb("lrow", [1, E], F32)
        onerow = sb("onerow", [1, E], F32)
        maskrow = sb("maskrow", [1, E], F32)
        tiny = sb("tiny", [1, 8], F32)
        fbuf = sb("fbuf", [1, 4], F32)
        ones1f = sb("ones1f", [1, 128], F32)
        win_s = sb("win_s", [128, KD, DCH], BF16)
        wres_s = sb("wres_s", [128, KD, DCH], BF16)
        wdt_s = sb("wdt_s", [128, DC2, DR], BF16)
        dtw_s = sb("dtw_s", [128, DCH], BF16)
        wbneg_s = sb("wbneg_s", [128, DC2, DS], BF16)
        wc_s = sb("wc_s", [128, DC2, DS], BF16)
        hstate = sb("hstate", [128, DC2, DS], F32)
        selb = sb("selb", [64, 32, 128], BF16)
        wbase_s = sb("wbase_s", [128, KD, D], BF16)
        wad_all = sb("wad_all", [128, KD, E, RA], BF16)
        wau_all = sb("wau_all", [RA, E, D], BF16)

        pfr = [stack.enter_context(nc.psum_tensor(f"pfr{i}", [128, F], F32))
               for i in range(2)]
        pY = stack.enter_context(nc.psum_tensor("pY", [128, 2 * F], F32))
        pB = stack.enter_context(nc.psum_tensor("pB", [128, 2 * F], F32))
        pC = stack.enter_context(nc.psum_tensor("pC", [128, 2 * F], F32))
        pb = [pfr[0][:, :], pfr[1][:, :], pY[:, 0:F], pY[:, F : 2 * F],
              pB[:, 0:F], pB[:, F : 2 * F], pC[:, 0:F], pC[:, F : 2 * F]]

        es_stream = ExitStack()
        stream = b.sb("stream", [128, KD, TC], F32, es_stream)

        with ExitStack() as p0:
            stage = b.sb("stage", [128, 8192], F32, p0)
            stgb = b.sb("stgb", [128, 8192], BF16, p0)

            s_c = b.sem("const")

            @block.sync
            def _(sy):
                s_c.incd(sy.dma_start(out=identf[:], in_=ident_h[:]))
                s_c.incd(sy.dma_start(out=bbase[:], in_=bbase_h[:]))
                s_c.incd(sy.dma_start(out=bd1s[:].rearrange("p (e h) -> p e h", e=E), in_=bd1_h.rearrange("e p h -> p e h")))
                s_c.incd(sy.dma_start(out=bd2s[:].rearrange("p (e k) -> p e k", e=E), in_=bd2_h.rearrange("e p k -> p e k")))
                s_c.incd(sy.dma_start(out=bads[:], in_=bad_h[:]))
                s_c.incd(sy.dma_start(out=baus[:].rearrange("p (e k) -> p e k", e=E), in_=bau_h.rearrange("e p k -> p e k")))
                s_c.incd(sy.dma_start(out=convb[:], in_=convb_h[:]))
                s_c.incd(sy.dma_start(out=dvec[:], in_=dvec_h[:]))
                s_c.incd(sy.dma_start(out=dtbn[:], in_=dtb_h[:]))
                s_c.incd(sy.dma_start(out=ppos[:], in_=alog_h.rearrange("p a b -> p (a b)")))
                s_c.incd(sy.dma_start(out=hb05[:, 0:8], in_=convw_h.rearrange("p a b -> p (a b)")))

            s_c0 = b.sem("c0")

            @block.vector
            def _(v):
                s_c.wait(v)
                v.tensor_copy(out=identb[:], in_=identf[:])
                TSC(v, negidb[:], identf[:], -1.0, ALU.mult)
                v.memset(ones1b[:], 1.0)
                v.memset(ones128f[:], 1.0)
                v.memset(ones1f[:], 1.0)
                v.memset(hstate.rearrange("p a b -> p (a b)"), 0.0)
                v.memset(onerow[:], 1.0)
                for dc in range(DC2):
                    for i in range(4):
                        TSC(v, convdiag[:, dc * 4 + i, :], identf[:],
                            hb05[:, dc * 4 + i : dc * 4 + i + 1], ALU.mult)
                s_c0.inc(v.tensor_copy(out=tiny[0:1, 0:2], in_=hb05[0:1, 0:2]))

            @block.scalar
            def _(s):
                s_c.wait(s)
                s_c0.inc(s.activation(out=ppos[:], in_=ppos[:], func=ACT.Exp))
                s.wait_ge(s_c0.h, 2)
                s_c0.inc(s.activation(out=ppos[:], in_=ppos[:],
                                      func=ACT.Identity, scale=-1.0))

            jobs = []

            def wj(src, dst, cache, parts, cols, neg=False):
                assert cols <= 4096
                jobs.append((src, dst, cache, parts, cols, neg))

            for hh in range(2):
                wj(wbase_h[hh * 512 : (hh + 1) * 512].rearrange("(a p) c -> p a c", p=128),
                   wbase_s[:, hh * 4 : (hh + 1) * 4, :].rearrange("p a c -> p (a c)"),
                   None, 128, KD * D // 2)
            wj(win_h.rearrange("(a p) c -> p a c", p=128),
               win_s.rearrange("p a c -> p (a c)"), None, 128, KD * DCH)
            wj(wres_h.rearrange("(a p) c -> p a c", p=128),
               wres_s.rearrange("p a c -> p (a c)"), None, 128, KD * DCH)
            wj(wdt_h.rearrange("(a p) c -> p a c", p=128),
               wdt_s.rearrange("p a c -> p (a c)"), None, 128, DC2 * DR)
            wj(dtw_h[:], dtw_s[:], None, 128, DCH)
            wj(wb_h.rearrange("(a p) c -> p a c", p=128),
               wbneg_s.rearrange("p a c -> p (a c)"), None, 128, DC2 * DS)
            wj(wc_h.rearrange("(a p) c -> p a c", p=128),
               wc_s.rearrange("p a c -> p (a c)"), None, 128, DC2 * DS)
            wj(sel_h.rearrange("k n p -> k (n p)"),
               selb.rearrange("k n p -> k (n p)"), None, 64, 32 * 128)
            for e in range(E):
                wj(wad_h[e].rearrange("(a p) r -> p a r", p=128),
                   wad_all[:, :, e, :], None, 128, KD * RA)
                wj(wau_h[e][:], wau_all[:, e, :], None, RA, D)
                wj(wd1_h[e].rearrange("(a p) c -> p a c", p=128), None,
                   wd1c_h[e][:], 128, KD * H)
                wj(wd2_h[e].rearrange("(a p) c -> p a c", p=128), None,
                   wd2c_h[e][:], 128, HC * D)
            for qq in range(4):
                wj(wout_h[qq * 512 : (qq + 1) * 512].rearrange("(a p) c -> p a c", p=128),
                   None, woutc_h[:, qq * 4 * D : (qq + 1) * 4 * D], 128, 4 * D)

            s_wl = DStream(nc, "wload")
            s_wc = b.sem("wcast")
            s_ws = DStream(nc, "wstore")
            ncache = sum(1 for j in jobs if j[2] is not None)

            @block.sync
            def _(sy):
                for i, (src, dst, cache, parts, cols, neg) in enumerate(jobs):
                    if i >= 2:
                        sy.wait_ge(s_wc.h, i - 1)
                    half = (i % 2) * 4096
                    dst_st = stage[0:parts, half : half + cols]
                    if len(src.shape) == 3:
                        dst_st = dst_st.rearrange("p (a c) -> p a c", c=src.shape[2])
                    s_wl.inc(sy.dma_start(out=dst_st, in_=src), i)

            cache_seen = [0]

            @block.vector
            def _(v):
                nstore = 0
                for i, (src, dst, cache, parts, cols, neg) in enumerate(jobs):
                    half = (i % 2) * 4096
                    s_wl.wait(v, i)
                    if cache is not None:
                        if nstore >= 2:
                            s_ws.wait(v, nstore - 2)
                        tgt = stgb[0:parts, (nstore % 2) * 4096 : (nstore % 2) * 4096 + cols]
                        nstore += 1
                    else:
                        tgt = dst
                    if neg:
                        s_wc.inc(TSC(v, tgt, stage[0:parts, half : half + cols], -1.0, ALU.mult))
                    else:
                        s_wc.inc(v.tensor_copy(out=tgt, in_=stage[0:parts, half : half + cols]))

            @block.scalar
            def _(s):
                nstore = 0
                for i, (src, dst, cache, parts, cols, neg) in enumerate(jobs):
                    if cache is None:
                        continue
                    s.wait_ge(s_wc.h, i + 1)
                    src_b = stgb[0:parts, (nstore % 2) * 4096 : (nstore % 2) * 4096 + cols]
                    s_ws.inc(s.dma_start(out=cache, in_=src_b), nstore)
                    nstore += 1

            s_si = DStream(nc, "sin")
            s_tr = b.sem("trp")
            s_se = b.sem("sev")

            @block.sync
            def _(sy):
                sy.wait_ge(s_wc.h, len(jobs))
                for tt in range(TC // 128):
                    half = (tt % 2) * 1024
                    if tt >= 2:
                        sy.wait_ge(s_tr.h, (tt - 1) * KD)
                    s_si.inc(sy.dma_start(out=stage[:, half : half + 1024],
                                          in_=x0_h[tt * 128 : (tt + 1) * 128, :]), tt)

            @block.tensor
            def _(t):
                for tt in range(TC // 128):
                    s_si.wait(t, tt)
                    half = (tt % 2) * 1024
                    for kc in range(KD):
                        j = tt * KD + kc
                        if j >= 2:
                            t.wait_ge(s_se.h, j - 1)
                        s_tr.inc(t.transpose(pb[j % 2][:, 0:128],
                                             stage[:, half + kc * 128 : half + (kc + 1) * 128],
                                             identf[:]))

            @block.vector
            def _(v):
                for tt in range(TC // 128):
                    for kc in range(KD):
                        j = tt * KD + kc
                        v.wait_ge(s_tr.h, j + 1)
                        s_se.inc(v.tensor_copy(
                            out=stream[:, kc, tt * 128 : (tt + 1) * 128],
                            in_=pb[j % 2][:, 0:128]))

            b.fence(fbuf, dma_sems=s_ws.sems + s_si.sems)

        def emit_ln(uout, tag, lp=4):
            s_sq = b.pooled("lnsq")
            s_mm = b.pooled("lnmm")
            s_vs = b.pooled("lnvs")
            s_ro = b.pooled("lnro")
            s_bc = b.pooled("lnbc")
            s_ap = b.pooled("lnap")
            b_sq, b_mm, b_vs = s_sq.n, s_mm.n, s_vs.n
            b_ro, b_bc, b_ap = s_ro.n, s_bc.n, s_ap.n
            with ExitStack() as px:
                lnsq = b.sb(f"ln{tag}s", [128, KD, F], F32, px)
                lnm = b.sb(f"ln{tag}m", [1, TC], F32, px)
                lnr = b.sb(f"ln{tag}r", [1, TC], F32, px)
                lnap = b.sb(f"ln{tag}a", [128, F], F32, px)
                lnq = b.sb(f"ln{tag}q", [1, F], F32, px)
                lnsc = b.sb(f"ln{tag}c", [1, F], F32, px)

                @block.scalar
                def _(sc):
                    for tt in range(NT + 1):
                        if tt < NT:
                            if tt >= 1:
                                sc.wait_ge(s_mm.h, b_mm + 2 * tt)
                            ss = None
                            for kc in range(KD):
                                ss = sc.activation(out=lnsq[:, kc, :],
                                                   in_=stream[:, kc, tt * F : (tt + 1) * F],
                                                   func=ACT.Square)
                            s_sq.inc(ss)
                        if tt >= 1 and lp >= 2:
                            u = tt - 1
                            sc.wait_ge(s_vs.h, b_vs + u + 1)
                            sl = slice(u * F, (u + 1) * F)
                            sc.activation(out=lnr[0:1, sl], in_=lnq[0:1, :], func=ACT.Ln)
                            sc.drain()
                            s_ro.inc(sc.activation(out=lnr[0:1, sl], in_=lnr[0:1, sl],
                                                   func=ACT.Exp, scale=-0.5))

                @block.tensor
                def _(t):
                    for tt in range(NT + 1):
                        if tt < NT:
                            if tt >= 1:
                                t.wait_ge(s_vs.h, b_vs + tt)
                            for kc in range(KD):
                                t.matmul(pb[6][0:1, 0:F], ones128f[:],
                                         stream[:, kc, tt * F : (tt + 1) * F],
                                         start=(kc == 0), stop=(kc == KD - 1))
                            t.wait_ge(s_sq.h, b_sq + tt + 1)
                            mm = None
                            for kc in range(KD):
                                mm = t.matmul(pb[7][0:1, 0:F], ones128f[:], lnsq[:, kc, :],
                                              start=(kc == 0), stop=(kc == KD - 1))
                            s_mm.inc(mm, 2)
                        if tt >= 1 and lp >= 3:
                            u = tt - 1
                            t.wait_ge(s_ro.h, b_ro + u + 1)
                            if u >= 1:
                                t.wait_ge(s_ap.h, b_ap + u * KD)
                            t.matmul(pb[4][:, 0:F], ones1f[:],
                                     lnm[0:1, u * F : (u + 1) * F], start=True, stop=True)
                            s_bc.inc(t.matmul(pb[5][:, 0:F], ones1f[:],
                                              lnr[0:1, u * F : (u + 1) * F],
                                              start=True, stop=True))

                @block.vector
                def _(v):
                    for tt in range(NT + 1):
                        if tt < NT:
                            v.wait_ge(s_mm.h, b_mm + 2 * (tt + 1))
                            if tt >= 1:
                                v.wait_ge(s_ro.h, b_ro + tt)
                            sl = slice(tt * F, (tt + 1) * F)
                            TSC(v, lnm[0:1, sl], pb[6][0:1, 0:F], 1.0 / D, ALU.mult)
                            TSC(v, lnq[0:1, :], pb[7][0:1, 0:F], 1.0 / D, ALU.mult)
                            v.drain()
                            TT(v, lnsc[0:1, :], lnm[0:1, sl], lnm[0:1, sl])
                            v.drain()
                            TT(v, lnq[0:1, :], lnq[0:1, :], lnsc[0:1, :], ALU.subtract)
                            v.drain()
                            s_vs.inc(TSC(v, lnq[0:1, :], lnq[0:1, :], EPS, ALU.add))
                        if tt >= 1 and lp >= 4:
                            u = tt - 1
                            v.wait_ge(s_bc.h, b_bc + u + 1)
                            sl = slice(u * F, (u + 1) * F)
                            for kc in range(KD):
                                STT(v, lnap[:], stream[:, kc, sl], 1.0,
                                    pb[4][:, 0:F], ALU.mult, ALU.subtract)
                                v.drain()
                                s_ap.inc(TT(v, uout[:, kc, sl], lnap[:], pb[5][:, 0:F]))
                                v.drain()

                b.fence(fbuf)

        def emit_ffn(uin, call_idx, parts="all"):
            P = {"experts": 1, "loss": 2, "base": 3, "all": 4}[parts]
            s_w = DStream(nc, "fw", 4)
            s_h = b.pooled("fh")
            s_he = b.pooled("fhe")
            s_r = b.pooled("fr")
            s_sqd = b.pooled("fsq")
            s_ls = b.pooled("fls")
            s_ccd = b.pooled("fcc")
            s_msk = b.pooled("fmsk")
            s_sel = b.pooled("fsel")
            s_bmm = b.pooled("fbmm")
            s_bev = b.pooled("fbev")
            s_amm = b.pooled("famm")
            s_aev = b.pooled("faev")
            s_omm = b.pooled("fomm")
            s_oev = b.pooled("foev")
            b_h, b_he, b_r, b_sqd, b_ls = s_h.n, s_he.n, s_r.n, s_sqd.n, s_ls.n
            b_msk, b_bmm, b_bev = s_msk.n, s_bmm.n, s_bev.n
            b_amm, b_aev, b_omm, b_oev = s_amm.n, s_aev.n, s_omm.n, s_oev.n

            with ExitStack() as px:
                wexp = [b.sb(f"f{call_idx}wex{i}", [128, KD * H + HC * D], BF16, px) for i in range(2)]
                hbuf = b.sb(f"f{call_idx}hbuf", [128, HC, TC], BF16, px)
                abuf = b.sb(f"f{call_idx}abuf", [RA, TC], BF16, px)
                accb = b.sb(f"f{call_idx}accb", [128, E * KD * NT], F32, px)
                lcol = b.sb(f"f{call_idx}lcol", [128, E], F32, px)
                wadsel = b.sb(f"f{call_idx}wadsel", [128, KD, RA], BF16, px)
                wausel = b.sb(f"f{call_idx}wausel", [RA, D], BF16, px)
                badsel = b.sb(f"f{call_idx}badsel", [RA, 1], F32, px)

                @block.sync
                def _(sy):
                    for e in range(E):
                        if e >= 2:
                            sy.wait_ge(s_r.h, b_r + (e - 1) * KD * NT)
                        w = wexp[e % 2]
                        s_w.inc(sy.dma_start(out=w[:, 0 : KD * H], in_=wd1c_h[e][:]), 2 * e)
                        s_w.inc(sy.dma_start(out=w[:, KD * H :], in_=wd2c_h[e][:]), 2 * e + 1)

                @block.tensor
                def _(t):
                    for e in range(E):
                        s_w.wait(t, 2 * e)
                        s_w.wait(t, 2 * e + 1)
                        w = wexp[e % 2]
                        wd1v = w[:, 0 : KD * H].rearrange("p (a c) -> p a c", c=H)
                        wd2v = w[:, KD * H :].rearrange("p (a c) -> p a c", c=D)
                        for hc in range(HC):
                            for tt in range(NT):
                                j = hc * NT + tt
                                bank = pb[j % 2]
                                jj = e * HC * NT + j
                                if jj >= 2:
                                    t.wait_ge(s_he.h, b_he + jj - 1)
                                mm = None
                                for kc in range(KD):
                                    mm = t.matmul(bank[:, 0:F],
                                                  wd1v[:, kc, hc * 128 : (hc + 1) * 128],
                                                  uin[:, kc, tt * F : (tt + 1) * F],
                                                  start=(kc == 0), stop=(kc == KD - 1))
                                s_h.inc(mm)
                        t.wait_ge(s_he.h, b_he + (e + 1) * HC * NT)
                        for pc in range(KD):
                            for tt in range(NT):
                                j = pc * NT + tt
                                bank = pb[2 + j % 2]
                                jj = e * KD * NT + j
                                if jj >= 2:
                                    t.wait_ge(s_sqd.h, b_sqd + jj - 1)
                                for hc in range(HC):
                                    t.matmul(bank[:, 0:F],
                                             wd2v[:, hc, pc * 128 : (pc + 1) * 128],
                                             hbuf[:, hc, tt * F : (tt + 1) * F],
                                             start=(hc == 0), stop=False)
                                s_r.inc(t.matmul(bank[:, 0:F], negidb[:],
                                                 uin[:, pc, tt * F : (tt + 1) * F],
                                                 start=False, stop=True))

                @block.scalar
                def _(s):
                    for e in range(E):
                        for hc in range(HC):
                            for tt in range(NT):
                                j = hc * NT + tt
                                s.wait_ge(s_h.h, b_h + e * HC * NT + j + 1)
                                bank = pb[j % 2]
                                s_he.inc(s.activation(
                                    out=hbuf[:, hc, tt * F : (tt + 1) * F],
                                    in_=bank[:, 0:F], func=ACT.Relu,
                                    bias=bd1s[:, e * HC + hc : e * HC + hc + 1]))
                        for pc in range(KD):
                            for tt in range(NT):
                                j = pc * NT + tt
                                s.wait_ge(s_r.h, b_r + e * KD * NT + j + 1)
                                bank = pb[2 + j % 2]
                                s_sqd.inc(s.activation(
                                    out=bank[:, 0:F], in_=bank[:, 0:F], func=ACT.Square,
                                    bias=bd2s[:, e * KD + pc : e * KD + pc + 1],
                                    accum_out=accb[:, e * KD * NT + j : e * KD * NT + j + 1]))

                @block.tensor
                def _(t):
                    if P < 3:
                        return
                    for pc in range(KD):
                        for tt in range(NT):
                            j = pc * NT + tt
                            bank = pb[4 + j % 2]
                            if j >= 2:
                                t.wait_ge(s_bev.h, b_bev + j - 1)
                            mm = None
                            for kc in range(KD):
                                mm = t.matmul(bank[:, 0:F],
                                              wbase_s[:, kc, pc * 128 : (pc + 1) * 128],
                                              uin[:, kc, tt * F : (tt + 1) * F],
                                              start=(kc == 0), stop=(kc == KD - 1))
                            s_bmm.inc(mm)

                @block.vector
                def _(v):
                    if P < 3:
                        return
                    for pc in range(KD):
                        for tt in range(NT):
                            j = pc * NT + tt
                            v.wait_ge(s_bmm.h, b_bmm + j + 1)
                            bank = pb[4 + j % 2]
                            sl = slice(tt * F, (tt + 1) * F)
                            s_bev.inc(STT(v, stream[:, pc, sl], bank[:, 0:F], 0.5,
                                          stream[:, pc, sl]))

                @block.vector
                def _(v):
                    if P < 2:
                        return
                    for e in range(E):
                        v.wait_ge(s_sqd.h, b_sqd + (e + 1) * KD * NT)
                        s_ls.inc(v.tensor_reduce(out=lcol[:, e : e + 1],
                                                 in_=accb[:, e * KD * NT : (e + 1) * KD * NT], axis=AX.X, op=ALU.add))

                @block.tensor
                def _(t):
                    if P < 2:
                        return
                    t.wait_ge(s_ls.h, b_ls + E)
                    s_ls.inc(t.matmul(pb[6][0:1, 0:E], ones128f[:], lcol[:, :],
                                      start=True, stop=True))

                @block.vector
                def _(v):
                    if P < 2:
                        return
                    v.wait_ge(s_ls.h, b_ls + E + 1)
                    s_ls.inc(v.tensor_copy(out=lrow[:], in_=pb[6][0:1, 0:E]))

                s_cd1 = b.pooled("fcd1")
                s_cd2 = b.pooled("fcd2")
                b_cd1, b_ccd2 = s_cd1.n, s_ccd.n

                @block.gpsimd
                def _(g):
                    if P < 2:
                        return
                    g.wait_ge(s_ls.h, b_ls + E + 2)
                    s_cd1.incd(g.dma_start(out=loss_in_h[call_idx][:], in_=lrow[:]))
                    g.wait_ge(s_cd1.h, b_cd1 + 16)
                    s_ccd.inc(g.collective_compute(
                        "AllReduce", ALU.add, replica_groups=[list(range(NC))],
                        ins=[loss_in_h[call_idx].opt()],
                        outs=[loss_out_h[call_idx].opt()]))
                    g.wait_ge(s_ccd.h, b_ccd2 + 1)
                    s_cd2.incd(g.dma_start(out=lrow[:], in_=loss_out_h[call_idx][:]))

                @block.vector
                def _(v):
                    if P < 2:
                        return
                    s_cd2.wait(v)
                    s_msk.inc(v.tensor_reduce(out=tiny[0:1, 0:1], in_=lrow[:],
                                              axis=AX.X, op=ALU.min))
                    s_msk.wait(v)
                    TSC(v, maskrow[:], lrow[:], tiny[0:1, 0:1], ALU.is_equal)
                    v.drain()
                    s_msk.inc(v.tensor_tensor_scan(out=tiny[0:1, 0:E], data0=onerow[:],
                                                   data1=maskrow[:], initial=0.0,
                                                   op0=ALU.mult, op1=ALU.add))
                    s_msk.wait(v)
                    TSC(v, tiny[0:1, 0:E], tiny[0:1, 0:E], 1.0, ALU.is_equal)
                    v.drain()
                    s_msk.inc(TT(v, maskrow[:], maskrow[:], tiny[0:1, 0:E]))

                @block.tensor
                def _(t):
                    if P < 2:
                        return
                    s_msk.wait(t)
                    s_msk.inc(t.matmul(pb[6][:, 0:E], ones1f[:], maskrow[0:1, :],
                                       start=True, stop=True))

                @block.vector
                def _(v):
                    if P < 4:
                        return
                    s_msk.wait(v)
                    for e in range(E):
                        if e > 0:
                            v.drain()
                        sc = pb[6][:, e : e + 1]
                        scr = pb[6][0:RA, e : e + 1]
                        wadv = wadsel[:, :, :]
                        wallv = wad_all[:, :, e, :]
                        if e == 0:
                            TSC(v, wadv, wallv, sc, ALU.mult)
                            TSC(v, wausel[:], wau_all[:, e, :], scr, ALU.mult)
                            TSC(v, badsel[:], bads[:, e : e + 1], scr, ALU.mult)
                            TSC(v, hb05[:], baus[:, e * KD : (e + 1) * KD], sc, ALU.mult)
                        else:
                            STT(v, wadv, wallv, sc, wadv)
                            STT(v, wausel[:], wau_all[:, e, :], scr, wausel[:])
                            STT(v, badsel[:], bads[:, e : e + 1], scr, badsel[:])
                            STT(v, hb05[:], baus[:, e * KD : (e + 1) * KD], sc, hb05[:])
                    v.drain()
                    STT(v, hb05[:], bbase[:], 1.0, hb05[:], ALU.mult, ALU.add)
                    v.drain()
                    s_sel.inc(TSC(v, hb05[:], hb05[:], 0.5, ALU.mult))

                @block.tensor
                def _(t):
                    if P < 4:
                        return
                    s_sel.wait(t)
                    for tt in range(NT):
                        bank = pb[0 + tt % 2]
                        if tt >= 2:
                            t.wait_ge(s_aev.h, b_aev + tt - 1)
                        mm = None
                        for kc in range(KD):
                            mm = t.matmul(bank[0:RA, 0:F], wadsel[:, kc, :],
                                          uin[:, kc, tt * F : (tt + 1) * F],
                                          start=(kc == 0), stop=(kc == KD - 1))
                        s_amm.inc(mm)

                @block.scalar
                def _(s):
                    if P < 4:
                        return
                    for tt in range(NT):
                        s.wait_ge(s_amm.h, b_amm + tt + 1)
                        bank = pb[0 + tt % 2]
                        s_aev.inc(s.activation(out=abuf[:, tt * F : (tt + 1) * F],
                                               in_=bank[0:RA, 0:F], func=ACT.Relu,
                                               bias=badsel[:]))

                @block.tensor
                def _(t):
                    if P < 4:
                        return
                    for pc in range(KD):
                        for tt in range(NT):
                            j = pc * NT + tt
                            bank = pb[2 + j % 2]
                            t.wait_ge(s_aev.h, b_aev + tt + 1)
                            if j >= 2:
                                t.wait_ge(s_oev.h, b_oev + j - 1)
                            s_omm.inc(t.matmul(bank[:, 0:F],
                                               wausel[:, pc * 128 : (pc + 1) * 128],
                                               abuf[:, tt * F : (tt + 1) * F],
                                               start=True, stop=True))

                @block.vector
                def _(v):
                    if P < 4:
                        return
                    s_bev.wait(v)
                    for pc in range(KD):
                        for tt in range(NT):
                            j = pc * NT + tt
                            v.wait_ge(s_omm.h, b_omm + j + 1)
                            bank = pb[2 + j % 2]
                            sl = slice(tt * F, (tt + 1) * F)
                            STT(v, stream[:, pc, sl], bank[:, 0:F], 0.5, stream[:, pc, sl])
                            v.drain()
                            s_oev.inc(TSC(v, stream[:, pc, sl], stream[:, pc, sl],
                                          hb05[:, pc : pc + 1], ALU.add))

                b.fence(fbuf)

        with ExitStack() as pf:
            u1f = b.sb("u1f", [128, KD, TC], BF16, pf)
            if upto != "p0":
                lnlvl = {"lnA": 1, "lnB": 2, "lnC": 3}.get(upto, 4)
                emit_ln(u1f, "1", lp=lnlvl)
            if upto not in ("ln1", "p0", "lnA", "lnB", "lnC"):
                emit_ffn(u1f, 0, parts=("all" if upto in ("ffn1", "all") else upto))

        if upto in ("ffn1", "ln1", "experts", "loss", "base", "p0", "lnA", "lnB", "lnC"):
            emit_store(nc, block, b, stream, pb, identf, out_h, fbuf)
            es_stream.close()
            return nc

        s_ag = b.sem("ag")
        with ExitStack() as p3:
            u2f = b.sb("u2f", [128, KD, TC], BF16, p3)
            emit_ln(u2f, "2")
            s_u2d = b.sem("u2d")
            s_spl = b.sem("spl")

            @block.sync
            def _(sy):
                s_u2d.incd(sy.dma_start(out=u2_in_h.rearrange("(a p) t -> p a t", p=128),
                                        in_=u2f[:, :, :]))
                s_spl.incd(sy.dma_start(out=sspill_h[:],
                                        in_=stream.rearrange("p a t -> p (a t)")))

            @block.gpsimd
            def _(g):
                g.wait_ge(s_u2d.h, 16)
                s_ag.inc(g.collective_compute(
                    "AllGather", ALU.bypass, replica_groups=[list(range(NC))],
                    ins=[u2_in_h.opt()], outs=[u2_out_h.opt()]))

            b.fence(fbuf, dma_sems=[s_spl, s_u2d])
        es_stream.close()

        with ExitStack() as p4:
            u2t = [b.sb(f"u2t{i}", [128, KD, TS], BF16, p4) for i in range(2)]
            xcv = [b.sb(f"xcq{i}", [128, DC2, TS], F32, p4) for i in range(2)]
            xcb = b.sb("xcbq", [128, DC2, TS], BF16, p4)
            srv = [b.sb(f"srq{i}", [128, DC2, TS], F32, p4) for i in range(2)]
            xconv = [b.sb(f"xcvq{i}", [128, DC2, 3 + TS], BF16, p4) for i in range(2)]
            dpre = [b.sb(f"dpreq{i}", [128, TS], F32, p4) for i in range(2)]
            dpreb = b.sb("dprebq", [128, TS], BF16, p4)
            bcstage = [b.sb(f"bcstq{i}", [64, TS], F32, p4) for i in range(2)]
            bcb = [b.sb(f"bcbq{i}", [64, TS], BF16, p4) for i in range(2)]
            dnegv = b.sb("dnegv", [128, DC2, TS], F32, p4)
            dxn = b.sb("dxnq", [128, DC2, TS], F32, p4)
            scan_a = b.sb("scana", [128, 2, TS], F32, p4)
            scan_u = b.sb("scanu", [128, TS], F32, p4)
            scan_h = b.sb("scanh", [128, TS], F32, p4)
            scan_g = b.sb("scang", [128, 2, TS], BF16, p4)
            ytmp = b.sb("ytmp", [128, F], F32, p4)
            ybuf = b.sb("ybufq", [128, DC2, TS], BF16, p4)

            s_bcz = b.sem("bcz")

            @block.vector
            def _(v):
                for i in range(2):
                    s_bcz.inc(v.memset(bcb[i][0:64, :], 0.0))

            s_u2l = DStream(nc, "u2l")
            s_fm = b.sem("fm")
            s_fe = b.sem("fe")
            s_cb = b.sem("cb")
            s_gd = DStream(nc, "gd", n=4)
            s_ar = b.sem("ar")
            s_ld = DStream(nc, "arld", n=4)
            s_dc = b.sem("dcst")
            s_zm = b.sem("zm")
            s_ss = b.sem("sself")
            s_vs2 = b.sem("vself")
            s_bb = b.sem("bb")
            s_bc2 = b.sem("bc2")
            s_ze = b.sem("ze")
            s_dx = b.sem("dx")
            s_ae = b.sem("ae")
            s_ub = b.sem("ub")
            s_sc = b.sem("sc")
            s_hs = b.sem("hs")
            s_cm = b.sem("cm")
            s_ir = b.sem("ir")
            s_yf = b.sem("yf")
            s_ys = DStream(nc, "ys")
            s_sd = b.sem("sd")

            fm_after_res = {}
            fm_c = [0]
            fe_c = [0]

            def front_sync(s):
                bq = s % 2

                @block.sync
                def _(sy):
                    s_ag.wait(sy)
                    if s >= 2:
                        sy.wait_ge(s_fm.h, 18 * (s - 2) + 8)
                    src = u2_out_h[s // 2].rearrange("(a p) t -> p a t", p=128)
                    s_u2l.inc(sy.dma_start(
                        out=u2t[bq][:, :, :],
                        in_=src[:, :, (s % 2) * TS : (s % 2) * TS + TS]), s)

            FJOBS = []
            for dc_ in range(DC2):
                for tt_ in range(2):
                    FJOBS.append(("res", dc_, tt_))
            for dc_ in range(DC2):
                for tt_ in range(2):
                    FJOBS.append(("x", dc_, tt_))
            for dc_ in range(DC2):
                for tt_ in range(2):
                    FJOBS.append(("conv", dc_, tt_))
            FJOBS += [("B", None, 0), ("B", None, 1), ("C", None, 0), ("C", None, 1),
                      ("dpre", None, 0), ("dpre", None, 1)]
            ESCHED = {1: [0], 2: [1], 3: [2], 4: [3], 5: [4], 6: [5], 7: [6],
                      8: [7], 11: [8, 9], 13: [10, 11], 14: [12], 15: [13],
                      16: [14], 17: [15], 18: [16], 19: [17]}

            def emit_front_job(t, s, j):
                bq = s % 2
                kind, dc, tt = FJOBS[j]
                bank = pb[j % 2]
                fm0 = 18 * s
                gj = fm0 + j
                if gj >= 2:
                    t.wait_ge(s_fe.h, gj - 1)
                if kind == "x":
                    mm = None
                    for kc in range(KD):
                        mm = t.matmul(bank[:, 0:F],
                                      win_s[:, kc, dc * 128 : (dc + 1) * 128],
                                      u2t[bq][:, kc, tt * F : (tt + 1) * F],
                                      start=(kc == 0), stop=(kc == KD - 1))
                elif kind == "res":
                    if j == 0:
                        s_u2l.wait(t, s)
                    mm = None
                    for kc in range(KD):
                        mm = t.matmul(bank[:, 0:F],
                                      wres_s[:, kc, dc * 128 : (dc + 1) * 128],
                                      u2t[bq][:, kc, tt * F : (tt + 1) * F],
                                      start=(kc == 0), stop=(kc == KD - 1))
                elif kind == "conv":
                    t.wait_ge(s_fe.h, fm0 + 4 + dc * 2 + 2)
                    mm = None
                    for i4 in range(4):
                        mm = t.matmul(bank[:, 0:F],
                                      convdiag[:, dc * 4 + i4, :],
                                      xconv[bq][:, dc, tt * F + i4 : tt * F + i4 + F],
                                      start=(i4 == 0), stop=(i4 == 3))
                elif kind == "B":
                    t.wait_ge(s_cb.h, 2 * s + 2)
                    mm = None
                    for dcc in range(DC2):
                        mm = t.matmul(bank[0:DS, 0:F],
                                      wbneg_s[:, dcc, :],
                                      xcb[:, dcc, tt * F : (tt + 1) * F],
                                      start=(dcc == 0), stop=(dcc == DC2 - 1))
                elif kind == "C":
                    mm = None
                    for dcc in range(DC2):
                        mm = t.matmul(bank[0:DS, 0:F],
                                      wc_s[:, dcc, :],
                                      xcb[:, dcc, tt * F : (tt + 1) * F],
                                      start=(dcc == 0), stop=(dcc == DC2 - 1))
                elif kind == "dpre":
                    mm = None
                    for dcc in range(DC2):
                        mm = t.matmul(bank[:, 0:F],
                                      wdt_s[:, dcc, :],
                                      xcb[:, dcc, tt * F : (tt + 1) * F],
                                      start=(dcc == 0), stop=(dcc == DC2 - 1))
                s_fm.inc(mm)

            def emit_front_evac(sc, s, j):
                bq = s % 2
                kind, dc, tt = FJOBS[j]
                fm0 = 18 * s
                sc.wait_ge(s_fm.h, fm0 + j + 1)
                bank = pb[j % 2]
                if kind == "x":
                    ev = sc.activation(out=xconv[bq][:, dc, 3 + tt * F : 3 + (tt + 1) * F],
                                       in_=bank[:, 0:F], func=ACT.Identity)
                elif kind == "res":
                    ev = sc.activation(out=srv[bq][:, dc, tt * F : (tt + 1) * F],
                                       in_=bank[:, 0:F], func=ACT.Identity)
                elif kind == "conv":
                    ev = sc.activation(out=xcv[bq][:, dc, tt * F : (tt + 1) * F],
                                       in_=bank[:, 0:F], func=ACT.Silu,
                                       bias=convb[:, dc : dc + 1])
                elif kind == "B":
                    ev = sc.activation(out=bcstage[bq][0:DS, tt * F : (tt + 1) * F],
                                       in_=bank[0:DS, 0:F], func=ACT.Identity)
                elif kind == "C":
                    ev = sc.activation(out=bcstage[bq][32 : 32 + DS, tt * F : (tt + 1) * F],
                                       in_=bank[0:DS, 0:F], func=ACT.Identity)
                elif kind == "dpre":
                    ev = sc.activation(out=dpre[bq][:, tt * F : (tt + 1) * F],
                                       in_=bank[:, 0:F], func=ACT.Identity)
                s_fe.inc(ev)
                if kind == "conv" and tt == 1:
                    sc.wait_ge(s_fe.h, fm0 + j + 1)
                    cv = sc.activation(out=xcb[:, dc, :], in_=xcv[bq][:, dc, :],
                                       func=ACT.Identity)
                    s_cb.inc(cv)

            def front_g(s):
                bq = s % 2

                @block.gpsimd
                def _(g):
                    g.wait_ge(s_fe.h, 18 * (s + 1))
                    s_gd.inc(g.dma_start(out=seg_in_h[s][0:DR, :], in_=dpre[bq][:]), 4 * s)
                    s_gd.inc(g.dma_start(out=seg_in_h[s][DR : DR + DS, :],
                                         in_=bcstage[bq][0:DS, :]), 4 * s + 1)
                    s_gd.inc(g.dma_start(out=seg_in_h[s][DR + DS : DR + 2 * DS, :],
                                         in_=bcstage[bq][32 : 32 + DS, :]), 4 * s + 2)
                    s_gd.wait(g, 4 * s)
                    s_gd.wait(g, 4 * s + 1)
                    s_gd.wait(g, 4 * s + 2)
                    s_ar.inc(g.collective_compute(
                        "AllReduce", ALU.add, replica_groups=[list(range(NC))],
                        ins=[seg_in_h[s].opt()], outs=[seg_out_h[s].opt()]))
                    g.wait_ge(s_ar.h, s + 1)
                    s_ld.inc(g.dma_start(out=dpre[bq][:], in_=seg_out_h[s][0:DR, :]), 4 * s)
                    s_ld.inc(g.dma_start(out=bcstage[bq][0:DS, :],
                                         in_=seg_out_h[s][DR : DR + DS, :]), 4 * s + 1)
                    s_ld.inc(g.dma_start(out=bcstage[bq][32 : 32 + DS, :],
                                         in_=seg_out_h[s][DR + DS : DR + 2 * DS, :]), 4 * s + 2)

            def emit_z_mms(t, s):
                t.wait_ge(s_dc.h, 2 * (s + 1))
                if s >= 1:
                    t.wait_ge(s_ub.h, 32 * s)
                for j in range(4):
                    dc, tt = j // 2, j % 2
                    bank = pb[4 + j % 2]
                    if j >= 2:
                        t.wait_ge(s_ze.h, 4 * s + j - 1)
                    mm = t.matmul(bank[:, 0:F], dtw_s[:, dc * 128 : (dc + 1) * 128],
                                  dpreb[:, tt * F : (tt + 1) * F],
                                  start=True, stop=True)
                    s_zm.inc(mm)

            def emit_z_chain_s(sc, s):
                bq = s % 2
                if s <= 1:
                    sc.wait_ge(s_bcz.h, 2)
                s_ld.wait(sc, 4 * s)
                s_ld.wait(sc, 4 * s + 1)
                s_ld.wait(sc, 4 * s + 2)
                sc.activation(out=dpreb[:], in_=dpre[bq][:], func=ACT.Identity)
                sc.activation(out=bcb[bq][0:DS, :], in_=bcstage[bq][0:DS, :],
                              func=ACT.Identity)
                s_dc.inc(sc.activation(out=bcb[bq][32 : 32 + DS, :],
                                       in_=bcstage[bq][32 : 32 + DS, :],
                                       func=ACT.Identity), 2)
                for j in range(4):
                    dc, tt = j // 2, j % 2
                    sc.wait_ge(s_zm.h, 4 * s + j + 1)
                    bank = pb[4 + j % 2]
                    sg = sc.activation(out=bank[:, 0:F], in_=bank[:, 0:F],
                                       func=ACT.Exp, bias=dtbn[:, dc : dc + 1])
                    s_ss.inc(sg)
                    sc.wait_ge(s_ss.h, s_ss.n)
                    ev = sc.activation(out=dnegv[:, dc, tt * F : (tt + 1) * F],
                                       in_=bank[:, 0:F], func=ACT.Ln, bias=1.0)
                    s_ze.inc(ev)

            def scan_te(s, with_front):
                bq = s % 2
                fq = list(range(18)) if with_front else []

                def ir_group(t, l):
                    dcl, n = l // DS, l % DS
                    li = (s * DC2 + dcl) * DS + n
                    t.wait_ge(s_cm.h, li + 1)
                    if n == 0:
                        if dcl == 0:
                            t.wait_ge(s_ze.h, 4 * (s + 1))
                        else:
                            t.wait_ge(s_yf.h, 4 * s + 2)
                    mm = None
                    for c in range(2):
                        mm = t.matmul(pb[2 + c][:, 0:F], identb[:],
                                      scan_g[:, dcl, c * F : (c + 1) * F],
                                      start=(n == 0), stop=(n == DS - 1))
                    s_ir.inc(mm)

                @block.tensor
                def _(t):
                    for l in range(DC2 * DS):
                        dcl, n = l // DS, l % DS
                        li = (s * DC2 + dcl) * DS + n
                        if l == 0:
                            t.wait_ge(s_ze.h, 4 * s + 4)
                        if li >= 1:
                            t.wait_ge(s_ub.h, li)
                        for c in range(2):
                            mm = t.matmul(pb[4 + c][:, 0:F], selb[:, n, :],
                                          bcb[bq][:, c * F : (c + 1) * F],
                                          start=True, stop=True)
                        s_bb.inc(mm)
                        if l >= 1:
                            ir_group(t, l - 1)
                        if li >= 1:
                            t.wait_ge(s_cm.h, li)
                        for c in range(2):
                            mm = t.matmul(pb[6 + c][:, 0:F], selb[:, DS + n, :],
                                          bcb[bq][:, c * F : (c + 1) * F],
                                          start=True, stop=True)
                        s_bc2.inc(mm)
                        if fq:
                            emit_front_job(t, s + 1, fq.pop(0))
                    ir_group(t, DC2 * DS - 1)
                    while fq:
                        emit_front_job(t, s + 1, fq.pop(0))
                    if s + 1 < NSEG:
                        emit_z_mms(t, s + 1)

            def scan_s(s, with_front):
                bq = s % 2

                @block.scalar
                def _(sc):
                    if with_front:
                        bq1 = (s + 1) % 2
                        if s + 1 >= 2:
                            sc.wait_ge(s_sd.h, s)
                        for dc in range(DC2):
                            sc.activation(out=xconv[bq1][:, dc, 0:3],
                                          in_=xconv[1 - bq1][:, dc, TS : TS + 3],
                                          func=ACT.Identity)
                    for dcl in range(DC2):
                        for n in range(DS):
                            i = dcl * DS + n
                            li = (s * DC2 + dcl) * DS + n
                            if li >= 1:
                                sc.wait_ge(s_sc.h, li)
                            ev = sc.activation(out=scan_a[:, dcl, :],
                                               in_=dnegv[:, dcl, :], func=ACT.Exp,
                                               scale=ppos[:, dcl * DS + n : dcl * DS + n + 1])
                            s_ae.inc(ev)
                            if with_front:
                                for j in ESCHED.get(i, []):
                                    emit_front_evac(sc, s + 1, j)
                    if with_front:
                        bq1 = (s + 1) % 2
                        sc.wait_ge(s_fe.h, 18 * (s + 1) + 4)
                        sc.activation(out=srv[bq1].rearrange("p a t -> p (a t)"),
                                      in_=srv[bq1].rearrange("p a t -> p (a t)"),
                                      func=ACT.Silu)
                    if s + 1 < NSEG:
                        emit_z_chain_s(sc, s + 1)

            def scan_v(s):
                bq = s % 2

                @block.vector
                def _(v):
                    v.wait_ge(s_ze.h, 4 * s + 4)
                    s_dx.inc(TT(v, dxn.rearrange("p a t -> p (a t)"),
                                dnegv.rearrange("p a t -> p (a t)"),
                                xcv[bq].rearrange("p a t -> p (a t)")))
                    v.wait_ge(s_dx.h, s_dx.n)
                    for dcl in range(DC2):
                        for n in range(DS):
                            li = (s * DC2 + dcl) * DS + n
                            v.wait_ge(s_bb.h, li + 1)
                            if li >= 1:
                                v.wait_ge(s_sc.h, li)
                            tv = TT(v, scan_u[:], dxn[:, dcl, :], pB[:, :])
                            s_ub.inc(tv)
                            v.wait_ge(s_ub.h, li + 1)
                            v.wait_ge(s_ae.h, li + 1)
                            if s >= 1:
                                v.wait_ge(s_hs.h, ((s - 1) * DC2 + dcl) * DS + n + 1)
                            tv = v.tensor_tensor_scan(
                                out=scan_h[:], data0=scan_a[:, dcl, :],
                                data1=scan_u[:],
                                initial=(0.0 if s == 0 else hstate[:, dcl, n : n + 1]),
                                op0=ALU.mult, op1=ALU.add)
                            s_sc.inc(tv)
                            v.wait_ge(s_sc.h, li + 1)
                            s_hs.inc(v.tensor_copy(out=hstate[:, dcl, n : n + 1],
                                                   in_=scan_h[:, TS - 1 : TS]))
                            v.wait_ge(s_bc2.h, li + 1)
                            tv = TT(v, scan_g[:, dcl, :], scan_h[:], pC[:, :])
                            s_cm.inc(tv)
                        v.wait_ge(s_ir.h, (s * DC2 + dcl) * DS + DS)
                        if 2 * s + dcl - 2 >= 0:
                            s_ys.wait(v, 2 * s + dcl - 2)
                        for c in range(2):
                            tv = STT(v, ytmp[:], xcv[bq][:, dcl, c * F : (c + 1) * F],
                                     dvec[:, dcl : dcl + 1], pb[2 + c][:, 0:F],
                                     ALU.mult, ALU.add)
                            s_vs2.inc(tv)
                            v.wait_ge(s_vs2.h, s_vs2.n)
                            tv = TT(v, ybuf[:, dcl, c * F : (c + 1) * F],
                                    ytmp[:], srv[bq][:, dcl, c * F : (c + 1) * F])
                            s_yf.inc(tv)
                            v.wait_ge(s_yf.h, s_yf.n)
                    s_sd.inc(v.memset(ytmp[0:1, 0:1], 0.0))

                @block.sync
                def _(sy):
                    for dcl in range(DC2):
                        sy.wait_ge(s_yf.h, 4 * s + 2 * (dcl + 1))
                        dst = y_in_h[s // 2, dcl * 128 + 0 : dcl * 128 + 128,
                                     (s % 2) * TS : (s % 2) * TS + TS]
                        s_ys.inc(sy.dma_start(out=dst, in_=ybuf[:, dcl, :]), 2 * s + dcl)

            front_sync(0)

            @block.tensor
            def _(t):
                for j in range(18):
                    emit_front_job(t, 0, j)

            @block.scalar
            def _(sc):
                bq1 = 0
                for dc in range(DC2):
                    sc.activation(out=xconv[0][:, dc, 0:3],
                                  in_=xconv[0][:, dc, 0:3], func=ACT.Identity,
                                  scale=0.0)
                for j in range(18):
                    emit_front_evac(sc, 0, j)
                sc.wait_ge(s_fe.h, 4)
                sc.activation(out=srv[0].rearrange("p a t -> p (a t)"),
                              in_=srv[0].rearrange("p a t -> p (a t)"),
                              func=ACT.Silu)
            front_g(0)

            @block.scalar
            def _(sc):
                pass

            @block.tensor
            def _(t):
                pass

            @block.scalar
            def _(sc):
                sc.wait_ge(s_bcz.h, 2)
                s_ld.wait(sc, 0)
                s_ld.wait(sc, 1)
                s_ld.wait(sc, 2)
                sc.activation(out=dpreb[:], in_=dpre[0][:], func=ACT.Identity)
                sc.activation(out=bcb[0][0:DS, :], in_=bcstage[0][0:DS, :],
                              func=ACT.Identity)
                s_dc.inc(sc.activation(out=bcb[0][32 : 32 + DS, :],
                                       in_=bcstage[0][32 : 32 + DS, :],
                                       func=ACT.Identity), 2)

            @block.tensor
            def _(t):
                emit_z_mms(t, 0)

            @block.scalar
            def _(sc):
                for j in range(4):
                    dc, tt = j // 2, j % 2
                    sc.wait_ge(s_zm.h, j + 1)
                    bank = pb[4 + j % 2]
                    sg = sc.activation(out=bank[:, 0:F], in_=bank[:, 0:F],
                                       func=ACT.Exp, bias=dtbn[:, dc : dc + 1])
                    s_ss.inc(sg)
                    sc.wait_ge(s_ss.h, s_ss.n)
                    ev = sc.activation(out=dnegv[:, dc, tt * F : (tt + 1) * F],
                                       in_=bank[:, 0:F], func=ACT.Ln, bias=1.0)
                    s_ze.inc(ev)

            for s_i in range(NSEG):
                if s_i + 1 < NSEG:
                    front_sync(s_i + 1)
                scan_te(s_i, with_front=(s_i + 1 < NSEG))
                scan_s(s_i, with_front=(s_i + 1 < NSEG))
                scan_v(s_i)
                if s_i + 1 < NSEG:
                    front_g(s_i + 1)

            @block.sync
            def _(sy):
                for _ss in s_ys.sems:
                    sy.wait_ge(_ss.h, _ss.n)
            b.fence(fbuf, dma_sems=s_ys.sems + s_ld.sems + s_u2l.sems)

        stream = b.sb("stream2", [128, KD, TC], F32, es_stream2 := ExitStack())
        s_a2a = b.sem("a2a")
        with ExitStack() as p5:
            wout_s = b.sb("wout_s", [128, DM // 128, D], BF16, p5)
            ytile = [b.sb(f"ytile{i}", [128, DM // 128, F], BF16, p5) for i in range(2)]
            s_wol = b.sem("wol")
            s_srl2 = b.sem("srl2")
            s_srl = DStream(nc, "srl")
            s_ytl = DStream(nc, "ytl")
            s_om = b.sem("om")
            s_oe = b.sem("oe")

            @block.gpsimd
            def _(g):
                from concourse.aot_env import allow_ici_all_to_all
                with allow_ici_all_to_all():
                    s_a2a.inc(g.collective_compute(
                        "AllToAll", ALU.bypass, replica_groups=[list(range(NC))],
                        ins=[y_in_h.opt()], outs=[y_out_h.opt()]))

            @block.sync
            def _(sy):
                s_wol.incd(sy.dma_start(
                    out=wout_s.rearrange("p a c -> p (a c)"), in_=woutc_h[:]))
                s_srl2.incd(sy.dma_start(out=stream.rearrange("p a t -> p (a t)"),
                                         in_=sspill_h[:]))
                s_a2a.wait(sy)
                yv = y_out_h.rearrange("i (c p) t -> p i c t", p=128)
                for tt in range(NT):
                    if tt >= 2:
                        sy.wait_ge(s_om.h, (tt - 1) * KD)
                    s_ytl.inc(sy.dma_start(
                        out=ytile[tt % 2].rearrange("p a f -> p a f"),
                        in_=yv[:, :, :, tt * F : (tt + 1) * F].rearrange("p i c t -> p (i c) t")), tt)

            @block.tensor
            def _(t):
                t.wait_ge(s_wol.h, 16)
                for tt in range(NT):
                    s_ytl.wait(t, tt)
                    for pc in range(KD):
                        j = tt * KD + pc
                        bank = pb[j % 2]
                        if j >= 2:
                            t.wait_ge(s_oe.h, j - 1)
                        mm = None
                        for kc in range(DM // 128):
                            mm = t.matmul(bank[:, 0:F],
                                          wout_s[:, kc, pc * 128 : (pc + 1) * 128],
                                          ytile[tt % 2][:, kc, :],
                                          start=(kc == 0), stop=(kc == DM // 128 - 1))
                        s_om.inc(mm)

            @block.vector
            def _(v):
                v.wait_ge(s_wol.h, 16)
                v.wait_ge(s_srl2.h, 16)
                for tt in range(NT):
                    for pc in range(KD):
                        j = tt * KD + pc
                        v.wait_ge(s_om.h, j + 1)
                        bank = pb[j % 2]
                        s_oe.inc(STT(v, stream[:, pc, tt * F : (tt + 1) * F],
                                     bank[:, 0:F], 1.0,
                                     stream[:, pc, tt * F : (tt + 1) * F]))

            b.fence(fbuf, dma_sems=s_ytl.sems + s_srl.sems)

        with ExitStack() as pf2:
            u3f = b.sb("u3f", [128, KD, TC], BF16, pf2)
            emit_ln(u3f, "3")
            emit_ffn(u3f, 1)

        emit_store(nc, block, b, stream, pb, identf, out_h, fbuf)
        es_stream2.close()
        return nc


def emit_store(nc, block, b, stream, pb, identf, out_h, tiny):
    s_tr = b.sem("otr")
    s_ev = b.sem("oev")
    s_st = DStream(nc, "ost")
    with ExitStack() as px:
        ostage = b.sb("ostage", [128, 2, 1024], F32, px)

        @block.tensor
        def _(t):
            for tt in range(TC // 128):
                for kc in range(KD):
                    j = tt * KD + kc
                    if j >= 2:
                        t.wait_ge(s_ev.h, j - 1)
                    s_tr.inc(t.transpose(pb[j % 2][:, 0:128],
                                         stream[:, kc, tt * 128 : (tt + 1) * 128],
                                         identf[:]))

        @block.vector
        def _(v):
            for tt in range(TC // 128):
                if tt >= 2:
                    s_st.wait(v, tt - 2)
                for kc in range(KD):
                    j = tt * KD + kc
                    v.wait_ge(s_tr.h, j + 1)
                    s_ev.inc(v.tensor_copy(out=ostage[:, tt % 2, kc * 128 : (kc + 1) * 128],
                                           in_=pb[j % 2][:, 0:128]))

        @block.sync
        def _(sy):
            for tt in range(TC // 128):
                sy.wait_ge(s_ev.h, (tt + 1) * KD)
                s_st.inc(sy.dma_start(out=out_h[tt * 128 : (tt + 1) * 128, :],
                                      in_=ostage[:, tt % 2, :]), tt)
            for _ss in s_st.sems:
                sy.wait_ge(_ss.h, _ss.n)


def _prep_inputs(inputs):
    g = lambda k: np.asarray(inputs[k])
    x = g("L0_attn_x_in")[0]
    per_core = []
    ident = np.eye(128, dtype=np.float32)
    sel32 = np.zeros((64, 32, 128), dtype=np.float32)
    for j in range(16):
        sel32[j, j, :] = 1.0
        sel32[32 + j, 16 + j, :] = 1.0
    for c in range(NC):
        ch = slice(c * DCH, (c + 1) * DCH)
        m = {}
        m["x0"] = np.ascontiguousarray(x[c * TC : (c + 1) * TC])
        m["ident"] = ident
        m["sel32"] = sel32
        m["w_base"] = np.ascontiguousarray(g("L0_ffn_W_base"))
        m["b_base"] = np.ascontiguousarray(g("L0_ffn_b_base").reshape(KD, 128).T)
        m["wd1"] = np.ascontiguousarray(g("L0_ffn_Wd1"))
        m["bd1"] = np.ascontiguousarray(g("L0_ffn_bd1").reshape(E, HC, 128).transpose(0, 2, 1))
        m["wd2"] = np.ascontiguousarray(g("L0_ffn_Wd2"))
        m["bd2"] = np.ascontiguousarray(g("L0_ffn_bd2").reshape(E, KD, 128).transpose(0, 2, 1))
        m["wad"] = np.ascontiguousarray(g("L0_ffn_Wa_down"))
        m["bad"] = np.ascontiguousarray(g("L0_ffn_ba_down").T)
        m["wau"] = np.ascontiguousarray(g("L0_ffn_Wa_up"))
        m["bau"] = np.ascontiguousarray(g("L0_ffn_ba_up").reshape(E, KD, 128).transpose(0, 2, 1))
        m["w_in"] = np.ascontiguousarray(g("L0_attn_W_in")[:, ch])
        m["w_res"] = np.ascontiguousarray(g("L0_attn_W_res")[:, ch])
        m["w_out"] = np.ascontiguousarray(g("L0_attn_W_out"))
        m["conv_w"] = np.ascontiguousarray(
            g("L0_attn_conv_w")[ch].reshape(DC2, 128, 4).transpose(1, 0, 2))
        m["conv_b"] = np.ascontiguousarray(
            g("L0_attn_conv_b")[ch].reshape(DC2, 128).T)
        m["a_log"] = np.ascontiguousarray(
            g("L0_attn_A_log")[ch].reshape(DC2, 128, DS).transpose(1, 0, 2))
        m["d_vec"] = np.ascontiguousarray(
            g("L0_attn_D")[ch].reshape(DC2, 128).T)
        m["w_b"] = np.ascontiguousarray(g("L0_attn_W_B")[ch])
        m["w_c"] = np.ascontiguousarray(g("L0_attn_W_C")[ch])
        m["w_dt"] = np.ascontiguousarray(g("L0_attn_W_dt")[ch])
        m["dt_w"] = np.ascontiguousarray(g("L0_attn_dt_w")[:, ch])
        m["dt_b"] = np.ascontiguousarray(
            g("L0_attn_dt_b")[ch].reshape(DC2, 128).T)
        per_core.append(m)
    return per_core


def kernel(**inputs):
    key = "all"
    if key not in _CACHE:
        _CACHE[key] = build_program(upto=key)
    nc = _CACHE[key]
    in_maps = _prep_inputs(inputs)
    res = run_bass_kernel_spmd(nc, in_maps, core_ids=list(range(NC)))
    out = np.concatenate([res.results[c]["out"] for c in range(NC)], axis=0)
    return out[None, :, :]
